# version 19
# baseline (speedup 1.0000x reference)
import os
import numpy as np
from contextlib import ExitStack

import concourse.bass as bass
import concourse.mybir as mybir
from concourse.bass import IndirectOffsetOnAxis
from concourse.bass_utils import run_bass_kernel_spmd

F32 = mybir.dt.float32
I32 = mybir.dt.int32
AF = mybir.ActivationFunctionType
ALU = mybir.AluOpType
AXX = mybir.AxisListType.X

FP32_EPS = 1.1920929e-07
LN_EPS = 1e-5
ENGINES = ("tensor", "vector", "scalar", "gpsimd", "sync")


def _cfg(n_cores=8, T=2048, D=1024, E=16, F=4096, TILE=512, S_TILE=512):
    import os
    c = dict(n_cores=n_cores, T=T, D=D, E=E, F=F, TILE=TILE, S_TILE=S_TILE)
    c["S"] = n_cores * T
    c["CAP"] = c["S"] // E
    c["nD"] = D // 128
    c["nSub"] = TILE // 128
    c["nTiles"] = int(os.environ.get("K_NTILES", T // TILE))
    c["nCols"] = T // 128
    c["EXP_PC"] = E // n_cores
    c["SLOTS"] = c["EXP_PC"] * c["CAP"]
    c["nF"] = F // 128
    c["nST"] = c["CAP"] // S_TILE
    return c


class Sched:
    def __init__(self, nc, es):
        self.nc, self.es = nc, es
        self.ops = {e: [] for e in ENGINES}
        self.sems, self.cnt = {}, {}

    def sem(self, name):
        if name not in self.sems:
            self.sems[name] = self.es.enter_context(self.nc.semaphore(name))
            self.cnt[name] = 0
        return self.sems[name]

    def val(self, name):
        return self.cnt.get(name, 0)

    def emit(self, eng, fn, waits=(), inc=None, amt=1):
        w = [(self.sem(s), v) for (s, v) in waits if v > 0]
        if inc is not None:
            self.sem(inc)
            self.cnt[inc] += amt
        self.ops[eng].append((fn, w, (self.sems[inc], amt) if inc else None))

    def wait(self, eng, waits):
        self.emit(eng, None, waits=waits)

    def barrier(self, engines=ENGINES, exclude=()):
        snap = [(s, v) for s, v in self.cnt.items() if v > 0 and s not in exclude]
        for e in engines:
            self.emit(e, None, waits=snap)

    def build(self, block):
        for ename in ENGINES:
            stream = self.ops[ename]
            if not stream:
                continue

            def mk(stream):
                def body(eng):
                    for fn, waits, inc in stream:
                        for sem, v in waits:
                            eng.wait_ge(sem, v)
                        inst = fn(eng) if fn is not None else None
                        if inc is not None:
                            assert inst is not None
                            inst.then_inc(inc[0], inc[1])
                return body

            getattr(block, ename)(mk(stream))


def build_nc(cfg, dbg=False, phases=5):
    nD, TILE, nSub, nTiles = cfg["nD"], cfg["TILE"], cfg["nSub"], cfg["nTiles"]
    D, E, F, T = cfg["D"], cfg["E"], cfg["F"], cfg["T"]
    CAP, S, SLOTS, nCols, nF = cfg["CAP"], cfg["S"], cfg["SLOTS"], cfg["nCols"], cfg["nF"]
    S_TILE, nST = cfg["S_TILE"], cfg["nST"]
    QK = 2 * D
    nQKe = max(1, QK // 512)
    nSW = (2 * D) // 128
    nDO = max(1, D // 512)
    BIG = float(1 << 22)

    assert TILE == 512, "FM chunk width is hardwired to 512"
    nc = bass.Bass("TRN2", target_bir_lowering=False, dynamic_dma_scratch_size=26624)

    def dram_in(name, shape, dt=F32):
        return nc.dram_tensor(name, shape, dt, kind="ExternalInput").ap()

    x_hbm = dram_in("x", [T, D])
    wqk = dram_in("wqk", [nD, 128, QK])
    wv = dram_in("wv", [nD, nD, 128, 128])
    wsw = dram_in("wsw", [nSW, nD, 128, 128])
    wout = dram_in("wout", [nD, 128, D])
    wg = dram_in("wg", [nD, 128, E])
    w1b = dram_in("w1b", [cfg["EXP_PC"], nF, nD, 128, 128])
    w2b = dram_in("w2b", [cfg["EXP_PC"], nD, nF, 128, 128])
    ident = dram_in("ident", [128, 128])
    iota16r = dram_in("iota16r", [128, E])
    iotaEc = dram_in("iotaEc", [E, 1])
    ebias = dram_in("ebias", [E, 1])
    rbias = dram_in("rbias", [E, 1])
    tokid1 = dram_in("tokid1", [128, nCols])
    prevmask = dram_in("prevmask", [8, 1])

    def dram(name, shape, dt=F32):
        return nc.dram_tensor(name, shape, dt).ap()

    BF16D = mybir.dt.bfloat16
    wqk_h = dram("wqk_h", [nD, 128, QK], BF16D)
    wqk_l = dram("wqk_l", [nD, 128, QK], BF16D)
    wv_h = dram("wv_h", [nD, nD, 128, 128], BF16D)
    wv_l = dram("wv_l", [nD, nD, 128, 128], BF16D)
    wsw_h = dram("wsw_h", [nSW, nD, 128, 128], BF16D)
    wsw_l = dram("wsw_l", [nSW, nD, 128, 128], BF16D)
    wout_h = dram("wout_h", [nD, 128, D], BF16D)
    wout_l = dram("wout_l", [nD, 128, D], BF16D)
    sp_wcs = dram("sp_wcs", [D, T])
    sp_kvcs = dram("sp_kvcs", [D, T])
    sp_sigq = dram("sp_sigq", [D, T])
    carry_in = dram("carry_in", [128, 16])
    carry_all = nc.dram_tensor("carry_all", [8 * 128, 16], F32, addr_space="Shared").ap()
    idxrow_d = dram("idxrow_d", [T])
    cnts_in = dram("cnts_in", [E])
    cnts_all = nc.dram_tensor("cnts_all", [8, E], F32, addr_space="Shared").ap()
    rowb = dram("rowb", [4, max(T, 2048)])
    xr_in = dram("xr_in", [T, D])
    xr_all = nc.dram_tensor("xr_all", [S, D], F32, addr_space="Shared").ap()
    smap_in = dram("smap_in", [S])
    smap_out = dram("smap_out", [SLOTS])
    eo_in = dram("eo_in", [SLOTS, D])
    eo_all = nc.dram_tensor("eo_all", [S, D], F32, addr_space="Shared").ap()
    out_hbm = nc.dram_tensor("out", [T, D], F32, kind="ExternalOutput").ap()
    dbg_hbm = nc.dram_tensor("dbgo", [16, max(T, D)], F32, kind="ExternalOutput").ap() if dbg else None

    es = ExitStack()

    AW = 16384 + 5 * (nSub * D) + 4 * nD * 512 + 2 * nD * 128 + 2048
    XM_O = 0
    B1_O = 16384
    B2_O = B1_O + nSub * D
    B3_O = B2_O + nSub * D
    B4_O = B3_O + nSub * D
    B5_O = B4_O + nSub * D
    WQ_O = B5_O + nSub * D
    WQ_W = 4 * nD * 512
    WS_O = WQ_O + 2 * nD * 512
    WS_W = 2 * nD * 128
    M2_O = WS_O + WS_W
    AW = M2_O + max(2048, 2 * D)
    AR = es.enter_context(nc.sbuf_tensor("AR", [128, AW], F32))

    def A(off, w, p0=0, p1=128):
        return AR[p0:p1, off:off + w]

    XM = A(XM_O, 16384)
    B1 = A(B1_O, nSub * D)
    B2 = A(B2_O, nSub * D)
    B3 = A(B3_O, nSub * D)
    B4 = A(B4_O, nSub * D)
    B5 = A(B5_O, nSub * D)
    WQh = [A(WQ_O, nD * 512), A(WQ_O + nD * 512, nD * 512)]
    WSh = [A(WS_O, nD * 128), A(WS_O + nD * 128, nD * 128)]
    WQp = [(A(WQ_O, nD * 256).bitcast(mybir.dt.bfloat16),
            A(WQ_O + nD * 256, nD * 256).bitcast(mybir.dt.bfloat16)),
           (A(WQ_O + nD * 512, nD * 256).bitcast(mybir.dt.bfloat16),
            A(WQ_O + nD * 512 + nD * 256, nD * 256).bitcast(mybir.dt.bfloat16))]
    WSp = [(A(WS_O, nD * 64).bitcast(mybir.dt.bfloat16),
            A(WS_O + nD * 64, nD * 64).bitcast(mybir.dt.bfloat16)),
           (A(WS_O + nD * 128, nD * 64).bitcast(mybir.dt.bfloat16),
            A(WS_O + nD * 128 + nD * 64, nD * 64).bitcast(mybir.dt.bfloat16))]
    XNh = A(B3_O, nSub * D // 2).bitcast(mybir.dt.bfloat16)
    XNl = A(B3_O + nSub * D // 2, nSub * D // 2).bitcast(mybir.dt.bfloat16)
    Yh = A(B2_O, nSub * D // 2).bitcast(mybir.dt.bfloat16)
    Yl = A(B2_O + nSub * D // 2, nSub * D // 2).bitcast(mybir.dt.bfloat16)
    SQA = A(XM_O, nSub * D)
    ROW_OH = A(B1_O, T, 0, 16)
    ROW_SC = A(B2_O, T, 0, 16)
    ROW_A = A(B3_O, T, 0, 16)
    ONES16 = A(B4_O, T, 0, 16)
    R1 = A(B5_O, T, 0, 1)
    R2 = A(B5_O + T, T, 0, 1)
    IDXROW = A(B5_O + 2 * T if 3 * T <= nSub * D else WQ_O, T, 0, 1)
    CGROW = A(M2_O, 2048, 0, 1)
    HS = A(B1_O, nF * S_TILE)
    XF = A(B5_O, nD * S_TILE)
    XG = A(M2_O, D)
    EOT = A(M2_O + D, D)
    W2HW = (nF // 2) * 128
    W2h = [A(WQ_O, W2HW), A(WQ_O + W2HW, W2HW)]
    OUT2D = A(WQ_O + 2 * W2HW, nD * S_TILE)
    GA = A(B1_O, D)
    OT = A(B2_O, D)

    assert nF * S_TILE <= 4 * nSub * D, "HS overflows B1..B4 region"
    assert nD * S_TILE <= nSub * D, "XF overflows B5"
    assert 2 * D <= max(2048, 2 * D), "XG/EOT overflow M2"
    assert 2 * W2HW + nD * S_TILE <= 2 * 2 * nD * 512, "W2+OUT2 overflow WQ region"

    def sbuf(name, shape, dt=F32):
        return es.enter_context(nc.sbuf_tensor(name, shape, dt))

    stat = sbuf("stat", [128, 96])
    carry = sbuf("carry", [128, 16])
    carryg = sbuf("carryg", [128, 24])
    gates = sbuf("gates", [128, nCols])
    idxc = sbuf("idxc", [128, nCols])
    mcol = sbuf("mcol", [128, nCols])
    rcol = sbuf("rcol", [128, nCols])
    dcol = sbuf("dcol", [128, nCols])
    dcoli = sbuf("dcoli", [128, nCols], I32)
    rcoli = sbuf("rcoli", [128, nCols], I32)
    tok1 = sbuf("tok1", [128, nCols])
    gmapi = sbuf("gmapi", [128, SLOTS // 128], I32)
    gmapf = sbuf("gmapf", [128, SLOTS // 128])
    id128 = sbuf("id128", [128, 128])
    iota16 = sbuf("iota16", [128, E])
    wgt = sbuf("wgt", [128, nD * E])
    smallc = sbuf("smallc", [16, 8])
    c8 = sbuf("c8", [8, E])
    pm8 = sbuf("pm8", [8, 1])
    ones512 = sbuf("ones512", [128, 512])
    zz = sbuf("zz", [128, max(nCols, S // 128, 16)])
    offrow = sbuf("offrow", [1, E])

    ps = [es.enter_context(nc.psum_tensor(f"ps{i}", [128, 512], F32)) for i in range(8)]

    S_ = Sched(nc, es)
    emit, val, wait = S_.emit, S_.val, S_.wait

    def dma(eng, out, in_, sem, waits=(), slow=False):
        emit(eng, lambda e, o=out, i=in_, s=slow: e.dma_start(
            out=o, in_=i, allow_slow_non_contiguous=s),
             waits=waits, inc=sem, amt=16)
        return val(sem)

    dma("sync", id128[:], ident, "c_ld")
    dma("sync", iota16[:], iota16r, "c_ld")
    dma("sync", smallc[:, 0:1], iotaEc, "c_ld")
    dma("sync", smallc[:, 1:2], ebias, "c_ld")
    dma("sync", smallc[:, 2:3], rbias, "c_ld")
    dma("sync", tok1[:], tokid1, "c_ld")
    dma("sync", pm8[:], prevmask, "c_ld")
    for k in range(nD):
        dma("sync", wgt[:, k * E:(k + 1) * E], wg[k], "c_ld")
    C_LD = val("c_ld")
    emit("vector", lambda v: v.memset(carry[:], 0.0))
    emit("vector", lambda v: v.memset(zz[:], 0.0))
    emit("vector", lambda v: v.memset(ones512[:], 1.0))
    emit("vector", lambda v: v.drain(), inc="v")
    V_INIT = val("v")
    dma("gpsimd", smap_in.rearrange("(p c) -> p c", p=128), zz[:, 0:S // 128],
        "g_z", [("v", V_INIT)])
    Z_S = val("g_z")

    if os.environ.get("K_PH0"):
        wait("vector", [("c_ld", C_LD), ("v", V_INIT), ("g_z", Z_S)])
        dma("gpsimd", out_hbm[0:128, :], AR[:, 0:D], "d_out")
        wait("gpsimd", [("d_out", val("d_out"))])
        blk0 = es.enter_context(nc.Block())
        S_.build(blk0)
        es.close()
        return nc

    def ln_alpha(src_ap, rms_after=True):
        emit("vector", lambda v, s=src_ap: v.bn_stats(stat[:, 48:54], s[:, 0:512]))
        emit("vector", lambda v, s=src_ap: v.bn_stats(stat[:, 54:60], s[:, 512:1024]))
        emit("vector", lambda v: v.drain())
        emit("vector", lambda v: v.bn_aggr(stat[:, 44:46], stat[:, 48:60]))
        emit("vector", lambda v: v.drain())
        emit("vector", lambda v: v.tensor_scalar(
            stat[:, 60:61], stat[:, 45:46], LN_EPS, None, ALU.add))
        emit("vector", lambda v: v.drain())
        emit("vector", lambda v: v.reciprocal(stat[:, 61:62], stat[:, 60:61]))
        emit("vector", lambda v: v.drain())
        if rms_after:
            emit("vector", lambda v: v.tensor_tensor(
                stat[:, 62:63], stat[:, 45:46], stat[:, 61:62], ALU.mult))
            emit("vector", lambda v: v.drain())
            emit("vector", lambda v: v.tensor_scalar(
                stat[:, 63:64], stat[:, 62:63], FP32_EPS, None, ALU.add))
            emit("vector", lambda v: v.drain())
            emit("vector", lambda v: v.reciprocal(stat[:, 64:65], stat[:, 63:64]))
        emit("vector", lambda v: v.drain(), inc="v")
        vv = val("v")
        emit("scalar", lambda sc: sc.activation(stat[:, 65:66], stat[:, 61:62], AF.Sqrt),
             waits=[("v", vv)])
        if rms_after:
            emit("scalar", lambda sc: sc.drain())
            emit("scalar", lambda sc: sc.activation(stat[:, 66:67], stat[:, 64:65], AF.Sqrt))
        emit("scalar", lambda sc: sc.drain(), inc="s")
        sv = val("s")
        if rms_after:
            emit("vector", lambda v: v.tensor_tensor(
                stat[:, 45:46], stat[:, 65:66], stat[:, 66:67], ALU.mult),
                waits=[("s", sv)])
        else:
            emit("vector", lambda v: v.tensor_copy(stat[:, 45:46], stat[:, 65:66]),
                 waits=[("s", sv)])
        emit("vector", lambda v: v.drain(), inc="v")

    def transpose_block(src_fn, dst_fn, n, bankpair, tag, pre_waits, rows=128):
        for i in range(n):
            pb = bankpair[i % 2][0:rows, 0:128]
            emit("tensor", lambda tt, i=i, pb=pb: tt.transpose(
                pb, src_fn(i), id128[:]),
                waits=(pre_waits if i == 0 else []) +
                      ([(f"v{tag}", val(f"v{tag}") - 1)] if i >= 2 else []),
                inc=f"t{tag}")
            emit("vector", lambda v, i=i, pb=pb: v.tensor_copy(dst_fn(i), pb),
                waits=[(f"t{tag}", val(f"t{tag}"))], inc=f"v{tag}")
        return val(f"v{tag}")

    BF16 = mybir.dt.bfloat16
    X0LD = dma("sync", SQA[:, 0:nSub * D].rearrange("p (s d) -> p s d", s=nSub),
               x_hbm[0:TILE, :].rearrange("(s p) d -> p s d", p=128), "d_x")
    PW = [dict(land=A(XM_O + 4096 + i * 4096, 2048),
               hi=A(XM_O + 4096 + i * 4096 + 2048, 1024).bitcast(BF16),
               lo=A(XM_O + 4096 + i * 4096 + 3072, 1024).bitcast(BF16))
          for i in range(2)]
    pw_n = [0]

    def presplit(fp32_src, hi_dst, lo_dst, width, on_gp=False):
        i = pw_n[0] % 2
        pw_n[0] += 1
        st = PW[i]
        ld = dma("scalar", st["land"][:, 0:width], fp32_src, f"d_pwi{i}",
                 [(f"d_pwo{i}", val(f"d_pwo{i}"))])
        if on_gp:
            emit("gpsimd", lambda g, st=st, width=width: g.tensor_copy(
                st["hi"][:, 0:width], st["land"][:, 0:width]),
                waits=[(f"d_pwi{i}", ld)], inc="g_pwh")
            emit("gpsimd", lambda g, st=st, width=width: g.tensor_tensor(
                st["lo"][:, 0:width], st["land"][:, 0:width], st["hi"][:, 0:width],
                ALU.subtract),
                waits=[("g_pwh", val("g_pwh"))], inc="g_pwl")
            PWV = ("g_pwl", val("g_pwl"))
        else:
            emit("scalar", lambda sc, st=st, width=width: sc.copy(
                st["hi"][:, 0:width], st["land"][:, 0:width]),
                waits=[(f"d_pwi{i}", ld)], inc="s_pw")
            emit("vector", lambda v, st=st, width=width: v.tensor_tensor(
                st["lo"][:, 0:width], st["land"][:, 0:width], st["hi"][:, 0:width],
                ALU.subtract),
                waits=[("s_pw", val("s_pw"))], inc="v_pw")
            PWV = ("v_pw", val("v_pw"))
        dma("scalar", hi_dst, st["hi"][:, 0:width], f"d_pwo{i}", [PWV])
        dma("scalar", lo_dst, st["lo"][:, 0:width], f"d_pwo{i}")

    MARKS = {}

    def presplit_qkv():
        for k in range(nD):
            presplit(wqk[k], wqk_h[k], wqk_l[k], QK)
        MARKS["QK"] = [("d_pwo0", val("d_pwo0")), ("d_pwo1", val("d_pwo1"))]
        for e in range(nD):
            presplit(wv[e].rearrange("k p f -> p k f"),
                     wv_h[e].rearrange("k p f -> p k f"),
                     wv_l[e].rearrange("k p f -> p k f"), nD * 128)
        MARKS["WV"] = [("d_pwo0", val("d_pwo0")), ("d_pwo1", val("d_pwo1"))]

    def prefetch_stats(xld_mark):
        B = 72
        for s in range(nSub):
            xs = SQA[:, s * D:(s + 1) * D]
            emit("vector", lambda v, xs=xs: v.bn_stats(stat[:, B + 4:B + 10], xs[:, 0:512]),
                 waits=[("d_x", xld_mark)] if s == 0 else [])
            emit("vector", lambda v, xs=xs: v.bn_stats(stat[:, B + 10:B + 16], xs[:, 512:1024]))
            emit("vector", lambda v: v.drain())
            emit("vector", lambda v: v.bn_aggr(stat[:, B:B + 2], stat[:, B + 4:B + 16]))
            emit("vector", lambda v: v.drain())
            emit("vector", lambda v: v.tensor_scalar(
                stat[:, B + 16:B + 17], stat[:, B + 1:B + 2], LN_EPS, None, ALU.add))
            emit("vector", lambda v: v.drain())
            emit("vector", lambda v: v.reciprocal(stat[:, B + 17:B + 18], stat[:, B + 16:B + 17]))
            emit("vector", lambda v: v.drain())
            emit("vector", lambda v: v.tensor_tensor(
                stat[:, B + 18:B + 19], stat[:, B + 1:B + 2], stat[:, B + 17:B + 18], ALU.mult))
            emit("vector", lambda v: v.drain())
            emit("vector", lambda v: v.tensor_scalar(
                stat[:, B + 19:B + 20], stat[:, B + 18:B + 19], FP32_EPS, None, ALU.add))
            emit("vector", lambda v: v.drain())
            emit("vector", lambda v: v.reciprocal(stat[:, B + 20:B + 21], stat[:, B + 19:B + 20]))
            emit("vector", lambda v: v.drain(), inc="v_pf")
            PFV = val("v_pf")
            emit("scalar", lambda sc: sc.activation(
                stat[:, B + 21:B + 22], stat[:, B + 17:B + 18], AF.Sqrt),
                waits=[("v_pf", PFV)])
            emit("scalar", lambda sc: sc.drain())
            emit("scalar", lambda sc: sc.activation(
                stat[:, B + 22:B + 23], stat[:, B + 20:B + 21], AF.Sqrt))
            emit("scalar", lambda sc: sc.drain(), inc="s_pf")
            SPF = val("s_pf")
            emit("vector", lambda v, s=s: v.tensor_copy(stat[:, s:s + 1], stat[:, B:B + 1]),
                 waits=[("s_pf", SPF)])
            emit("vector", lambda v, s=s: v.tensor_tensor(
                stat[:, 8 + s:9 + s], stat[:, B + 21:B + 22], stat[:, B + 22:B + 23],
                ALU.mult))
            emit("vector", lambda v: v.drain(), inc="v_pf")
        return val("v_pf")

    STOPA = int(os.environ.get("K_STOPA", "0"))

    def bail():
        snap = [(s, v) for s, v in S_.cnt.items() if v > 0]
        for e in ENGINES:
            S_.emit(e, None, waits=snap)
        dma("gpsimd", out_hbm[0:128, :], AR[:, 0:D], "d_out")
        wait("gpsimd", [("d_out", val("d_out"))])
        blkb = es.enter_context(nc.Block())
        S_.build(blkb)
        es.close()
        return nc

    for t in range(nTiles):
        TB = t * TILE
        tb_snap = [(s, v) for s, v in S_.cnt.items() if v > 0]
        if t == 0:
            wait("vector", [("c_ld", C_LD), ("v", V_INIT)])
            PF = prefetch_stats(X0LD)
        for s in range(nSub):
            emit("vector", lambda v, s=s: v.tensor_scalar(
                B2[:, s * D:(s + 1) * D], SQA[:, s * D:(s + 1) * D],
                stat[:, s:s + 1], stat[:, 8 + s:9 + s],
                ALU.subtract, ALU.mult),
                waits=([("v_pf", val("v_pf")), ("s_pf", val("s_pf"))] +
                       (tb_snap if t > 0 else [])) if s == 0 else [])
            emit("vector", lambda v: v.drain(), inc="v_xn")
        XN_V = val("v_xn")
        if t == 0:
            presplit_qkv()
        if STOPA == 2:
            return bail()

        for i in range(nSub * nD):
            s_, k_ = i // nD, i % nD
            pb = (ps[4], ps[5])[i % 2][:, 0:128]
            co = k_ * 512 + s_ * 128
            co2 = k_ * 512 + s_ * 128
            emit("tensor", lambda tt, i=i, s_=s_, k_=k_, pb=pb: tt.transpose(
                pb, B2[:, s_ * D + k_ * 128:s_ * D + k_ * 128 + 128], id128[:]),
                waits=([("v_xn", XN_V)] if i == 0 else []) +
                      ([("vtA", val("vtA") - 1), ("stA", val("stA") - 1)] if i >= 2 else []),
                inc="ttA")
            emit("scalar", lambda sc, co=co, pb=pb: sc.activation(
                XNh[:, co:co + 128], pb, AF.Copy),
                waits=[("ttA", val("ttA"))], inc="stA")
            emit("vector", lambda v, co=co, pb=pb: v.tensor_tensor(
                XNl[:, co:co + 128], pb, XNh[:, co:co + 128], ALU.subtract),
                waits=[("ttA", val("ttA")), ("stA", val("stA"))], inc="vtA")
        XNF_V, XNF_S = val("vtA"), val("stA")
        if STOPA == 3:
            return bail()

        for ec in range(nQKe):
            half = ec % 2
            wqld = dma("sync", WQp[half][0].rearrange("p (k e) -> p k e", k=nD),
                       wqk_h[:, :, ec * 512:(ec + 1) * 512].rearrange("k p e -> p k e"),
                       f"d_wqk{half}",
                       MARKS["QK"] +
                       ([("t_qk", max(0, val("t_qk") - nSub))] if val("t_qk") > nSub else []))
            wqld = dma("sync", WQp[half][1].rearrange("p (k e) -> p k e", k=nD),
                       wqk_l[:, :, ec * 512:(ec + 1) * 512].rearrange("k p e -> p k e"),
                       f"d_wqk{half}")
            for s in range(nSub):
                bank = (ec * nSub + s) % 2
                j = 0
                for (xp, wp) in ((XNh, WQp[half][0]), (XNh, WQp[half][1]),
                                 (XNl, WQp[half][0])):
                    for k in range(nD):
                        emit("tensor", lambda tt, s=s, k=k, j=j, xp=xp, wp=wp, bank=bank: tt.matmul(
                            ps[bank][:],
                            xp[:, k * 512 + s * 128:k * 512 + s * 128 + 128],
                            wp[:, k * 512:(k + 1) * 512],
                            start=(j == 0), stop=(j == 3 * nD - 1)),
                            waits=([("vtA", XNF_V), ("stA", XNF_S), (f"d_wqk{half}", wqld)] +
                                   ([("v_qk", val("v_qk") - 1)] if val("v_qk") >= 2 else []))
                                  if j == 0 else [],
                            inc=("t_qk" if j == 3 * nD - 1 else None))
                        j += 1
                dst = B2 if ec < nQKe // 2 else B1
                eoff = (ec % (nQKe // 2)) * 512
                emit("vector", lambda v, s=s, dst=dst, eoff=eoff, bank=bank: v.tensor_copy(
                    dst[:, s * D + eoff:s * D + eoff + 512], ps[bank][:]),
                    waits=[("t_qk", val("t_qk"))], inc="v_qk")
        QK_V = val("v_qk")
        if t + 1 < nTiles:
            xldn = dma("sync", SQA[:, 0:nSub * D].rearrange("p (s d) -> p s d", s=nSub),
                       x_hbm[(t + 1) * TILE:(t + 2) * TILE, :].rearrange(
                           "(s p) d -> p s d", p=128),
                       "d_x", [("v_xn", XN_V), ("s_sq", val("s_sq"))])
            prefetch_stats(xldn)
        if STOPA == 4:
            return bail()

        for ec in range(nD):
            half = ec % 2
            wvld = dma("sync", WSp[half][0].rearrange("p (k f) -> p k f", k=nD),
                       wv_h[ec].rearrange("k p f -> p k f"),
                       f"d_wv{half}",
                       MARKS["WV"] +
                       ([("t_v", max(0, val("t_v") - 1))] if val("t_v") > 1 else []))
            wvld = dma("sync", WSp[half][1].rearrange("p (k f) -> p k f", k=nD),
                       wv_l[ec].rearrange("k p f -> p k f"), f"d_wv{half}")
            bank = 2 + ec % 2
            j = 0
            for (wp, xp) in ((WSp[half][0], XNh), (WSp[half][0], XNl),
                             (WSp[half][1], XNh)):
                for k in range(nD):
                    emit("tensor", lambda tt, k=k, j=j, wp=wp, xp=xp, bank=bank: tt.matmul(
                        ps[bank][:],
                        wp[:, k * 128:(k + 1) * 128],
                        xp[:, k * 512:(k + 1) * 512],
                        start=(j == 0), stop=(j == 3 * nD - 1)),
                        waits=([("vtA", XNF_V), ("stA", XNF_S), (f"d_wv{half}", wvld)] +
                               ([("v_v", val("v_v") - 1)] if val("v_v") >= 2 else []))
                              if j == 0 else [],
                        inc=("t_v" if j == 3 * nD - 1 else None))
                    j += 1
            emit("vector", lambda v, ec=ec, bank=bank: v.tensor_copy(
                B4[:, ec * 512:(ec + 1) * 512], ps[bank][:]),
                waits=[("t_v", val("t_v"))], inc="v_v")
        V_V = val("v_v")
        if STOPA == 5:
            return bail()

        for s in range(nSub):
            emit("scalar", lambda sc, s=s: sc.activation(
                A(B3_O, nSub * D)[:, s * D:(s + 1) * D], B2[:, s * D:(s + 1) * D],
                AF.Square, accum_out=stat[:, 16 + s:17 + s]),
                waits=[("v_qk", QK_V), ("t_v", val("t_v"))], inc="s_sq")
            emit("scalar", lambda sc: sc.drain())
            emit("scalar", lambda sc, s=s: sc.activation(
                A(B3_O, nSub * D)[:, s * D:(s + 1) * D], B1[:, s * D:(s + 1) * D],
                AF.Square,
                accum_out=stat[:, 20 + s:21 + s]), inc="s_sq")
            SQS = val("s_sq")
            for j, off in ((16 + s, 24 + s), (20 + s, 28 + s)):
                emit("vector", lambda v, j=j: v.tensor_scalar(
                    stat[:, 60:61], stat[:, j:j + 1], 1.0 / D, FP32_EPS, ALU.mult, ALU.add),
                    waits=[("s_sq", SQS), ("s_a", val("s_a"))])
                emit("vector", lambda v: v.drain())
                emit("vector", lambda v: v.reciprocal(stat[:, 61:62], stat[:, 60:61]))
                emit("vector", lambda v: v.drain(), inc="v_a")
                emit("scalar", lambda sc, off=off: sc.activation(
                    stat[:, off:off + 1], stat[:, 61:62], AF.Sqrt),
                    waits=[("v_a", val("v_a"))], inc="s_a")
            AQK = val("s_a")
            emit("scalar", lambda sc, s=s: sc.activation(
                B2[:, s * D:(s + 1) * D], B2[:, s * D:(s + 1) * D], AF.Sigmoid,
                scale=stat[:, 24 + s:25 + s]), waits=[("s_a", AQK)], inc="s_g")
            emit("scalar", lambda sc, s=s: sc.activation(
                B1[:, s * D:(s + 1) * D], B1[:, s * D:(s + 1) * D], AF.Exp,
                scale=stat[:, 28 + s:29 + s]), inc="s_g")
        GATE_S = val("s_g")
        if STOPA == 6:
            return bail()

        TRB_V = transpose_block(
            lambda i: B2[:, (i // nD) * D + (i % nD) * 128:(i // nD) * D + (i % nD) * 128 + 128],
            lambda i: B5[:, (i % nD) * 512 + (i // nD) * 128:(i % nD) * 512 + (i // nD) * 128 + 128],
            nSub * nD, (ps[4], ps[5]), "tB", [("s_g", GATE_S)])
        TRC_V = transpose_block(
            lambda i: B1[:, (i // nD) * D + (i % nD) * 128:(i // nD) * D + (i % nD) * 128 + 128],
            lambda i: B3[:, (i % nD) * 512 + (i // nD) * 128:(i % nD) * 512 + (i // nD) * 128 + 128],
            nSub * nD, (ps[6], ps[7]), "tC",
            [("s_g", GATE_S), ("t_v", val("t_v")), ("v_v", V_V)])
        if STOPA == 7:
            return bail()

        for k in range(nD):
            emit("vector", lambda v, k=k: v.tensor_tensor(
                B2[:, k * 512:(k + 1) * 512], B3[:, k * 512:(k + 1) * 512],
                B4[:, k * 512:(k + 1) * 512], ALU.mult),
                waits=[("vtC", TRC_V), ("v_v", V_V)] if k == 0 else [],
                inc="v_kv")
        KV_V = val("v_kv")
        for k in range(nD):
            emit("vector", lambda v, k=k: v.tensor_tensor_scan(
                B1[:, k * 512:(k + 1) * 512], ones512[:], B3[:, k * 512:(k + 1) * 512],
                carry[:, k:k + 1], ALU.mult, ALU.add),
                waits=[("v_kv", KV_V)] if k == 0 else [])
            emit("vector", lambda v: v.drain())
            emit("vector", lambda v, k=k: v.tensor_copy(
                carry[:, k:k + 1], B1[:, (k + 1) * 512 - 1:(k + 1) * 512]))
            emit("vector", lambda v: v.drain(), inc="v_sc")
            emit("vector", lambda v, k=k: v.tensor_tensor_scan(
                B4[:, k * 512:(k + 1) * 512], ones512[:], B2[:, k * 512:(k + 1) * 512],
                carry[:, 8 + k:9 + k], ALU.mult, ALU.add))
            emit("vector", lambda v: v.drain())
            emit("vector", lambda v, k=k: v.tensor_copy(
                carry[:, 8 + k:9 + k], B4[:, (k + 1) * 512 - 1:(k + 1) * 512]))
            emit("vector", lambda v: v.drain(), inc="v_sc")
        SC_V = val("v_sc")
        if STOPA == 8:
            return bail()

        for k in range(nD):
            dma("scalar", sp_sigq[k * 128:(k + 1) * 128, TB:TB + TILE],
                B5[:, k * 512:(k + 1) * 512], "d_sp",
                [("vtB", TRB_V)] if k == 0 else [])
            dma("scalar", sp_wcs[k * 128:(k + 1) * 128, TB:TB + TILE],
                B1[:, k * 512:(k + 1) * 512], "d_sp",
                [("v_sc", SC_V)] if k == 0 else [])
            dma("scalar", sp_kvcs[k * 128:(k + 1) * 128, TB:TB + TILE],
                B4[:, k * 512:(k + 1) * 512], "d_sp", [])
    SPILL_D = val("d_sp")
    TVF = val("t_v")

    for f in range(nSW):
        presplit(wsw[f].rearrange("k p f2 -> p k f2"),
                 wsw_h[f].rearrange("k p f2 -> p k f2"),
                 wsw_l[f].rearrange("k p f2 -> p k f2"), nD * 128)
    MARK_SW = [("d_pwo0", val("d_pwo0")), ("d_pwo1", val("d_pwo1"))]
    for f in range(nD):
        presplit(wout[f], wout_h[f], wout_l[f], D)
    MARK_WO = [("d_pwo0", val("d_pwo0")), ("d_pwo1", val("d_pwo1"))]

    SKIP_CC = bool(os.environ.get("K_SKIP_CC"))
    dma("gpsimd", carry_in, carry[:], "g_d", [("v_sc", val("v_sc"))])
    if not SKIP_CC:
        emit("gpsimd", lambda g: g.collective_compute(
            "AllGather", ALU.bypass, replica_groups=[list(range(cfg["n_cores"]))],
            ins=[carry_in.opt()], outs=[carry_all.opt()]),
            waits=[("g_d", val("g_d"))], inc="g_cc")
    else:
        dma("gpsimd", carry_all[0:128, :], carry_in, "g_d", [("g_d", val("g_d"))])
        emit("gpsimd", lambda g: g.nop(), waits=[("g_d", val("g_d"))], inc="g_cc")
    CC_CARRY = val("g_cc")
    car8 = A(B1_O, 2048, 0, 8)
    dma("gpsimd", car8.rearrange("j (p c) -> j p c", p=128),
        carry_all.rearrange("(j p) c -> j p c", j=8),
        "g_d", [("g_cc", CC_CARRY), ("d_sp", SPILL_D)])
    CAR_D = val("g_d")
    for q in range(4):
        emit("tensor", lambda tt, q=q: tt.matmul(
            ps[7][0:1, :], pm8[:], car8[:, q * 512:(q + 1) * 512],
            start=True, stop=True),
            waits=[("g_d", CAR_D), ("c_ld", C_LD)] + ([("v_cg", val("v_cg"))] if q else []),
            inc="t_cg")
        emit("vector", lambda v, q=q: v.tensor_copy(
            CGROW[:, q * 512:(q + 1) * 512], ps[7][0:1, :]),
            waits=[("t_cg", val("t_cg"))], inc="v_cg")
    CG_V = val("v_cg")
    dma("gpsimd", rowb[3, 0:2048], CGROW[:], "g_d", [("v_cg", CG_V)])
    dma("gpsimd", carryg[:, 0:16], rowb[3, 0:2048].rearrange("(p c) -> p c", p=128),
        "g_d", [("g_d", val("g_d"))])
    CARRYG_D = val("g_d")

    for k in range(nD):
        dma("sync", B5[:, k * 512:(k + 1) * 512],
            sp_sigq[k * 128:(k + 1) * 128, 0:TILE], "d_ldB",
            [("d_sp", SPILL_D)] if k == 0 else [])
        dma("sync", B4[:, k * 512:(k + 1) * 512],
            sp_kvcs[k * 128:(k + 1) * 128, 0:TILE], "d_ldB")
    dma("sync", B3[:, 0:nSub * D].rearrange("p (s d) -> p s d", s=nSub),
        x_hbm[0:TILE, :].rearrange("(s p) d -> p s d", p=128), "d_ldB")
    for k in range(nD):
        dma("sync", B1[:, k * 512:(k + 1) * 512],
            sp_wcs[k * 128:(k + 1) * 128, 0:TILE], "d_ldB",
            [("v_cg", CG_V)] if k == 0 else [])
    LDB0 = val("d_ldB")

    def finish_early():
        dma("gpsimd", out_hbm[0:128, :], zz[:, 0:1].partition_broadcast(128) if False else AR[:, 0:D], "d_out")
        wait("gpsimd", [("d_out", val("d_out"))])
        blk = es.enter_context(nc.Block())
        S_.build(blk)
        es.close()
        return nc

    if phases < 2:
        S_.barrier()
        return finish_early()


    for t in range(nTiles):
        TB = t * TILE
        tb_snap = [(s, v) for s, v in S_.cnt.items()
                   if v > 0 and s[0] in "tvs"] + [("d_xr", val("d_xr"))]
        if t > 0:
            for k in range(nD):
                dma("sync", B5[:, k * 512:(k + 1) * 512],
                    sp_sigq[k * 128:(k + 1) * 128, TB:TB + TILE], "d_ldB",
                    tb_snap if k == 0 else [])
                dma("sync", B1[:, k * 512:(k + 1) * 512],
                    sp_wcs[k * 128:(k + 1) * 128, TB:TB + TILE], "d_ldB")
                dma("sync", B4[:, k * 512:(k + 1) * 512],
                    sp_kvcs[k * 128:(k + 1) * 128, TB:TB + TILE], "d_ldB")
            dma("sync", B3[:, 0:nSub * D].rearrange("p (s d) -> p s d", s=nSub),
                x_hbm[TB:TB + TILE, :].rearrange("(s p) d -> p s d", p=128),
                "d_ldB")
        LDB = val("d_ldB")
        CWP_PRE = val("v_y")

        for s in range(nSub):
            xs3 = B3[:, s * D:(s + 1) * D]
            emit("vector", lambda v, xs3=xs3: v.bn_stats(stat[:, 48:54], xs3[:, 0:512]),
                 waits=[("d_ldB", LDB)] if s == 0 else [])
            emit("vector", lambda v, xs3=xs3: v.bn_stats(stat[:, 54:60], xs3[:, 512:1024]))
            emit("vector", lambda v: v.drain())
            emit("vector", lambda v: v.bn_aggr(stat[:, 44:46], stat[:, 48:60]))
            emit("vector", lambda v: v.drain())
            emit("vector", lambda v: v.tensor_scalar(
                stat[:, 60:61], stat[:, 45:46], LN_EPS, None, ALU.add))
            emit("vector", lambda v: v.drain())
            emit("vector", lambda v: v.reciprocal(stat[:, 61:62], stat[:, 60:61]))
            emit("vector", lambda v: v.drain(), inc="v_y")
            VXZ = val("v_y")
            emit("scalar", lambda sc: sc.activation(stat[:, 62:63], stat[:, 61:62], AF.Sqrt),
                 waits=[("v_y", VXZ)], inc="s_xz")
            SXZ = val("s_xz")
            emit("vector", lambda v: v.tensor_scalar(
                stat[:, 63:64], stat[:, 62:63], 1.0, None, ALU.add),
                waits=[("s_xz", SXZ)])
            emit("vector", lambda v: v.drain())
            emit("vector", lambda v: v.reciprocal(stat[:, 64:65], stat[:, 63:64]))
            emit("vector", lambda v: v.drain())
            emit("vector", lambda v: v.tensor_tensor(
                stat[:, 65:66], stat[:, 44:45], stat[:, 62:63], ALU.mult))
            emit("vector", lambda v: v.drain())
            emit("vector", lambda v: v.tensor_tensor(
                stat[:, 65:66], stat[:, 65:66], stat[:, 64:65], ALU.mult))
            emit("vector", lambda v: v.drain())
            emit("vector", lambda v, xs3=xs3: v.tensor_scalar(
                xs3, xs3, stat[:, 65:66], stat[:, 63:64], ALU.subtract, ALU.mult))
            emit("vector", lambda v: v.drain(), inc="v_y")
        XZ_V = val("v_y")
        if t == 0:
            emit("vector", lambda v: v.tensor_scalar(
                carryg[:, 16:24], carryg[:, 0:8], 1e-6, None, ALU.add),
                waits=[("g_d", CARRYG_D)])
            emit("vector", lambda v: v.drain(), inc="v_y")
        CWP_V = val("v_y") if t == 0 else CWP_PRE
        for k in range(nD):
            emit("scalar", lambda sc, k=k: sc.activation(
                B1[:, k * 512:(k + 1) * 512], B1[:, k * 512:(k + 1) * 512], AF.Ln,
                bias=carryg[:, 16 + k:17 + k]),
                waits=[("v_y", CWP_V), ("d_ldB", LDB)] if k == 0 else [])
            emit("scalar", lambda sc: sc.drain())
            emit("scalar", lambda sc, k=k: sc.activation(
                B1[:, k * 512:(k + 1) * 512], B1[:, k * 512:(k + 1) * 512], AF.Exp,
                scale=-1.0), inc="s_y")
            SY = val("s_y")
            emit("vector", lambda v, k=k: v.tensor_scalar(
                B4[:, k * 512:(k + 1) * 512], B4[:, k * 512:(k + 1) * 512],
                carryg[:, 8 + k:9 + k], None, ALU.add),
                waits=[("s_y", SY), ("d_ldB", LDB)])
            emit("vector", lambda v: v.drain())
            emit("vector", lambda v, k=k: v.tensor_tensor(
                B4[:, k * 512:(k + 1) * 512], B4[:, k * 512:(k + 1) * 512],
                B1[:, k * 512:(k + 1) * 512], ALU.mult))
            emit("vector", lambda v: v.drain())
            emit("vector", lambda v, k=k: v.tensor_tensor(
                B1[:, k * 512:(k + 1) * 512], B5[:, k * 512:(k + 1) * 512],
                B4[:, k * 512:(k + 1) * 512], ALU.mult))
            emit("vector", lambda v: v.drain())
            emit("vector", lambda v, k=k: v.tensor_copy(
                Yh[:, k * 512:(k + 1) * 512], B1[:, k * 512:(k + 1) * 512]))
            emit("vector", lambda v: v.drain())
            emit("vector", lambda v, k=k: v.tensor_tensor(
                Yl[:, k * 512:(k + 1) * 512], B1[:, k * 512:(k + 1) * 512],
                Yh[:, k * 512:(k + 1) * 512], ALU.subtract), inc="v_y")
        Y_V = val("v_y")

        sw_marks = []
        for fc in range(nSW):
            half = fc % 2
            wld = dma("sync", WSp[half][0].rearrange("p (k f) -> p k f", k=nD),
                      wsw_h[fc].rearrange("k p f -> p k f"),
                      f"d_wsw{half}",
                      MARK_SW +
                      ([("t_v", TVF)] if t == 0 and fc == 0 else []) +
                      ([("t_sw", max(0, val("t_sw") - 1))] if val("t_sw") > 1 else []))
            wld = dma("sync", WSp[half][1].rearrange("p (k f) -> p k f", k=nD),
                      wsw_l[fc].rearrange("k p f -> p k f"), f"d_wsw{half}")
            bank = fc % 2
            j = 0
            for (wp, yp) in ((WSp[half][0], Yh), (WSp[half][0], Yl),
                             (WSp[half][1], Yh)):
                for k in range(nD):
                    emit("tensor", lambda tt, k=k, j=j, wp=wp, yp=yp, bank=bank: tt.matmul(
                        ps[bank][:],
                        wp[:, k * 128:(k + 1) * 128],
                        yp[:, k * 512:(k + 1) * 512],
                        start=(j == 0), stop=(j == 3 * nD - 1)),
                        waits=([("v_y", Y_V), (f"d_wsw{half}", wld)] +
                               ([sw_marks[fc - 2]] if fc >= 2 else []))
                              if j == 0 else [],
                        inc=("t_sw" if j == 3 * nD - 1 else None))
                    j += 1
            if fc < nSW // 2:
                emit("vector", lambda v, fc=fc, bank=bank: v.tensor_copy(
                    B4[:, fc * 512:(fc + 1) * 512], ps[bank][:]),
                    waits=[("t_sw", val("t_sw"))], inc="v_sw")
                sw_marks.append(("v_sw", val("v_sw")))
            else:
                emit("scalar", lambda sc, fc=fc, bank=bank: sc.activation(
                    B1[:, (fc - nSW // 2) * 512:(fc - nSW // 2 + 1) * 512],
                    ps[bank][:], AF.Silu),
                    waits=[("t_sw", val("t_sw"))], inc="s_sw")
                sw_marks.append(("s_sw", val("s_sw")))
        SW_V, SW_S = val("v_sw"), val("s_sw")
        for k in range(nD):
            emit("vector", lambda v, k=k: v.tensor_tensor(
                B5[:, k * 512:(k + 1) * 512], B4[:, k * 512:(k + 1) * 512],
                B1[:, k * 512:(k + 1) * 512], ALU.mult),
                waits=[("v_sw", SW_V), ("s_sw", SW_S), ("t_sw", val("t_sw"))]
                      if k == 0 else [])
            emit("vector", lambda v: v.drain())
            emit("vector", lambda v, k=k: v.tensor_copy(
                Yh[:, k * 512:(k + 1) * 512], B5[:, k * 512:(k + 1) * 512]))
            emit("vector", lambda v: v.drain())
            emit("vector", lambda v, k=k: v.tensor_tensor(
                Yl[:, k * 512:(k + 1) * 512], B5[:, k * 512:(k + 1) * 512],
                Yh[:, k * 512:(k + 1) * 512], ALU.subtract), inc="v_usg")
        USG_V = val("v_usg")

        for dc in range(nDO):
            wld = dma("sync", WQp[dc % 2][0].rearrange("p (f d) -> p f d", f=nD),
                      wout_h[:, :, dc * 512:(dc + 1) * 512].rearrange("f p d -> p f d"),
                      f"d_wo{dc % 2}",
                      MARK_WO +
                      ([("t_ou", max(0, val("t_ou") - nSub))] if val("t_ou") > nSub else []))
            wld = dma("sync", WQp[dc % 2][1].rearrange("p (f d) -> p f d", f=nD),
                      wout_l[:, :, dc * 512:(dc + 1) * 512].rearrange("f p d -> p f d"),
                      f"d_wo{dc % 2}")
            for s in range(nSub):
                bank = 2 + (dc * nSub + s) % 2
                j = 0
                for (up, wp) in ((Yh, WQp[dc % 2][0]), (Yh, WQp[dc % 2][1]),
                                 (Yl, WQp[dc % 2][0])):
                    for k in range(nD):
                        emit("tensor", lambda tt, s=s, k=k, j=j, up=up, wp=wp, bank=bank: tt.matmul(
                            ps[bank][:],
                            up[:, k * 512 + s * 128:k * 512 + s * 128 + 128],
                            wp[:, k * 512:(k + 1) * 512],
                            start=(j == 0), stop=(j == 3 * nD - 1)),
                            waits=([("v_usg", USG_V), (f"d_wo{dc % 2}", wld)] +
                                   ([("v_xm", val("v_xm") - 1)] if val("v_xm") >= 2 else []))
                                  if j == 0 else [],
                            inc=("t_ou" if j == 3 * nD - 1 else None))
                        j += 1
                col = t * nSub + s
                emit("vector", lambda v, s=s, dc=dc, col=col, bank=bank: v.scalar_tensor_tensor(
                    XM[:, col * D + dc * 512:col * D + (dc + 1) * 512],
                    ps[bank][:], 1.0, B3[:, s * D + dc * 512:s * D + (dc + 1) * 512],
                    ALU.mult, ALU.add),
                    waits=[("t_ou", val("t_ou")), ("d_ldB", LDB), ("v_y", XZ_V)] +
                          (MARK_WO if t == 0 and dc == 0 and s == 0 else []),
                    inc="v_xm")
        XM_V = val("v_xm")

        for s in range(nSub):
            col = t * nSub + s
            xms = XM[:, col * D:(col + 1) * D]
            wait("vector", [("v_xm", XM_V)])
            ln_alpha(xms, rms_after=False)
            emit("vector", lambda v, s=s, xms=xms: v.tensor_scalar(
                B3[:, s * D:(s + 1) * D], xms, stat[:, 44:45], stat[:, 45:46],
                ALU.subtract, ALU.mult))
            emit("vector", lambda v: v.drain(), inc="v_xr")
        XR_V = val("v_xr")
        dma("scalar", xr_in[TB:TB + TILE, :].rearrange("(s p) d -> p s d", p=128),
            B3[:, 0:nSub * D].rearrange("p (s d) -> p s d", s=nSub),
            "d_xr", [("v_xr", XR_V)])
        emit("gpsimd", lambda g, t=t, TB=TB: g.collective_compute(
            "AllGather", ALU.bypass, replica_groups=[list(range(cfg["n_cores"]))],
            ins=[xr_in[TB:TB + TILE, :].opt()],
            outs=[xr_all[t * cfg["n_cores"] * TILE:(t + 1) * cfg["n_cores"] * TILE, :].opt()]),
            waits=[("d_xr", val("d_xr"))], inc="g_xr")

        XRF_V = transpose_block(
            lambda i: B3[:, (i // nD) * D + (i % nD) * 128:(i // nD) * D + (i % nD) * 128 + 128],
            lambda i: B5[:, (i % nD) * 512 + (i // nD) * 128:(i % nD) * 512 + (i // nD) * 128 + 128],
            nSub * nD, (ps[6], ps[7]), "tD",
            [("v_xr", XR_V), ("t_ou", val("t_ou"))])

        for s in range(nSub):
            col = t * nSub + s
            for k in range(nD):
                emit("tensor", lambda tt, s=s, k=k: tt.matmul(
                    ps[7][0:128, 0:E],
                    B5[:, k * 512 + s * 128:k * 512 + s * 128 + 128],
                    wgt[:, k * E:(k + 1) * E],
                    start=(k == 0), stop=(k == nD - 1)),
                    waits=([("vtD", XRF_V), ("c_ld", C_LD),
                            ("v_gt", val("v_gt")), ("s_gt", val("s_gt"))])
                          if k == 0 else [],
                    inc=("t_gt" if k == nD - 1 else None))
            TGT = val("t_gt")
            emit("vector", lambda v: v.tensor_copy(stat[:, 32:48], ps[7][0:128, 0:E]),
                 waits=[("t_gt", TGT)])
            emit("vector", lambda v: v.drain())
            emit("vector", lambda v: v.reduce_max(stat[:, 48:49], stat[:, 32:48], axis=AXX))
            emit("vector", lambda v: v.drain())
            emit("vector", lambda v: v.tensor_scalar(
                stat[:, 49:50], stat[:, 48:49], -1.0, None, ALU.mult))
            emit("vector", lambda v: v.drain())
            emit("vector", lambda v: v.tensor_scalar(
                stat[:, 50:66], stat[:, 32:48], stat[:, 48:49], None, ALU.is_equal))
            emit("vector", lambda v: v.drain())
            emit("vector", lambda v, col=col: v.scalar_tensor_tensor(
                stat[:, 50:66], stat[:, 50:66], 1.0, iota16[:],
                ALU.mult, ALU.mult, accum_out=idxc[:, col:col + 1]))
            emit("vector", lambda v: v.drain(), inc="v_gt")
            VGT = val("v_gt")
            emit("scalar", lambda sc: sc.activation(
                stat[:, 50:66], stat[:, 32:48], AF.Exp, bias=stat[:, 49:50],
                accum_out=stat[:, 66:67]), waits=[("v_gt", VGT)], inc="s_gt")
            SGT = val("s_gt")
            emit("vector", lambda v, col=col: v.reciprocal(
                gates[:, col:col + 1], stat[:, 66:67]), waits=[("s_gt", SGT)])
            emit("vector", lambda v: v.drain(), inc="v_gt")
    GATES_V = val("v_gt")
    XR_D = val("d_xr")

    CC_XR = ("g_xr", val("g_xr"))

    if phases < 3:
        S_.barrier(exclude=())
        return finish_early()

    S_.barrier(exclude=("g_cc",))
    wait("gpsimd", [("g_cc", CC_CARRY)])
    for j in range(nCols):
        pb = ps[6 + j % 2][0:1, 0:128]
        emit("tensor", lambda tt, j=j, pb=pb: tt.transpose(
            pb, idxc[:, j:j + 1], id128[:]),
            waits=([("v_gt", GATES_V)] if j == 0 else []) +
                  ([("v_ir", val("v_ir") - 1)] if j >= 2 else []),
            inc="t_ir")
        emit("vector", lambda v, j=j, pb=pb: v.tensor_copy(
            IDXROW[:, j * 128:(j + 1) * 128], pb),
            waits=[("t_ir", val("t_ir"))], inc="v_ir")
    IR_V = val("v_ir")
    dma("gpsimd", idxrow_d, IDXROW[0:1, :], "g_d", [("v_ir", IR_V)])
    IRD = val("g_d")
    dma("gpsimd", A(B1_O, T, 0, 16), idxrow_d.unsqueeze(0).partition_broadcast(16),
        "g_d", [("g_d", IRD)])
    B16 = val("g_d")
    emit("vector", lambda v: v.memset(ONES16[:], 1.0))
    emit("vector", lambda v: v.drain(), inc="v_rt")
    emit("vector", lambda v: v.tensor_scalar(
        ROW_OH[:], ROW_OH[:], smallc[:, 0:1], None, ALU.is_equal),
        waits=[("g_d", B16), ("c_ld", C_LD), ("v_rt", val("v_rt"))])
    emit("vector", lambda v: v.drain())
    emit("vector", lambda v: v.tensor_tensor_scan(
        ROW_SC[:], ONES16[:], ROW_OH[:], 0.0, ALU.mult, ALU.add))
    emit("vector", lambda v: v.drain())
    emit("vector", lambda v: v.tensor_copy(stat[0:16, 70:71], ROW_SC[:, T - 1:T]))
    emit("vector", lambda v: v.drain(), inc="v_rt")
    RT1 = val("v_rt")
    dma("gpsimd", cnts_in, stat[0:16, 70:71], "g_d", [("v_rt", RT1)])
    emit("gpsimd", lambda g: g.collective_compute(
        "AllGather", ALU.bypass, replica_groups=[list(range(cfg["n_cores"]))],
        ins=[cnts_in.opt()], outs=[cnts_all.opt()]),
        waits=[("g_d", val("g_d"))], inc="g_cc")
    CC_CNT = val("g_cc")
    dma("gpsimd", c8[:], cnts_all, "g_d", [("g_cc", CC_CNT)])
    C8D = val("g_d")
    emit("tensor", lambda tt: tt.matmul(ps[7][0:1, 0:E], pm8[:], c8[:],
                                        start=True, stop=True),
         waits=[("g_d", C8D), ("c_ld", C_LD), ("v_rt", RT1)], inc="t_rt")
    emit("vector", lambda v: v.tensor_copy(offrow[:], ps[7][0:1, 0:E]),
         waits=[("t_rt", val("t_rt"))])
    emit("vector", lambda v: v.drain(), inc="v_rt")
    emit("tensor", lambda tt: tt.transpose(ps[7][0:16, 0:1], offrow[:], id128[0:1, 0:1]),
         waits=[("v_rt", val("v_rt"))], inc="t_rt")
    emit("vector", lambda v: v.tensor_copy(smallc[:, 3:4], ps[7][0:16, 0:1]),
         waits=[("t_rt", val("t_rt"))])
    emit("vector", lambda v: v.drain(), inc="v_rt")
    OFF_V = val("v_rt")

    emit("vector", lambda v: v.tensor_scalar(
        stat[0:16, 68:69], smallc[:, 3:4], -1.0, float(CAP + 1), ALU.mult, ALU.add),
        waits=[("v_rt", OFF_V)])
    emit("vector", lambda v: v.drain())
    emit("vector", lambda v: v.tensor_scalar(
        ROW_A[:], ROW_SC[:], stat[0:16, 68:69], None, ALU.is_lt))
    emit("vector", lambda v: v.drain())
    emit("vector", lambda v: v.tensor_tensor(ROW_A[:], ROW_A[:], ROW_OH[:], ALU.mult))
    emit("vector", lambda v: v.drain(), inc="v_rt")
    emit("vector", lambda v: v.tensor_tensor(
        stat[0:16, 69:70], smallc[:, 1:2], smallc[:, 3:4], ALU.add))
    emit("vector", lambda v: v.drain())
    emit("vector", lambda v: v.tensor_scalar(
        stat[0:16, 67:68], smallc[:, 3:4], 1.0, -1.0, ALU.mult, ALU.add))
    emit("vector", lambda v: v.drain())
    emit("vector", lambda v: v.tensor_scalar(
        ONES16[:], ROW_SC[:], stat[0:16, 69:70], None, ALU.add))
    emit("vector", lambda v: v.drain())
    emit("vector", lambda v: v.tensor_tensor(ONES16[:], ONES16[:], ROW_A[:], ALU.mult))
    emit("vector", lambda v: v.drain(), inc="v_rt")
    emit("vector", lambda v: v.tensor_scalar(
        ROW_SC[:], ROW_SC[:], stat[0:16, 67:68], float(CAP - 1), ALU.add, ALU.min))
    emit("vector", lambda v: v.drain())
    emit("vector", lambda v: v.tensor_scalar(
        ROW_SC[:], ROW_SC[:], smallc[:, 2:3], None, ALU.add))
    emit("vector", lambda v: v.drain())
    emit("vector", lambda v: v.tensor_tensor(ROW_SC[:], ROW_SC[:], ROW_OH[:], ALU.mult))
    emit("vector", lambda v: v.drain(), inc="v_rt")
    ROWS_V = val("v_rt")
    o16 = sbuf("o16", [16, 1])
    emit("vector", lambda v: v.memset(o16[:], 1.0))
    emit("vector", lambda v: v.drain(), inc="v_rt")
    O16V = val("v_rt")
    nQ = (T + 511) // 512
    for q in range(nQ):
        w = min(512, T - q * 512)
        emit("tensor", lambda tt, q=q, w=w: tt.matmul(
            ps[6][0:1, 0:w], o16[:], ONES16[:, q * 512:q * 512 + w],
            start=True, stop=True),
            waits=[("v_rt", ROWS_V), ("v_rt", O16V)] +
                  ([("v_rq", val("v_rq"))] if q else []),
            inc="t_rq")
        emit("tensor", lambda tt, q=q, w=w: tt.matmul(
            ps[5][0:1, 0:w], o16[:], ROW_A[:, q * 512:q * 512 + w],
            start=True, stop=True), inc="t_rq")
        emit("tensor", lambda tt, q=q, w=w: tt.matmul(
            ps[4][0:1, 0:w], o16[:], ROW_SC[:, q * 512:q * 512 + w],
            start=True, stop=True), inc="t_rq")
        emit("vector", lambda v, q=q, w=w: v.tensor_copy(
            R1[:, q * 512:q * 512 + w], ps[6][0:1, 0:w]),
            waits=[("t_rq", val("t_rq"))])
        emit("vector", lambda v, q=q, w=w: v.tensor_copy(
            R2[:, q * 512:q * 512 + w], ps[5][0:1, 0:w]))
        emit("vector", lambda v, q=q, w=w: v.tensor_copy(
            IDXROW[:, q * 512:q * 512 + w], ps[4][0:1, 0:w]))
        emit("vector", lambda v: v.drain(), inc="v_rq")
    CS_V = val("v_rq")
    emit("vector", lambda v: v.tensor_scalar(
        R2[:], R2[:], -BIG, BIG, ALU.mult, ALU.add), waits=[("v_rq", CS_V)])
    emit("vector", lambda v: v.drain())
    emit("vector", lambda v: v.tensor_tensor(R1[:], R1[:], R2[:], ALU.add))
    emit("vector", lambda v: v.drain(), inc="v_rq")
    DF_V = val("v_rq")
    dma("gpsimd", rowb[0, 0:T], R1[0:1, :], "g_d", [("v_rq", DF_V)])
    dma("gpsimd", rowb[1, 0:T], IDXROW[0:1, :], "g_d", [("v_rq", DF_V)])
    dma("gpsimd", rowb[2, 0:T], R2[0:1, :], "g_d", [("v_rq", DF_V)])
    RB_D = val("g_d")
    dma("gpsimd", dcol[:], rowb[0, 0:T].rearrange("(c p) -> p c", p=128),
        "g_d", [("g_d", RB_D)], slow=True)
    dma("gpsimd", rcol[:], rowb[1, 0:T].rearrange("(c p) -> p c", p=128), "g_d", slow=True)
    dma("gpsimd", mcol[:], rowb[2, 0:T].rearrange("(c p) -> p c", p=128), "g_d", slow=True)
    COLS_D = val("g_d")
    emit("vector", lambda v: v.tensor_copy(dcoli[:], dcol[:]),
         waits=[("g_d", COLS_D)])
    emit("vector", lambda v: v.tensor_copy(rcoli[:], rcol[:]))
    emit("vector", lambda v: v.drain())
    emit("vector", lambda v: v.tensor_scalar(mcol[:], mcol[:], 0.0, None, ALU.is_equal))
    emit("vector", lambda v: v.drain())
    emit("vector", lambda v: v.tensor_tensor(mcol[:], mcol[:], gates[:], ALU.mult))
    emit("vector", lambda v: v.drain(), inc="v_cl")
    CL_V = val("v_cl")

    for j in range(nCols):
        emit("gpsimd", lambda g, j=j: g.indirect_dma_start(
            out=smap_in.unsqueeze(0),
            out_offset=IndirectOffsetOnAxis(ap=dcoli[:, j:j + 1], axis=1),
            in_=tok1[:, j:j + 1], in_offset=None,
            bounds_check=S - 1, oob_is_err=False),
            waits=([("v_cl", CL_V), ("g_z", Z_S), ("c_ld", C_LD)] if j == 0
                   else [("g_sc", val("g_sc"))]),
            inc="g_sc", amt=16)
    SCAT = val("g_sc")
    emit("gpsimd", lambda g: g.collective_compute(
        "ReduceScatter", ALU.add, replica_groups=[list(range(cfg["n_cores"]))],
        ins=[smap_in.opt()], outs=[smap_out.opt()]),
        waits=[("g_sc", SCAT)], inc="g_cc")
    CC_RS = val("g_cc")
    dma("gpsimd", gmapf[:], smap_out.rearrange("(c p) -> p c", p=128),
        "g_d", [("g_cc", CC_RS)], slow=True)
    GM_D = val("g_d")
    emit("vector", lambda v: v.tensor_scalar(
        gmapf[:], gmapf[:], -1.0, 0.0, ALU.add, ALU.max), waits=[("g_d", GM_D)])
    emit("vector", lambda v: v.drain())
    emit("vector", lambda v: v.tensor_copy(gmapi[:], gmapf[:]))
    emit("vector", lambda v: v.drain(), inc="v_gm")
    GM_V = val("v_gm")

    if phases < 4:
        S_.barrier(exclude=())
        return finish_early()

    S_.barrier(exclude=("g_cc", "g_xr"))
    wait("gpsimd", [CC_XR])
    BF16 = mybir.dt.bfloat16
    HSh = A(B1_O, nF * S_TILE // 2).bitcast(BF16)
    HSl = A(B1_O + nF * S_TILE // 2, nF * S_TILE // 2).bitcast(BF16)
    XFh = A(B5_O, nD * S_TILE // 2).bitcast(BF16)
    XFl = A(B5_O + nD * S_TILE // 2, nD * S_TILE // 2).bitcast(BF16)
    W2land = [A(WQ_O, 1024), A(WQ_O + 1024, 1024)]
    W2p = [(A(WQ_O + 2048, 512).bitcast(BF16), A(WQ_O + 2560, 512).bitcast(BF16)),
           (A(WQ_O + 3072, 512).bitcast(BF16), A(WQ_O + 3584, 512).bitcast(BF16))]
    W1land = [A(WQ_O + 4096, 1024), A(WQ_O + 5120, 1024)]
    W1p = [(A(WQ_O + 6144, 512).bitcast(BF16), A(WQ_O + 6656, 512).bitcast(BF16)),
           (A(WQ_O + 7168, 512).bitcast(BF16), A(WQ_O + 7680, 512).bitcast(BF16))]
    XG2 = A(M2_O, D)
    STRIP = A(M2_O + D, 512)
    PIECE = [A(WS_O, 128), A(WS_O + 128, 128)]
    eo_marks = []
    for ex in range(cfg["EXP_PC"]):
        for sti in range(nST):
            for b in range(S_TILE // 128):
                jj = ex * (CAP // 128) + sti * (S_TILE // 128) + b
                emit("gpsimd", lambda g, jj=jj: g.indirect_dma_start(
                    out=XG2[:], out_offset=None,
                    in_=xr_all,
                    in_offset=IndirectOffsetOnAxis(ap=gmapi[:, jj:jj + 1], axis=0)),
                    waits=[("v_gm", GM_V), CC_XR,
                           ("txf", val("txf")), ("t_h", val("t_h")),
                           ("g_ig", val("g_ig"))],
                    inc="g_ig", amt=16)
                IG = val("g_ig")
                for k in range(nD):
                    i = b * nD + k
                    pb = ps[4 + i % 2][:, 0:128]
                    emit("tensor", lambda tt, k=k, i=i, pb=pb: tt.transpose(
                        pb, XG2[:, k * 128:(k + 1) * 128], id128[:]),
                        waits=[("g_ig", IG)] +
                              ([("vxf", val("vxf") - 1), ("sxf", val("sxf") - 1)]
                               if val("vxf") >= 2 else []),
                        inc="txf")
                    co = k * S_TILE + b * 128
                    emit("scalar", lambda sc, co=co, pb=pb: sc.activation(
                        XFh[:, co:co + 128], pb, AF.Copy),
                        waits=[("txf", val("txf"))], inc="sxf")
                    emit("vector", lambda v, co=co, pb=pb: v.tensor_tensor(
                        XFl[:, co:co + 128], pb, XFh[:, co:co + 128], ALU.subtract),
                        waits=[("txf", val("txf")), ("sxf", val("sxf"))], inc="vxf")
            XF_V, XF_S = val("vxf"), val("sxf")

            w1split = {}

            def w1_stage(fc):
                half = fc % 2
                wld = dma("sync", W1land[half][:].rearrange("p (k f) -> p k f", k=nD),
                          w1b[ex, fc].rearrange("k p f -> p k f"),
                          f"d_w1{half}",
                          [("s_w1", max(0, val("s_w1") - 1)),
                           ("v_w1", max(0, val("v_w1") - 1))] if val("v_w1") > 1 else [])
                emit("scalar", lambda sc, half=half: sc.copy(
                    W1p[half][0][:], W1land[half][:]),
                    waits=[(f"d_w1{half}", wld)] +
                          ([("t_h", max(0, val("t_h") - 1))] if val("t_h") > 1 else []),
                    inc="s_w1")
                emit("vector", lambda v, half=half: v.tensor_tensor(
                    W1p[half][1][:], W1land[half][:], W1p[half][0][:], ALU.subtract),
                    waits=[("s_w1", val("s_w1"))] +
                          ([("t_h", max(0, val("t_h") - 1))] if val("t_h") > 1 else []),
                    inc="v_w1")
                w1split[fc] = val("v_w1")

            w1_stage(0)
            for fc in range(nF):
                half = fc % 2
                bank = fc % 2
                mm3 = [(W1p[half][0], XFh), (W1p[half][0], XFl), (W1p[half][1], XFh)]
                nmm = 3 * nD
                j = 0
                for (wp, xp) in mm3:
                    for k in range(nD):
                        emit("tensor", lambda tt, wp=wp, xp=xp, k=k, j=j, bank=bank: tt.matmul(
                            ps[bank][:, 0:S_TILE],
                            wp[:, k * 128:(k + 1) * 128],
                            xp[:, k * S_TILE:(k + 1) * S_TILE],
                            start=(j == 0), stop=(j == nmm - 1)),
                            waits=([("vxf", XF_V), ("sxf", XF_S), ("v_w1", w1split[fc])] +
                                   ([("s_h", val("s_h") - 1), ("v_h", val("v_h") - 1)]
                                    if val("s_h") >= 2 else []))
                                  if j == 0 else [],
                            inc=("t_h" if j == nmm - 1 else None))
                        j += 1
                if fc + 1 < nF:
                    w1_stage(fc + 1)
                ho = fc * S_TILE
                emit("scalar", lambda sc, ho=ho, bank=bank: sc.activation(
                    HSh[:, ho:ho + S_TILE], ps[bank][:, 0:S_TILE], AF.Relu),
                    waits=[("t_h", val("t_h"))] +
                          ([("t_o2g", val("t_o2g"))] if fc == 0 else []), inc="s_h")
                emit("vector", lambda v, ho=ho, bank=bank: v.scalar_tensor_tensor(
                    HSl[:, ho:ho + S_TILE], ps[bank][:, 0:S_TILE], 0.0,
                    HSh[:, ho:ho + S_TILE], ALU.max, ALU.subtract),
                    waits=[("t_h", val("t_h")), ("s_h", val("s_h"))], inc="v_h")
            H_S, H_V = val("s_h"), val("v_h")

            nFG = 8
            nG = nF // nFG
            w2split = {}

            def w2_stage(ex, dc, g):
                gh = g % 2
                w2ld = dma("sync",
                           W2land[gh][:].rearrange("p (f d) -> p f d", f=nFG),
                           w2b[ex, dc, g * nFG:(g + 1) * nFG].rearrange("f p d -> p f d"),
                           f"d_w2{gh}",
                           [("s_w2", max(0, val("s_w2") - 1)),
                            ("v_w2", max(0, val("v_w2") - 1))] if val("v_w2") > 1 else [])
                emit("scalar", lambda sc, gh=gh: sc.copy(
                    W2p[gh][0][:], W2land[gh][:]),
                    waits=[(f"d_w2{gh}", w2ld)] +
                          ([("t_o2g", max(0, val("t_o2g") - 1))] if val("t_o2g") > 1 else []),
                    inc="s_w2")
                emit("vector", lambda v, gh=gh: v.tensor_tensor(
                    W2p[gh][1][:], W2land[gh][:], W2p[gh][0][:], ALU.subtract),
                    waits=[("s_w2", val("s_w2"))] +
                          ([("t_o2g", max(0, val("t_o2g") - 1))] if val("t_o2g") > 1 else []),
                    inc="v_w2")
                w2split[(dc, g)] = val("v_w2")

            for dc in range(nD):
                bank = 2 + dc % 2
                j2 = 0
                nmm2 = 3 * nF
                if dc == 0:
                    w2_stage(ex, 0, 0)
                for g in range(nG):
                    gh = g % 2
                    for (sel, xsel) in ((0, "h"), (0, "l"), (1, "h")):
                        for fi in range(nFG):
                            fc = g * nFG + fi
                            hs = HSh if xsel == "h" else HSl
                            emit("tensor", lambda tt, sel=sel, hs=hs, fi=fi, fc=fc,
                                 gh=gh, j2=j2, bank=bank: tt.matmul(
                                ps[bank][:, 0:S_TILE],
                                W2p[gh][sel][:, fi * 128:(fi + 1) * 128],
                                hs[:, fc * S_TILE:(fc + 1) * S_TILE],
                                start=(j2 == 0), stop=(j2 == nmm2 - 1)),
                                waits=(([("s_h", H_S), ("v_h", H_V),
                                         ("v_w2", w2split[(dc, g)])] +
                                        ([("v_o2", val("v_o2") - 1)] if val("v_o2") >= 2 else []))
                                       if j2 == 0 else
                                       ([("v_w2", w2split[(dc, g)])]
                                        if fi == 0 and sel == 0 and xsel == "h" else [])),
                                inc=("t_o2g" if (fi == nFG - 1 and sel == 1) else None))
                            j2 += 1
                    if g + 1 < nG:
                        w2_stage(ex, dc, g + 1)
                    elif dc + 1 < nD:
                        w2_stage(ex, dc + 1, 0)
                emit("vector", lambda v, bank=bank: v.tensor_copy(
                    STRIP[:], ps[bank][:, 0:S_TILE]),
                    waits=[("t_o2g", val("t_o2g"))], inc="v_o2")
                O2V = val("v_o2")
                for b in range(S_TILE // 128):
                    pb = ps[6 + b % 2][:, 0:128]
                    emit("tensor", lambda tt, b=b, pb=pb: tt.transpose(
                        pb, STRIP[:, b * 128:(b + 1) * 128], id128[:]),
                        waits=[("v_o2", O2V)] +
                              ([("veo", val("veo") - 1)] if val("veo") >= 2 else []),
                        inc="teo")
                    par = b % 2
                    emit("vector", lambda v, b=b, pb=pb, par=par: v.tensor_copy(
                        PIECE[par][:], pb),
                        waits=[("teo", val("teo")),
                               (f"d_eo{par}", val(f"d_eo{par}"))], inc="veo")
                    base = ex * CAP + sti * S_TILE + b * 128
                    dma("scalar", eo_in[base:base + 128, dc * 128:(dc + 1) * 128],
                        PIECE[par][:], f"d_eo{par}", [("veo", val("veo"))])
        eo_marks.append((ex, val("d_eo0"), val("d_eo1")))
    for (ex, v0, v1) in eo_marks:
        emit("gpsimd", lambda g, ex=ex: g.collective_compute(
            "AllGather", ALU.bypass, replica_groups=[list(range(cfg["n_cores"]))],
            ins=[eo_in[ex * CAP:(ex + 1) * CAP, :].opt()],
            outs=[eo_all[ex * (S // 2):(ex + 1) * (S // 2), :].opt()]),
            waits=[("d_eo0", v0), ("d_eo1", v1)], inc="g_cc")
    CC_EO = val("g_cc")
    S_.barrier(exclude=("g_cc",))
    GAp = [A(B1_O, D), A(B1_O + D, D)]
    OTp = [A(B2_O, D), A(B2_O + D, D)]
    for j in range(nCols):
        par = j % 2
        emit("gpsimd", lambda g, j=j, par=par: g.indirect_dma_start(
            out=GAp[par][:], out_offset=None,
            in_=eo_all,
            in_offset=IndirectOffsetOnAxis(ap=rcoli[:, j:j + 1], axis=0)),
            waits=([("g_cc", CC_EO)] if j == 0 else []) +
                  [("v_fin", max(0, val("v_fin") - 1)),
                   (f"g_ig{par}", val(f"g_ig{par}"))],
            inc=f"g_ig{par}", amt=16)
        IGE = val(f"g_ig{par}")
        emit("vector", lambda v, j=j, par=par: v.scalar_tensor_tensor(
            OTp[par][:], GAp[par][:], mcol[:, j:j + 1], XM[:, j * D:(j + 1) * D],
            ALU.mult, ALU.add),
            waits=[(f"g_ig{par}", IGE), (f"d_out{par}", val(f"d_out{par}"))])
        emit("vector", lambda v: v.drain(), inc="v_fin")
        dma("scalar", out_hbm[j * 128:(j + 1) * 128, :], OTp[par][:],
            f"d_out{par}", [("v_fin", val("v_fin"))])
    wait("gpsimd", [("d_out0", val("d_out0")), ("d_out1", val("d_out1"))])
    wait("sync", [("d_out0", val("d_out0")), ("d_out1", val("d_out1"))])

    block = es.enter_context(nc.Block())
    S_.build(block)
    es.close()
    return nc


def host_prep(cfg, L0_attn_x, L0_attn_w_qkv, L0_attn_w_swiglu, L0_attn_w_out,
              L0_ffn_Wg, L0_ffn_w1, L0_ffn_w2):
    n, T, D, E, F = cfg["n_cores"], cfg["T"], cfg["D"], cfg["E"], cfg["F"]
    nD, nF, CAP, nCols = cfg["nD"], cfg["nF"], cfg["CAP"], cfg["nCols"]
    QK = 2 * D
    nSW = (2 * D) // 128
    x = np.asarray(L0_attn_x, dtype=np.float32).reshape(n * T, D)
    wqkv = np.asarray(L0_attn_w_qkv, dtype=np.float32)
    wswg = np.asarray(L0_attn_w_swiglu, dtype=np.float32)
    wout = np.asarray(L0_attn_w_out, dtype=np.float32)
    wgm = np.asarray(L0_ffn_Wg, dtype=np.float32)
    w1 = np.asarray(L0_ffn_w1, dtype=np.float32)
    w2 = np.asarray(L0_ffn_w2, dtype=np.float32)

    wqk_b = np.ascontiguousarray(wqkv[:QK].T.reshape(nD, 128, QK))
    wv_b = np.ascontiguousarray(
        wqkv[QK:].T.reshape(nD, 128, nD, 128).transpose(2, 0, 1, 3))
    wsw_b = np.ascontiguousarray(
        wswg.T.reshape(nD, 128, nSW, 128).transpose(2, 0, 1, 3))
    wout_b = np.ascontiguousarray(wout.T.reshape(nD, 128, D))
    wg_b = np.ascontiguousarray(wgm.reshape(nD, 128, E))
    ident = np.eye(128, dtype=np.float32)
    iota16r = np.tile(np.arange(E, dtype=np.float32), (128, 1))
    iotaEc = np.arange(E, dtype=np.float32).reshape(E, 1)
    ebias = (np.arange(E, dtype=np.float32) * CAP - 1.0).reshape(E, 1)
    eids = np.arange(E, dtype=np.float32)
    rbias = ((eids % 2) * (n * T / 2) + (eids // 2) * CAP).reshape(E, 1).astype(np.float32)

    in_maps = []
    epc = cfg["EXP_PC"]
    for c in range(n):
        w1c = w1[epc * c:epc * (c + 1)]
        w2c = w2[epc * c:epc * (c + 1)]
        w1_b = np.ascontiguousarray(
            w1c.reshape(epc, nD, 128, nF, 128).transpose(0, 3, 1, 2, 4))
        w2_b = np.ascontiguousarray(
            w2c.reshape(epc, nF, 128, nD, 128).transpose(0, 3, 1, 2, 4))
        TILE_ = cfg["TILE"]
        cols = np.arange(nCols)
        pp = np.arange(128)
        tt_ = cols // (TILE_ // 128)
        off = (cols % (TILE_ // 128))[None, :] * 128 + pp[:, None]
        tokid1 = (tt_[None, :] * (n * TILE_) + c * TILE_ + off + 1.0).astype(np.float32)
        pm = np.zeros((8, 1), dtype=np.float32)
        pm[:c] = 1.0
        in_maps.append({
            "x": np.ascontiguousarray(x[c * T:(c + 1) * T]),
            "wqk": wqk_b, "wv": wv_b, "wsw": wsw_b, "wout": wout_b, "wg": wg_b,
            "w1b": w1_b, "w2b": w2_b,
            "ident": ident, "iota16r": iota16r, "iotaEc": iotaEc,
            "ebias": ebias, "rbias": rbias,
            "tokid1": np.ascontiguousarray(tokid1),
            "prevmask": pm,
        })
    return in_maps


def kernel(L0_attn_x, L0_attn_w_qkv, L0_attn_w_swiglu, L0_attn_w_out,
           L0_ffn_Wg, L0_ffn_w1, L0_ffn_w2):
    cfg = _cfg()
    in_maps = host_prep(cfg, L0_attn_x, L0_attn_w_qkv, L0_attn_w_swiglu,
                        L0_attn_w_out, L0_ffn_Wg, L0_ffn_w1, L0_ffn_w2)
    nc = build_nc(cfg)
    res = run_bass_kernel_spmd(nc, in_maps, core_ids=list(range(cfg["n_cores"])))
    out = np.concatenate([np.asarray(r["out"]) for r in res.results], axis=0)
    return out.reshape(1, cfg["S"], cfg["D"]).astype(np.float32)


# version 20
# speedup vs baseline: 1.0125x; 1.0125x over previous
import os
import numpy as np
from contextlib import ExitStack

import concourse.bass as bass
import concourse.mybir as mybir
from concourse.bass import IndirectOffsetOnAxis
from concourse.bass_utils import run_bass_kernel_spmd

F32 = mybir.dt.float32
I32 = mybir.dt.int32
AF = mybir.ActivationFunctionType
ALU = mybir.AluOpType
AXX = mybir.AxisListType.X

FP32_EPS = 1.1920929e-07
LN_EPS = 1e-5
ENGINES = ("tensor", "vector", "scalar", "gpsimd", "sync")


def _cfg(n_cores=8, T=2048, D=1024, E=16, F=4096, TILE=512, S_TILE=512):
    import os
    c = dict(n_cores=n_cores, T=T, D=D, E=E, F=F, TILE=TILE, S_TILE=S_TILE)
    c["S"] = n_cores * T
    c["CAP"] = c["S"] // E
    c["nD"] = D // 128
    c["nSub"] = TILE // 128
    c["nTiles"] = int(os.environ.get("K_NTILES", T // TILE))
    c["nCols"] = T // 128
    c["EXP_PC"] = E // n_cores
    c["SLOTS"] = c["EXP_PC"] * c["CAP"]
    c["nF"] = F // 128
    c["nST"] = c["CAP"] // S_TILE
    return c


class Sched:
    def __init__(self, nc, es):
        self.nc, self.es = nc, es
        self.ops = {e: [] for e in ENGINES}
        self.sems, self.cnt = {}, {}

    def sem(self, name):
        if name not in self.sems:
            self.sems[name] = self.es.enter_context(self.nc.semaphore(name))
            self.cnt[name] = 0
        return self.sems[name]

    def val(self, name):
        return self.cnt.get(name, 0)

    def emit(self, eng, fn, waits=(), inc=None, amt=1):
        w = [(self.sem(s), v) for (s, v) in waits if v > 0]
        if inc is not None:
            self.sem(inc)
            self.cnt[inc] += amt
        self.ops[eng].append((fn, w, (self.sems[inc], amt) if inc else None))

    def wait(self, eng, waits):
        self.emit(eng, None, waits=waits)

    def barrier(self, engines=ENGINES, exclude=()):
        snap = [(s, v) for s, v in self.cnt.items() if v > 0 and s not in exclude]
        for e in engines:
            self.emit(e, None, waits=snap)

    def build(self, block):
        for ename in ENGINES:
            stream = self.ops[ename]
            if not stream:
                continue

            def mk(stream):
                def body(eng):
                    for fn, waits, inc in stream:
                        for sem, v in waits:
                            eng.wait_ge(sem, v)
                        inst = fn(eng) if fn is not None else None
                        if inc is not None:
                            assert inst is not None
                            inst.then_inc(inc[0], inc[1])
                return body

            getattr(block, ename)(mk(stream))


def build_nc(cfg, dbg=False, phases=5):
    nD, TILE, nSub, nTiles = cfg["nD"], cfg["TILE"], cfg["nSub"], cfg["nTiles"]
    D, E, F, T = cfg["D"], cfg["E"], cfg["F"], cfg["T"]
    CAP, S, SLOTS, nCols, nF = cfg["CAP"], cfg["S"], cfg["SLOTS"], cfg["nCols"], cfg["nF"]
    S_TILE, nST = cfg["S_TILE"], cfg["nST"]
    QK = 2 * D
    nQKe = max(1, QK // 512)
    nSW = (2 * D) // 128
    nDO = max(1, D // 512)
    BIG = float(1 << 22)

    assert TILE == 512, "FM chunk width is hardwired to 512"
    nc = bass.Bass("TRN2", target_bir_lowering=False, dynamic_dma_scratch_size=26624)

    def dram_in(name, shape, dt=F32):
        return nc.dram_tensor(name, shape, dt, kind="ExternalInput").ap()

    x_hbm = dram_in("x", [T, D])
    wqk = dram_in("wqk", [nD, 128, QK])
    wv = dram_in("wv", [nD, nD, 128, 128])
    wsw = dram_in("wsw", [nSW, nD, 128, 128])
    wout = dram_in("wout", [nD, 128, D])
    wg = dram_in("wg", [nD, 128, E])
    w1b = dram_in("w1b", [cfg["EXP_PC"], nF, nD, 128, 128])
    w2b = dram_in("w2b", [cfg["EXP_PC"], nD, nF, 128, 128])
    ident = dram_in("ident", [128, 128])
    iota16r = dram_in("iota16r", [128, E])
    iotaEc = dram_in("iotaEc", [E, 1])
    ebias = dram_in("ebias", [E, 1])
    rbias = dram_in("rbias", [E, 1])
    tokid1 = dram_in("tokid1", [128, nCols])
    prevmask = dram_in("prevmask", [8, 1])

    def dram(name, shape, dt=F32):
        return nc.dram_tensor(name, shape, dt).ap()

    BF16D = mybir.dt.bfloat16
    wqk_h = dram("wqk_h", [nD, 128, QK], BF16D)
    wqk_l = dram("wqk_l", [nD, 128, QK], BF16D)
    wv_h = dram("wv_h", [nD, nD, 128, 128], BF16D)
    wv_l = dram("wv_l", [nD, nD, 128, 128], BF16D)
    wsw_h = dram("wsw_h", [nSW, nD, 128, 128], BF16D)
    wsw_l = dram("wsw_l", [nSW, nD, 128, 128], BF16D)
    wout_h = dram("wout_h", [nD, 128, D], BF16D)
    wout_l = dram("wout_l", [nD, 128, D], BF16D)
    sp_wcs = dram("sp_wcs", [D, T])
    sp_kvcs = dram("sp_kvcs", [D, T])
    sp_sigq = dram("sp_sigq", [D, T])
    sp_ms = dram("sp_ms", [128, nTiles * 2 * nSub])
    carry_in = dram("carry_in", [128, 16])
    carry_all = nc.dram_tensor("carry_all", [8 * 128, 16], F32, addr_space="Shared").ap()
    idxrow_d = dram("idxrow_d", [T])
    cnts_in = dram("cnts_in", [E])
    cnts_all = nc.dram_tensor("cnts_all", [8, E], F32, addr_space="Shared").ap()
    rowb = dram("rowb", [4, max(T, 2048)])
    xr_in = dram("xr_in", [T, D])
    xr_all = nc.dram_tensor("xr_all", [S, D], F32, addr_space="Shared").ap()
    smap_in = dram("smap_in", [S])
    smap_out = dram("smap_out", [SLOTS])
    eo_in = dram("eo_in", [SLOTS, D])
    eo_all = nc.dram_tensor("eo_all", [S, D], F32, addr_space="Shared").ap()
    out_hbm = nc.dram_tensor("out", [T, D], F32, kind="ExternalOutput").ap()
    dbg_hbm = nc.dram_tensor("dbgo", [16, max(T, D)], F32, kind="ExternalOutput").ap() if dbg else None

    es = ExitStack()

    AW = 16384 + 5 * (nSub * D) + 4 * nD * 512 + 2 * nD * 128 + 2048
    XM_O = 0
    B1_O = 16384
    B2_O = B1_O + nSub * D
    B3_O = B2_O + nSub * D
    B4_O = B3_O + nSub * D
    B5_O = B4_O + nSub * D
    WQ_O = B5_O + nSub * D
    WQ_W = 4 * nD * 512
    WS_O = WQ_O + 2 * nD * 512
    WS_W = 2 * nD * 128
    M2_O = WS_O + WS_W
    AW = M2_O + max(2048, 2 * D)
    AR = es.enter_context(nc.sbuf_tensor("AR", [128, AW], F32))

    def A(off, w, p0=0, p1=128):
        return AR[p0:p1, off:off + w]

    XM = A(XM_O, 16384)
    B1 = A(B1_O, nSub * D)
    B2 = A(B2_O, nSub * D)
    B3 = A(B3_O, nSub * D)
    B4 = A(B4_O, nSub * D)
    B5 = A(B5_O, nSub * D)
    WQh = [A(WQ_O, nD * 512), A(WQ_O + nD * 512, nD * 512)]
    WSh = [A(WS_O, nD * 128), A(WS_O + nD * 128, nD * 128)]
    WQp = [(A(WQ_O, nD * 256).bitcast(mybir.dt.bfloat16),
            A(WQ_O + nD * 256, nD * 256).bitcast(mybir.dt.bfloat16)),
           (A(WQ_O + nD * 512, nD * 256).bitcast(mybir.dt.bfloat16),
            A(WQ_O + nD * 512 + nD * 256, nD * 256).bitcast(mybir.dt.bfloat16))]
    WSp = [(A(WS_O, nD * 64).bitcast(mybir.dt.bfloat16),
            A(WS_O + nD * 64, nD * 64).bitcast(mybir.dt.bfloat16)),
           (A(WS_O + nD * 128, nD * 64).bitcast(mybir.dt.bfloat16),
            A(WS_O + nD * 128 + nD * 64, nD * 64).bitcast(mybir.dt.bfloat16))]
    XNh = A(B3_O, nSub * D // 2).bitcast(mybir.dt.bfloat16)
    XNl = A(B3_O + nSub * D // 2, nSub * D // 2).bitcast(mybir.dt.bfloat16)
    Yh = A(B2_O, nSub * D // 2).bitcast(mybir.dt.bfloat16)
    Yl = A(B2_O + nSub * D // 2, nSub * D // 2).bitcast(mybir.dt.bfloat16)
    SQA = A(XM_O, nSub * D)
    ROW_OH = A(B1_O, T, 0, 16)
    ROW_SC = A(B2_O, T, 0, 16)
    ROW_A = A(B3_O, T, 0, 16)
    ONES16 = A(B4_O, T, 0, 16)
    R1 = A(B5_O, T, 0, 1)
    R2 = A(B5_O + T, T, 0, 1)
    IDXROW = A(B5_O + 2 * T if 3 * T <= nSub * D else WQ_O, T, 0, 1)
    CGROW = A(M2_O, 2048, 0, 1)
    HS = A(B1_O, nF * S_TILE)
    XF = A(B5_O, nD * S_TILE)
    XG = A(M2_O, D)
    EOT = A(M2_O + D, D)
    W2HW = (nF // 2) * 128
    W2h = [A(WQ_O, W2HW), A(WQ_O + W2HW, W2HW)]
    OUT2D = A(WQ_O + 2 * W2HW, nD * S_TILE)
    GA = A(B1_O, D)
    OT = A(B2_O, D)

    assert nF * S_TILE <= 4 * nSub * D, "HS overflows B1..B4 region"
    assert nD * S_TILE <= nSub * D, "XF overflows B5"
    assert 2 * D <= max(2048, 2 * D), "XG/EOT overflow M2"
    assert 2 * W2HW + nD * S_TILE <= 2 * 2 * nD * 512, "W2+OUT2 overflow WQ region"

    def sbuf(name, shape, dt=F32):
        return es.enter_context(nc.sbuf_tensor(name, shape, dt))

    stat = sbuf("stat", [128, 96])
    carry = sbuf("carry", [128, 16])
    carryg = sbuf("carryg", [128, 24])
    gates = sbuf("gates", [128, nCols])
    idxc = sbuf("idxc", [128, nCols])
    mcol = sbuf("mcol", [128, nCols])
    rcol = sbuf("rcol", [128, nCols])
    dcol = sbuf("dcol", [128, nCols])
    dcoli = sbuf("dcoli", [128, nCols], I32)
    rcoli = sbuf("rcoli", [128, nCols], I32)
    tok1 = sbuf("tok1", [128, nCols])
    gmapi = sbuf("gmapi", [128, SLOTS // 128], I32)
    gmapf = sbuf("gmapf", [128, SLOTS // 128])
    id128 = sbuf("id128", [128, 128])
    iota16 = sbuf("iota16", [128, E])
    wgt = sbuf("wgt", [128, nD * E])
    smallc = sbuf("smallc", [16, 8])
    c8 = sbuf("c8", [8, E])
    pm8 = sbuf("pm8", [8, 1])
    ones512 = sbuf("ones512", [128, 512])
    zz = sbuf("zz", [128, max(nCols, S // 128, 16)])
    offrow = sbuf("offrow", [1, E])

    ps = [es.enter_context(nc.psum_tensor(f"ps{i}", [128, 512], F32)) for i in range(8)]

    S_ = Sched(nc, es)
    emit, val, wait = S_.emit, S_.val, S_.wait

    def dma(eng, out, in_, sem, waits=(), slow=False):
        emit(eng, lambda e, o=out, i=in_, s=slow: e.dma_start(
            out=o, in_=i, allow_slow_non_contiguous=s),
             waits=waits, inc=sem, amt=16)
        return val(sem)

    dma("sync", id128[:], ident, "c_ld")
    dma("sync", iota16[:], iota16r, "c_ld")
    dma("sync", smallc[:, 0:1], iotaEc, "c_ld")
    dma("sync", smallc[:, 1:2], ebias, "c_ld")
    dma("sync", smallc[:, 2:3], rbias, "c_ld")
    dma("sync", tok1[:], tokid1, "c_ld")
    dma("sync", pm8[:], prevmask, "c_ld")
    for k in range(nD):
        dma("sync", wgt[:, k * E:(k + 1) * E], wg[k], "c_ld")
    C_LD = val("c_ld")
    emit("vector", lambda v: v.memset(carry[:], 0.0))
    emit("vector", lambda v: v.memset(zz[:], 0.0))
    emit("vector", lambda v: v.memset(ones512[:], 1.0))
    emit("vector", lambda v: v.drain(), inc="v")
    V_INIT = val("v")
    dma("gpsimd", smap_in.rearrange("(p c) -> p c", p=128), zz[:, 0:S // 128],
        "g_z", [("v", V_INIT)])
    Z_S = val("g_z")

    if os.environ.get("K_PH0"):
        wait("vector", [("c_ld", C_LD), ("v", V_INIT), ("g_z", Z_S)])
        dma("gpsimd", out_hbm[0:128, :], AR[:, 0:D], "d_out")
        wait("gpsimd", [("d_out", val("d_out"))])
        blk0 = es.enter_context(nc.Block())
        S_.build(blk0)
        es.close()
        return nc

    def ln_alpha(src_ap, rms_after=True):
        emit("vector", lambda v, s=src_ap: v.bn_stats(stat[:, 48:54], s[:, 0:512]))
        emit("vector", lambda v, s=src_ap: v.bn_stats(stat[:, 54:60], s[:, 512:1024]))
        emit("vector", lambda v: v.drain())
        emit("vector", lambda v: v.bn_aggr(stat[:, 44:46], stat[:, 48:60]))
        emit("vector", lambda v: v.drain())
        emit("vector", lambda v: v.tensor_scalar(
            stat[:, 60:61], stat[:, 45:46], LN_EPS, None, ALU.add))
        emit("vector", lambda v: v.drain())
        emit("vector", lambda v: v.reciprocal(stat[:, 61:62], stat[:, 60:61]))
        emit("vector", lambda v: v.drain())
        if rms_after:
            emit("vector", lambda v: v.tensor_tensor(
                stat[:, 62:63], stat[:, 45:46], stat[:, 61:62], ALU.mult))
            emit("vector", lambda v: v.drain())
            emit("vector", lambda v: v.tensor_scalar(
                stat[:, 63:64], stat[:, 62:63], FP32_EPS, None, ALU.add))
            emit("vector", lambda v: v.drain())
            emit("vector", lambda v: v.reciprocal(stat[:, 64:65], stat[:, 63:64]))
        emit("vector", lambda v: v.drain(), inc="v")
        vv = val("v")
        emit("scalar", lambda sc: sc.activation(stat[:, 65:66], stat[:, 61:62], AF.Sqrt),
             waits=[("v", vv)])
        if rms_after:
            emit("scalar", lambda sc: sc.drain())
            emit("scalar", lambda sc: sc.activation(stat[:, 66:67], stat[:, 64:65], AF.Sqrt))
        emit("scalar", lambda sc: sc.drain(), inc="s")
        sv = val("s")
        if rms_after:
            emit("vector", lambda v: v.tensor_tensor(
                stat[:, 45:46], stat[:, 65:66], stat[:, 66:67], ALU.mult),
                waits=[("s", sv)])
        else:
            emit("vector", lambda v: v.tensor_copy(stat[:, 45:46], stat[:, 65:66]),
                 waits=[("s", sv)])
        emit("vector", lambda v: v.drain(), inc="v")

    def transpose_block(src_fn, dst_fn, n, bankpair, tag, pre_waits, rows=128):
        for i in range(n):
            pb = bankpair[i % 2][0:rows, 0:128]
            emit("tensor", lambda tt, i=i, pb=pb: tt.transpose(
                pb, src_fn(i), id128[:]),
                waits=(pre_waits if i == 0 else []) +
                      ([(f"v{tag}", val(f"v{tag}") - 1)] if i >= 2 else []),
                inc=f"t{tag}")
            emit("vector", lambda v, i=i, pb=pb: v.tensor_copy(dst_fn(i), pb),
                waits=[(f"t{tag}", val(f"t{tag}"))], inc=f"v{tag}")
        return val(f"v{tag}")

    BF16 = mybir.dt.bfloat16
    X0LD = dma("sync", SQA[:, 0:nSub * D].rearrange("p (s d) -> p s d", s=nSub),
               x_hbm[0:TILE, :].rearrange("(s p) d -> p s d", p=128), "d_x")
    PW = [dict(land=A(XM_O + 4096 + i * 4096, 2048),
               hi=A(XM_O + 4096 + i * 4096 + 2048, 1024).bitcast(BF16),
               lo=A(XM_O + 4096 + i * 4096 + 3072, 1024).bitcast(BF16))
          for i in range(2)]
    pw_n = [0]

    def presplit(fp32_src, hi_dst, lo_dst, width, on_gp=False):
        i = pw_n[0] % 2
        pw_n[0] += 1
        st = PW[i]
        ld = dma("scalar", st["land"][:, 0:width], fp32_src, f"d_pwi{i}",
                 [(f"d_pwo{i}", val(f"d_pwo{i}"))])
        if on_gp:
            emit("gpsimd", lambda g, st=st, width=width: g.tensor_copy(
                st["hi"][:, 0:width], st["land"][:, 0:width]),
                waits=[(f"d_pwi{i}", ld)], inc="g_pwh")
            emit("gpsimd", lambda g, st=st, width=width: g.tensor_tensor(
                st["lo"][:, 0:width], st["land"][:, 0:width], st["hi"][:, 0:width],
                ALU.subtract),
                waits=[("g_pwh", val("g_pwh"))], inc="g_pwl")
            PWV = ("g_pwl", val("g_pwl"))
        else:
            emit("scalar", lambda sc, st=st, width=width: sc.copy(
                st["hi"][:, 0:width], st["land"][:, 0:width]),
                waits=[(f"d_pwi{i}", ld)], inc="s_pw")
            emit("vector", lambda v, st=st, width=width: v.tensor_tensor(
                st["lo"][:, 0:width], st["land"][:, 0:width], st["hi"][:, 0:width],
                ALU.subtract),
                waits=[("s_pw", val("s_pw"))], inc="v_pw")
            PWV = ("v_pw", val("v_pw"))
        dma("scalar", hi_dst, st["hi"][:, 0:width], f"d_pwo{i}", [PWV])
        dma("scalar", lo_dst, st["lo"][:, 0:width], f"d_pwo{i}")

    MARKS = {}

    def presplit_qkv():
        for k in range(nD):
            presplit(wqk[k], wqk_h[k], wqk_l[k], QK)
        MARKS["QK"] = [("d_pwo0", val("d_pwo0")), ("d_pwo1", val("d_pwo1"))]
        for e in range(nD):
            presplit(wv[e].rearrange("k p f -> p k f"),
                     wv_h[e].rearrange("k p f -> p k f"),
                     wv_l[e].rearrange("k p f -> p k f"), nD * 128)
        MARKS["WV"] = [("d_pwo0", val("d_pwo0")), ("d_pwo1", val("d_pwo1"))]

    def prefetch_stats(xld_mark, tile):
        B = 72
        DMS0 = val("d_sms")
        for s in range(nSub):
            xs = SQA[:, s * D:(s + 1) * D]
            emit("vector", lambda v, xs=xs: v.bn_stats(stat[:, B + 4:B + 10], xs[:, 0:512]),
                 waits=[("d_x", xld_mark)] if s == 0 else [])
            emit("vector", lambda v, xs=xs: v.bn_stats(stat[:, B + 10:B + 16], xs[:, 512:1024]))
            emit("vector", lambda v: v.drain())
            emit("vector", lambda v: v.bn_aggr(stat[:, B:B + 2], stat[:, B + 4:B + 16]))
            emit("vector", lambda v: v.drain())
            emit("vector", lambda v: v.tensor_scalar(
                stat[:, B + 16:B + 17], stat[:, B + 1:B + 2], LN_EPS, None, ALU.add))
            emit("vector", lambda v: v.drain())
            emit("vector", lambda v: v.reciprocal(stat[:, B + 17:B + 18], stat[:, B + 16:B + 17]))
            emit("vector", lambda v: v.drain())
            emit("vector", lambda v: v.tensor_tensor(
                stat[:, B + 18:B + 19], stat[:, B + 1:B + 2], stat[:, B + 17:B + 18], ALU.mult))
            emit("vector", lambda v: v.drain())
            emit("vector", lambda v: v.tensor_scalar(
                stat[:, B + 19:B + 20], stat[:, B + 18:B + 19], FP32_EPS, None, ALU.add))
            emit("vector", lambda v: v.drain())
            emit("vector", lambda v: v.reciprocal(stat[:, B + 20:B + 21], stat[:, B + 19:B + 20]))
            emit("vector", lambda v: v.drain(), inc="v_pf")
            PFV = val("v_pf")
            emit("scalar", lambda sc: sc.activation(
                stat[:, B + 21:B + 22], stat[:, B + 17:B + 18], AF.Sqrt),
                waits=[("v_pf", PFV)])
            emit("scalar", lambda sc: sc.drain())
            emit("scalar", lambda sc: sc.activation(
                stat[:, B + 22:B + 23], stat[:, B + 20:B + 21], AF.Sqrt))
            emit("scalar", lambda sc: sc.drain(), inc="s_pf")
            SPF = val("s_pf")
            emit("vector", lambda v, s=s: v.tensor_copy(stat[:, s:s + 1], stat[:, B:B + 1]),
                 waits=[("s_pf", SPF)] + ([("d_sms", DMS0)] if s == 0 else []))
            emit("vector", lambda v, s=s: v.tensor_copy(
                stat[:, nSub + s:nSub + s + 1], stat[:, B + 21:B + 22]))
            emit("vector", lambda v, s=s: v.tensor_tensor(
                stat[:, 8 + s:9 + s], stat[:, B + 21:B + 22], stat[:, B + 22:B + 23],
                ALU.mult))
            emit("vector", lambda v: v.drain(), inc="v_pf")
        dma("scalar", sp_ms[:, tile * 2 * nSub:(tile + 1) * 2 * nSub],
            stat[:, 0:2 * nSub], "d_sms", [("v_pf", val("v_pf"))])
        return val("v_pf")

    STOPA = int(os.environ.get("K_STOPA", "0"))

    def bail():
        snap = [(s, v) for s, v in S_.cnt.items() if v > 0]
        for e in ENGINES:
            S_.emit(e, None, waits=snap)
        dma("gpsimd", out_hbm[0:128, :], AR[:, 0:D], "d_out")
        wait("gpsimd", [("d_out", val("d_out"))])
        blkb = es.enter_context(nc.Block())
        S_.build(blkb)
        es.close()
        return nc

    for t in range(nTiles):
        TB = t * TILE
        tb_snap = [(s, v) for s, v in S_.cnt.items() if v > 0]
        if t == 0:
            wait("vector", [("c_ld", C_LD), ("v", V_INIT)])
            PF = prefetch_stats(X0LD, 0)
        for s in range(nSub):
            emit("vector", lambda v, s=s: v.tensor_scalar(
                B2[:, s * D:(s + 1) * D], SQA[:, s * D:(s + 1) * D],
                stat[:, s:s + 1], stat[:, 8 + s:9 + s],
                ALU.subtract, ALU.mult),
                waits=([("v_pf", val("v_pf")), ("s_pf", val("s_pf"))] +
                       (tb_snap if t > 0 else [])) if s == 0 else [])
            emit("vector", lambda v: v.drain(), inc="v_xn")
        XN_V = val("v_xn")
        if t == 0:
            presplit_qkv()
        if STOPA == 2:
            return bail()

        for i in range(nSub * nD):
            s_, k_ = i // nD, i % nD
            pb = (ps[4], ps[5])[i % 2][:, 0:128]
            co = k_ * 512 + s_ * 128
            co2 = k_ * 512 + s_ * 128
            emit("tensor", lambda tt, i=i, s_=s_, k_=k_, pb=pb: tt.transpose(
                pb, B2[:, s_ * D + k_ * 128:s_ * D + k_ * 128 + 128], id128[:]),
                waits=([("v_xn", XN_V)] if i == 0 else []) +
                      ([("vtA", val("vtA") - 1), ("stA", val("stA") - 1)] if i >= 2 else []),
                inc="ttA")
            emit("scalar", lambda sc, co=co, pb=pb: sc.activation(
                XNh[:, co:co + 128], pb, AF.Copy),
                waits=[("ttA", val("ttA"))], inc="stA")
            emit("vector", lambda v, co=co, pb=pb: v.tensor_tensor(
                XNl[:, co:co + 128], pb, XNh[:, co:co + 128], ALU.subtract),
                waits=[("ttA", val("ttA")), ("stA", val("stA"))], inc="vtA")
        XNF_V, XNF_S = val("vtA"), val("stA")
        if STOPA == 3:
            return bail()

        for ec in range(nQKe):
            half = ec % 2
            wqld = dma("sync", WQp[half][0].rearrange("p (k e) -> p k e", k=nD),
                       wqk_h[:, :, ec * 512:(ec + 1) * 512].rearrange("k p e -> p k e"),
                       f"d_wqk{half}",
                       MARKS["QK"] +
                       ([("t_qk", max(0, val("t_qk") - nSub))] if val("t_qk") > nSub else []))
            wqld = dma("sync", WQp[half][1].rearrange("p (k e) -> p k e", k=nD),
                       wqk_l[:, :, ec * 512:(ec + 1) * 512].rearrange("k p e -> p k e"),
                       f"d_wqk{half}")
            for s in range(nSub):
                bank = (ec * nSub + s) % 2
                j = 0
                for (xp, wp) in ((XNh, WQp[half][0]), (XNh, WQp[half][1]),
                                 (XNl, WQp[half][0])):
                    for k in range(nD):
                        emit("tensor", lambda tt, s=s, k=k, j=j, xp=xp, wp=wp, bank=bank: tt.matmul(
                            ps[bank][:],
                            xp[:, k * 512 + s * 128:k * 512 + s * 128 + 128],
                            wp[:, k * 512:(k + 1) * 512],
                            start=(j == 0), stop=(j == 3 * nD - 1)),
                            waits=([("vtA", XNF_V), ("stA", XNF_S), (f"d_wqk{half}", wqld)] +
                                   ([("v_qk", val("v_qk") - 1)] if val("v_qk") >= 2 else []))
                                  if j == 0 else [],
                            inc=("t_qk" if j == 3 * nD - 1 else None))
                        j += 1
                dst = B2 if ec < nQKe // 2 else B1
                eoff = (ec % (nQKe // 2)) * 512
                emit("vector", lambda v, s=s, dst=dst, eoff=eoff, bank=bank: v.tensor_copy(
                    dst[:, s * D + eoff:s * D + eoff + 512], ps[bank][:]),
                    waits=[("t_qk", val("t_qk"))], inc="v_qk")
        QK_V = val("v_qk")
        if t + 1 < nTiles:
            xldn = dma("sync", SQA[:, 0:nSub * D].rearrange("p (s d) -> p s d", s=nSub),
                       x_hbm[(t + 1) * TILE:(t + 2) * TILE, :].rearrange(
                           "(s p) d -> p s d", p=128),
                       "d_x", [("v_xn", XN_V), ("s_sq", val("s_sq"))])
            prefetch_stats(xldn, t + 1)
        if STOPA == 4:
            return bail()

        for ec in range(nD):
            half = ec % 2
            wvld = dma("sync", WSp[half][0].rearrange("p (k f) -> p k f", k=nD),
                       wv_h[ec].rearrange("k p f -> p k f"),
                       f"d_wv{half}",
                       MARKS["WV"] +
                       ([("t_v", max(0, val("t_v") - 1))] if val("t_v") > 1 else []))
            wvld = dma("sync", WSp[half][1].rearrange("p (k f) -> p k f", k=nD),
                       wv_l[ec].rearrange("k p f -> p k f"), f"d_wv{half}")
            bank = 2 + ec % 2
            j = 0
            for (wp, xp) in ((WSp[half][0], XNh), (WSp[half][0], XNl),
                             (WSp[half][1], XNh)):
                for k in range(nD):
                    emit("tensor", lambda tt, k=k, j=j, wp=wp, xp=xp, bank=bank: tt.matmul(
                        ps[bank][:],
                        wp[:, k * 128:(k + 1) * 128],
                        xp[:, k * 512:(k + 1) * 512],
                        start=(j == 0), stop=(j == 3 * nD - 1)),
                        waits=([("vtA", XNF_V), ("stA", XNF_S), (f"d_wv{half}", wvld)] +
                               ([("v_v", val("v_v") - 1)] if val("v_v") >= 2 else []))
                              if j == 0 else [],
                        inc=("t_v" if j == 3 * nD - 1 else None))
                    j += 1
            emit("vector", lambda v, ec=ec, bank=bank: v.tensor_copy(
                B4[:, ec * 512:(ec + 1) * 512], ps[bank][:]),
                waits=[("t_v", val("t_v"))], inc="v_v")
        V_V = val("v_v")
        if STOPA == 5:
            return bail()

        for s in range(nSub):
            emit("scalar", lambda sc, s=s: sc.activation(
                A(B3_O, nSub * D)[:, s * D:(s + 1) * D], B2[:, s * D:(s + 1) * D],
                AF.Square, accum_out=stat[:, 16 + s:17 + s]),
                waits=[("v_qk", QK_V), ("t_v", val("t_v"))], inc="s_sq")
            emit("scalar", lambda sc: sc.drain())
            emit("scalar", lambda sc, s=s: sc.activation(
                A(B3_O, nSub * D)[:, s * D:(s + 1) * D], B1[:, s * D:(s + 1) * D],
                AF.Square,
                accum_out=stat[:, 20 + s:21 + s]), inc="s_sq")
            SQS = val("s_sq")
            for j, off in ((16 + s, 24 + s), (20 + s, 28 + s)):
                emit("vector", lambda v, j=j: v.tensor_scalar(
                    stat[:, 60:61], stat[:, j:j + 1], 1.0 / D, FP32_EPS, ALU.mult, ALU.add),
                    waits=[("s_sq", SQS), ("s_a", val("s_a"))])
                emit("vector", lambda v: v.drain())
                emit("vector", lambda v: v.reciprocal(stat[:, 61:62], stat[:, 60:61]))
                emit("vector", lambda v: v.drain(), inc="v_a")
                emit("scalar", lambda sc, off=off: sc.activation(
                    stat[:, off:off + 1], stat[:, 61:62], AF.Sqrt),
                    waits=[("v_a", val("v_a"))], inc="s_a")
            AQK = val("s_a")
            emit("scalar", lambda sc, s=s: sc.activation(
                B2[:, s * D:(s + 1) * D], B2[:, s * D:(s + 1) * D], AF.Sigmoid,
                scale=stat[:, 24 + s:25 + s]), waits=[("s_a", AQK)], inc="s_g")
            emit("scalar", lambda sc, s=s: sc.activation(
                B1[:, s * D:(s + 1) * D], B1[:, s * D:(s + 1) * D], AF.Exp,
                scale=stat[:, 28 + s:29 + s]), inc="s_g")
        GATE_S = val("s_g")
        if STOPA == 6:
            return bail()

        TRB_V = transpose_block(
            lambda i: B2[:, (i // nD) * D + (i % nD) * 128:(i // nD) * D + (i % nD) * 128 + 128],
            lambda i: B5[:, (i % nD) * 512 + (i // nD) * 128:(i % nD) * 512 + (i // nD) * 128 + 128],
            nSub * nD, (ps[4], ps[5]), "tB", [("s_g", GATE_S)])
        TRC_V = transpose_block(
            lambda i: B1[:, (i // nD) * D + (i % nD) * 128:(i // nD) * D + (i % nD) * 128 + 128],
            lambda i: B3[:, (i % nD) * 512 + (i // nD) * 128:(i % nD) * 512 + (i // nD) * 128 + 128],
            nSub * nD, (ps[6], ps[7]), "tC",
            [("s_g", GATE_S), ("t_v", val("t_v")), ("v_v", V_V)])
        if STOPA == 7:
            return bail()

        for k in range(nD):
            emit("vector", lambda v, k=k: v.tensor_tensor(
                B2[:, k * 512:(k + 1) * 512], B3[:, k * 512:(k + 1) * 512],
                B4[:, k * 512:(k + 1) * 512], ALU.mult),
                waits=[("vtC", TRC_V), ("v_v", V_V)] if k == 0 else [],
                inc="v_kv")
        KV_V = val("v_kv")
        for k in range(nD):
            emit("vector", lambda v, k=k: v.tensor_tensor_scan(
                B1[:, k * 512:(k + 1) * 512], ones512[:], B3[:, k * 512:(k + 1) * 512],
                carry[:, k:k + 1], ALU.mult, ALU.add),
                waits=[("v_kv", KV_V)] if k == 0 else [])
            emit("vector", lambda v: v.drain())
            emit("vector", lambda v, k=k: v.tensor_copy(
                carry[:, k:k + 1], B1[:, (k + 1) * 512 - 1:(k + 1) * 512]))
            emit("vector", lambda v: v.drain(), inc="v_sc")
            emit("vector", lambda v, k=k: v.tensor_tensor_scan(
                B4[:, k * 512:(k + 1) * 512], ones512[:], B2[:, k * 512:(k + 1) * 512],
                carry[:, 8 + k:9 + k], ALU.mult, ALU.add))
            emit("vector", lambda v: v.drain())
            emit("vector", lambda v, k=k: v.tensor_copy(
                carry[:, 8 + k:9 + k], B4[:, (k + 1) * 512 - 1:(k + 1) * 512]))
            emit("vector", lambda v: v.drain(), inc="v_sc")
        SC_V = val("v_sc")
        if STOPA == 8:
            return bail()

        for k in range(nD):
            dma("scalar", sp_sigq[k * 128:(k + 1) * 128, TB:TB + TILE],
                B5[:, k * 512:(k + 1) * 512], "d_sp",
                [("vtB", TRB_V)] if k == 0 else [])
            dma("scalar", sp_wcs[k * 128:(k + 1) * 128, TB:TB + TILE],
                B1[:, k * 512:(k + 1) * 512], "d_sp",
                [("v_sc", SC_V)] if k == 0 else [])
            dma("scalar", sp_kvcs[k * 128:(k + 1) * 128, TB:TB + TILE],
                B4[:, k * 512:(k + 1) * 512], "d_sp", [])
    SPILL_D = val("d_sp")
    TVF = val("t_v")

    for f in range(nSW):
        presplit(wsw[f].rearrange("k p f2 -> p k f2"),
                 wsw_h[f].rearrange("k p f2 -> p k f2"),
                 wsw_l[f].rearrange("k p f2 -> p k f2"), nD * 128)
    MARK_SW = [("d_pwo0", val("d_pwo0")), ("d_pwo1", val("d_pwo1"))]
    for f in range(nD):
        presplit(wout[f], wout_h[f], wout_l[f], D)
    MARK_WO = [("d_pwo0", val("d_pwo0")), ("d_pwo1", val("d_pwo1"))]

    SKIP_CC = bool(os.environ.get("K_SKIP_CC"))
    dma("gpsimd", carry_in, carry[:], "g_d", [("v_sc", val("v_sc"))])
    if not SKIP_CC:
        emit("gpsimd", lambda g: g.collective_compute(
            "AllGather", ALU.bypass, replica_groups=[list(range(cfg["n_cores"]))],
            ins=[carry_in.opt()], outs=[carry_all.opt()]),
            waits=[("g_d", val("g_d"))], inc="g_cc")
    else:
        dma("gpsimd", carry_all[0:128, :], carry_in, "g_d", [("g_d", val("g_d"))])
        emit("gpsimd", lambda g: g.nop(), waits=[("g_d", val("g_d"))], inc="g_cc")
    CC_CARRY = val("g_cc")
    car8 = A(B1_O, 2048, 0, 8)
    dma("gpsimd", car8.rearrange("j (p c) -> j p c", p=128),
        carry_all.rearrange("(j p) c -> j p c", j=8),
        "g_d", [("g_cc", CC_CARRY), ("d_sp", SPILL_D)])
    CAR_D = val("g_d")
    for q in range(4):
        emit("tensor", lambda tt, q=q: tt.matmul(
            ps[7][0:1, :], pm8[:], car8[:, q * 512:(q + 1) * 512],
            start=True, stop=True),
            waits=[("g_d", CAR_D), ("c_ld", C_LD)] + ([("v_cg", val("v_cg"))] if q else []),
            inc="t_cg")
        emit("vector", lambda v, q=q: v.tensor_copy(
            CGROW[:, q * 512:(q + 1) * 512], ps[7][0:1, :]),
            waits=[("t_cg", val("t_cg"))], inc="v_cg")
    CG_V = val("v_cg")
    dma("gpsimd", rowb[3, 0:2048], CGROW[:], "g_d", [("v_cg", CG_V)])
    dma("gpsimd", carryg[:, 0:16], rowb[3, 0:2048].rearrange("(p c) -> p c", p=128),
        "g_d", [("g_d", val("g_d"))])
    CARRYG_D = val("g_d")

    for k in range(nD):
        dma("sync", B5[:, k * 512:(k + 1) * 512],
            sp_sigq[k * 128:(k + 1) * 128, 0:TILE], "d_ldB",
            [("d_sp", SPILL_D)] if k == 0 else [])
        dma("sync", B4[:, k * 512:(k + 1) * 512],
            sp_kvcs[k * 128:(k + 1) * 128, 0:TILE], "d_ldB")
    dma("sync", B3[:, 0:nSub * D].rearrange("p (s d) -> p s d", s=nSub),
        x_hbm[0:TILE, :].rearrange("(s p) d -> p s d", p=128), "d_ldB")
    dma("sync", stat[:, 16:16 + 2 * nSub], sp_ms[:, 0:2 * nSub], "d_ldB",
        [("d_sms", val("d_sms"))])
    for k in range(nD):
        dma("sync", B1[:, k * 512:(k + 1) * 512],
            sp_wcs[k * 128:(k + 1) * 128, 0:TILE], "d_ldB",
            [("v_cg", CG_V)] if k == 0 else [])
    LDB0 = val("d_ldB")

    def finish_early():
        dma("gpsimd", out_hbm[0:128, :], zz[:, 0:1].partition_broadcast(128) if False else AR[:, 0:D], "d_out")
        wait("gpsimd", [("d_out", val("d_out"))])
        blk = es.enter_context(nc.Block())
        S_.build(blk)
        es.close()
        return nc

    if phases < 2:
        S_.barrier()
        return finish_early()


    for t in range(nTiles):
        TB = t * TILE
        tb_snap = [(s, v) for s, v in S_.cnt.items()
                   if v > 0 and s[0] in "tvs"] + [("d_xr", val("d_xr"))]
        if t > 0:
            for k in range(nD):
                dma("sync", B5[:, k * 512:(k + 1) * 512],
                    sp_sigq[k * 128:(k + 1) * 128, TB:TB + TILE], "d_ldB",
                    tb_snap if k == 0 else [])
                dma("sync", B1[:, k * 512:(k + 1) * 512],
                    sp_wcs[k * 128:(k + 1) * 128, TB:TB + TILE], "d_ldB")
                dma("sync", B4[:, k * 512:(k + 1) * 512],
                    sp_kvcs[k * 128:(k + 1) * 128, TB:TB + TILE], "d_ldB")
            dma("sync", B3[:, 0:nSub * D].rearrange("p (s d) -> p s d", s=nSub),
                x_hbm[TB:TB + TILE, :].rearrange("(s p) d -> p s d", p=128),
                "d_ldB")
            dma("sync", stat[:, 16:16 + 2 * nSub],
                sp_ms[:, t * 2 * nSub:(t + 1) * 2 * nSub], "d_ldB",
                [("d_sms", val("d_sms"))])
        LDB = val("d_ldB")
        CWP_PRE = val("v_y")

        for s in range(nSub):
            xs3 = B3[:, s * D:(s + 1) * D]
            emit("vector", lambda v, s=s: v.tensor_scalar(
                stat[:, 63:64], stat[:, 16 + nSub + s:17 + nSub + s], 1.0,
                None, ALU.add),
                waits=[("d_ldB", LDB)] if s == 0 else [])
            emit("vector", lambda v: v.drain())
            emit("vector", lambda v: v.reciprocal(stat[:, 64:65], stat[:, 63:64]))
            emit("vector", lambda v: v.drain())
            emit("vector", lambda v, s=s: v.tensor_tensor(
                stat[:, 65:66], stat[:, 16 + s:17 + s],
                stat[:, 16 + nSub + s:17 + nSub + s], ALU.mult))
            emit("vector", lambda v: v.drain())
            emit("vector", lambda v: v.tensor_tensor(
                stat[:, 65:66], stat[:, 65:66], stat[:, 64:65], ALU.mult))
            emit("vector", lambda v: v.drain())
            emit("vector", lambda v, xs3=xs3: v.tensor_scalar(
                xs3, xs3, stat[:, 65:66], stat[:, 63:64], ALU.subtract, ALU.mult))
            emit("vector", lambda v: v.drain(), inc="v_y")
        XZ_V = val("v_y")
        if t == 0:
            emit("vector", lambda v: v.tensor_scalar(
                carryg[:, 16:24], carryg[:, 0:8], 1e-6, None, ALU.add),
                waits=[("g_d", CARRYG_D)])
            emit("vector", lambda v: v.drain(), inc="v_y")
        CWP_V = val("v_y") if t == 0 else CWP_PRE
        for k in range(nD):
            emit("scalar", lambda sc, k=k: sc.activation(
                B1[:, k * 512:(k + 1) * 512], B1[:, k * 512:(k + 1) * 512], AF.Ln,
                bias=carryg[:, 16 + k:17 + k]),
                waits=[("v_y", CWP_V), ("d_ldB", LDB)] if k == 0 else [])
            emit("scalar", lambda sc: sc.drain())
            emit("scalar", lambda sc, k=k: sc.activation(
                B1[:, k * 512:(k + 1) * 512], B1[:, k * 512:(k + 1) * 512], AF.Exp,
                scale=-1.0), inc="s_y")
            SY = val("s_y")
            emit("vector", lambda v, k=k: v.tensor_scalar(
                B4[:, k * 512:(k + 1) * 512], B4[:, k * 512:(k + 1) * 512],
                carryg[:, 8 + k:9 + k], None, ALU.add),
                waits=[("s_y", SY), ("d_ldB", LDB)])
            emit("vector", lambda v: v.drain())
            emit("vector", lambda v, k=k: v.tensor_tensor(
                B4[:, k * 512:(k + 1) * 512], B4[:, k * 512:(k + 1) * 512],
                B1[:, k * 512:(k + 1) * 512], ALU.mult))
            emit("vector", lambda v: v.drain())
            emit("vector", lambda v, k=k: v.tensor_tensor(
                B1[:, k * 512:(k + 1) * 512], B5[:, k * 512:(k + 1) * 512],
                B4[:, k * 512:(k + 1) * 512], ALU.mult))
            emit("vector", lambda v: v.drain())
            emit("vector", lambda v, k=k: v.tensor_copy(
                Yh[:, k * 512:(k + 1) * 512], B1[:, k * 512:(k + 1) * 512]))
            emit("vector", lambda v: v.drain())
            emit("vector", lambda v, k=k: v.tensor_tensor(
                Yl[:, k * 512:(k + 1) * 512], B1[:, k * 512:(k + 1) * 512],
                Yh[:, k * 512:(k + 1) * 512], ALU.subtract), inc="v_y")
        Y_V = val("v_y")

        sw_marks = []
        for fc in range(nSW):
            half = fc % 2
            wld = dma("sync", WSp[half][0].rearrange("p (k f) -> p k f", k=nD),
                      wsw_h[fc].rearrange("k p f -> p k f"),
                      f"d_wsw{half}",
                      MARK_SW +
                      ([("t_v", TVF)] if t == 0 and fc == 0 else []) +
                      ([("t_sw", max(0, val("t_sw") - 1))] if val("t_sw") > 1 else []))
            wld = dma("sync", WSp[half][1].rearrange("p (k f) -> p k f", k=nD),
                      wsw_l[fc].rearrange("k p f -> p k f"), f"d_wsw{half}")
            bank = fc % 2
            j = 0
            for (wp, yp) in ((WSp[half][0], Yh), (WSp[half][0], Yl),
                             (WSp[half][1], Yh)):
                for k in range(nD):
                    emit("tensor", lambda tt, k=k, j=j, wp=wp, yp=yp, bank=bank: tt.matmul(
                        ps[bank][:],
                        wp[:, k * 128:(k + 1) * 128],
                        yp[:, k * 512:(k + 1) * 512],
                        start=(j == 0), stop=(j == 3 * nD - 1)),
                        waits=([("v_y", Y_V), (f"d_wsw{half}", wld)] +
                               ([sw_marks[fc - 2]] if fc >= 2 else []))
                              if j == 0 else [],
                        inc=("t_sw" if j == 3 * nD - 1 else None))
                    j += 1
            if fc < nSW // 2:
                emit("vector", lambda v, fc=fc, bank=bank: v.tensor_copy(
                    B4[:, fc * 512:(fc + 1) * 512], ps[bank][:]),
                    waits=[("t_sw", val("t_sw"))], inc="v_sw")
                sw_marks.append(("v_sw", val("v_sw")))
            else:
                emit("scalar", lambda sc, fc=fc, bank=bank: sc.activation(
                    B1[:, (fc - nSW // 2) * 512:(fc - nSW // 2 + 1) * 512],
                    ps[bank][:], AF.Silu),
                    waits=[("t_sw", val("t_sw"))], inc="s_sw")
                sw_marks.append(("s_sw", val("s_sw")))
        SW_V, SW_S = val("v_sw"), val("s_sw")
        for k in range(nD):
            emit("vector", lambda v, k=k: v.tensor_tensor(
                B5[:, k * 512:(k + 1) * 512], B4[:, k * 512:(k + 1) * 512],
                B1[:, k * 512:(k + 1) * 512], ALU.mult),
                waits=[("v_sw", SW_V), ("s_sw", SW_S), ("t_sw", val("t_sw"))]
                      if k == 0 else [])
            emit("vector", lambda v: v.drain())
            emit("vector", lambda v, k=k: v.tensor_copy(
                Yh[:, k * 512:(k + 1) * 512], B5[:, k * 512:(k + 1) * 512]))
            emit("vector", lambda v: v.drain())
            emit("vector", lambda v, k=k: v.tensor_tensor(
                Yl[:, k * 512:(k + 1) * 512], B5[:, k * 512:(k + 1) * 512],
                Yh[:, k * 512:(k + 1) * 512], ALU.subtract), inc="v_usg")
        USG_V = val("v_usg")

        for dc in range(nDO):
            wld = dma("sync", WQp[dc % 2][0].rearrange("p (f d) -> p f d", f=nD),
                      wout_h[:, :, dc * 512:(dc + 1) * 512].rearrange("f p d -> p f d"),
                      f"d_wo{dc % 2}",
                      MARK_WO +
                      ([("t_ou", max(0, val("t_ou") - nSub))] if val("t_ou") > nSub else []))
            wld = dma("sync", WQp[dc % 2][1].rearrange("p (f d) -> p f d", f=nD),
                      wout_l[:, :, dc * 512:(dc + 1) * 512].rearrange("f p d -> p f d"),
                      f"d_wo{dc % 2}")
            for s in range(nSub):
                bank = 2 + (dc * nSub + s) % 2
                j = 0
                for (up, wp) in ((Yh, WQp[dc % 2][0]), (Yh, WQp[dc % 2][1]),
                                 (Yl, WQp[dc % 2][0])):
                    for k in range(nD):
                        emit("tensor", lambda tt, s=s, k=k, j=j, up=up, wp=wp, bank=bank: tt.matmul(
                            ps[bank][:],
                            up[:, k * 512 + s * 128:k * 512 + s * 128 + 128],
                            wp[:, k * 512:(k + 1) * 512],
                            start=(j == 0), stop=(j == 3 * nD - 1)),
                            waits=([("v_usg", USG_V), (f"d_wo{dc % 2}", wld)] +
                                   ([("v_xm", val("v_xm") - 1)] if val("v_xm") >= 2 else []))
                                  if j == 0 else [],
                            inc=("t_ou" if j == 3 * nD - 1 else None))
                        j += 1
                col = t * nSub + s
                emit("vector", lambda v, s=s, dc=dc, col=col, bank=bank: v.scalar_tensor_tensor(
                    XM[:, col * D + dc * 512:col * D + (dc + 1) * 512],
                    ps[bank][:], 1.0, B3[:, s * D + dc * 512:s * D + (dc + 1) * 512],
                    ALU.mult, ALU.add),
                    waits=[("t_ou", val("t_ou")), ("d_ldB", LDB), ("v_y", XZ_V)] +
                          (MARK_WO if t == 0 and dc == 0 and s == 0 else []),
                    inc="v_xm")
        XM_V = val("v_xm")

        for s in range(nSub):
            col = t * nSub + s
            xms = XM[:, col * D:(col + 1) * D]
            wait("vector", [("v_xm", XM_V)])
            ln_alpha(xms, rms_after=False)
            emit("vector", lambda v, s=s, xms=xms: v.tensor_scalar(
                B3[:, s * D:(s + 1) * D], xms, stat[:, 44:45], stat[:, 45:46],
                ALU.subtract, ALU.mult))
            emit("vector", lambda v: v.drain(), inc="v_xr")
        XR_V = val("v_xr")
        dma("scalar", xr_in[TB:TB + TILE, :].rearrange("(s p) d -> p s d", p=128),
            B3[:, 0:nSub * D].rearrange("p (s d) -> p s d", s=nSub),
            "d_xr", [("v_xr", XR_V)])
        emit("gpsimd", lambda g, t=t, TB=TB: g.collective_compute(
            "AllGather", ALU.bypass, replica_groups=[list(range(cfg["n_cores"]))],
            ins=[xr_in[TB:TB + TILE, :].opt()],
            outs=[xr_all[t * cfg["n_cores"] * TILE:(t + 1) * cfg["n_cores"] * TILE, :].opt()]),
            waits=[("d_xr", val("d_xr"))], inc="g_xr")

        XRF_V = transpose_block(
            lambda i: B3[:, (i // nD) * D + (i % nD) * 128:(i // nD) * D + (i % nD) * 128 + 128],
            lambda i: B5[:, (i % nD) * 512 + (i // nD) * 128:(i % nD) * 512 + (i // nD) * 128 + 128],
            nSub * nD, (ps[6], ps[7]), "tD",
            [("v_xr", XR_V), ("t_ou", val("t_ou"))])

        for s in range(nSub):
            col = t * nSub + s
            for k in range(nD):
                emit("tensor", lambda tt, s=s, k=k: tt.matmul(
                    ps[7][0:128, 0:E],
                    B5[:, k * 512 + s * 128:k * 512 + s * 128 + 128],
                    wgt[:, k * E:(k + 1) * E],
                    start=(k == 0), stop=(k == nD - 1)),
                    waits=([("vtD", XRF_V), ("c_ld", C_LD),
                            ("v_gt", val("v_gt")), ("s_gt", val("s_gt"))])
                          if k == 0 else [],
                    inc=("t_gt" if k == nD - 1 else None))
            TGT = val("t_gt")
            emit("vector", lambda v: v.tensor_copy(stat[:, 32:48], ps[7][0:128, 0:E]),
                 waits=[("t_gt", TGT)])
            emit("vector", lambda v: v.drain())
            emit("vector", lambda v: v.reduce_max(stat[:, 48:49], stat[:, 32:48], axis=AXX))
            emit("vector", lambda v: v.drain())
            emit("vector", lambda v: v.tensor_scalar(
                stat[:, 49:50], stat[:, 48:49], -1.0, None, ALU.mult))
            emit("vector", lambda v: v.drain())
            emit("vector", lambda v: v.tensor_scalar(
                stat[:, 50:66], stat[:, 32:48], stat[:, 48:49], None, ALU.is_equal))
            emit("vector", lambda v: v.drain())
            emit("vector", lambda v, col=col: v.scalar_tensor_tensor(
                stat[:, 50:66], stat[:, 50:66], 1.0, iota16[:],
                ALU.mult, ALU.mult, accum_out=idxc[:, col:col + 1]))
            emit("vector", lambda v: v.drain(), inc="v_gt")
            VGT = val("v_gt")
            emit("scalar", lambda sc: sc.activation(
                stat[:, 50:66], stat[:, 32:48], AF.Exp, bias=stat[:, 49:50],
                accum_out=stat[:, 66:67]), waits=[("v_gt", VGT)], inc="s_gt")
            SGT = val("s_gt")
            emit("vector", lambda v, col=col: v.reciprocal(
                gates[:, col:col + 1], stat[:, 66:67]), waits=[("s_gt", SGT)])
            emit("vector", lambda v: v.drain(), inc="v_gt")
    GATES_V = val("v_gt")
    XR_D = val("d_xr")

    CC_XR = ("g_xr", val("g_xr"))

    if phases < 3:
        S_.barrier(exclude=())
        return finish_early()

    S_.barrier(exclude=("g_cc",))
    wait("gpsimd", [("g_cc", CC_CARRY)])
    for j in range(nCols):
        pb = ps[6 + j % 2][0:1, 0:128]
        emit("tensor", lambda tt, j=j, pb=pb: tt.transpose(
            pb, idxc[:, j:j + 1], id128[:]),
            waits=([("v_gt", GATES_V)] if j == 0 else []) +
                  ([("v_ir", val("v_ir") - 1)] if j >= 2 else []),
            inc="t_ir")
        emit("vector", lambda v, j=j, pb=pb: v.tensor_copy(
            IDXROW[:, j * 128:(j + 1) * 128], pb),
            waits=[("t_ir", val("t_ir"))], inc="v_ir")
    IR_V = val("v_ir")
    dma("gpsimd", idxrow_d, IDXROW[0:1, :], "g_d", [("v_ir", IR_V)])
    IRD = val("g_d")
    dma("gpsimd", A(B1_O, T, 0, 16), idxrow_d.unsqueeze(0).partition_broadcast(16),
        "g_d", [("g_d", IRD)])
    B16 = val("g_d")
    emit("vector", lambda v: v.memset(ONES16[:], 1.0))
    emit("vector", lambda v: v.drain(), inc="v_rt")
    emit("vector", lambda v: v.tensor_scalar(
        ROW_OH[:], ROW_OH[:], smallc[:, 0:1], None, ALU.is_equal),
        waits=[("g_d", B16), ("c_ld", C_LD), ("v_rt", val("v_rt"))])
    emit("vector", lambda v: v.drain())
    emit("vector", lambda v: v.tensor_tensor_scan(
        ROW_SC[:], ONES16[:], ROW_OH[:], 0.0, ALU.mult, ALU.add))
    emit("vector", lambda v: v.drain())
    emit("vector", lambda v: v.tensor_copy(stat[0:16, 70:71], ROW_SC[:, T - 1:T]))
    emit("vector", lambda v: v.drain(), inc="v_rt")
    RT1 = val("v_rt")
    dma("gpsimd", cnts_in, stat[0:16, 70:71], "g_d", [("v_rt", RT1)])
    emit("gpsimd", lambda g: g.collective_compute(
        "AllGather", ALU.bypass, replica_groups=[list(range(cfg["n_cores"]))],
        ins=[cnts_in.opt()], outs=[cnts_all.opt()]),
        waits=[("g_d", val("g_d"))], inc="g_cc")
    CC_CNT = val("g_cc")
    dma("gpsimd", c8[:], cnts_all, "g_d", [("g_cc", CC_CNT)])
    C8D = val("g_d")
    emit("tensor", lambda tt: tt.matmul(ps[7][0:1, 0:E], pm8[:], c8[:],
                                        start=True, stop=True),
         waits=[("g_d", C8D), ("c_ld", C_LD), ("v_rt", RT1)], inc="t_rt")
    emit("vector", lambda v: v.tensor_copy(offrow[:], ps[7][0:1, 0:E]),
         waits=[("t_rt", val("t_rt"))])
    emit("vector", lambda v: v.drain(), inc="v_rt")
    emit("tensor", lambda tt: tt.transpose(ps[7][0:16, 0:1], offrow[:], id128[0:1, 0:1]),
         waits=[("v_rt", val("v_rt"))], inc="t_rt")
    emit("vector", lambda v: v.tensor_copy(smallc[:, 3:4], ps[7][0:16, 0:1]),
         waits=[("t_rt", val("t_rt"))])
    emit("vector", lambda v: v.drain(), inc="v_rt")
    OFF_V = val("v_rt")

    emit("vector", lambda v: v.tensor_scalar(
        stat[0:16, 68:69], smallc[:, 3:4], -1.0, float(CAP + 1), ALU.mult, ALU.add),
        waits=[("v_rt", OFF_V)])
    emit("vector", lambda v: v.drain())
    emit("vector", lambda v: v.tensor_scalar(
        ROW_A[:], ROW_SC[:], stat[0:16, 68:69], None, ALU.is_lt))
    emit("vector", lambda v: v.drain())
    emit("vector", lambda v: v.tensor_tensor(ROW_A[:], ROW_A[:], ROW_OH[:], ALU.mult))
    emit("vector", lambda v: v.drain(), inc="v_rt")
    emit("vector", lambda v: v.tensor_tensor(
        stat[0:16, 69:70], smallc[:, 1:2], smallc[:, 3:4], ALU.add))
    emit("vector", lambda v: v.drain())
    emit("vector", lambda v: v.tensor_scalar(
        stat[0:16, 67:68], smallc[:, 3:4], 1.0, -1.0, ALU.mult, ALU.add))
    emit("vector", lambda v: v.drain())
    emit("vector", lambda v: v.tensor_scalar(
        ONES16[:], ROW_SC[:], stat[0:16, 69:70], None, ALU.add))
    emit("vector", lambda v: v.drain())
    emit("vector", lambda v: v.tensor_tensor(ONES16[:], ONES16[:], ROW_A[:], ALU.mult))
    emit("vector", lambda v: v.drain(), inc="v_rt")
    emit("vector", lambda v: v.tensor_scalar(
        ROW_SC[:], ROW_SC[:], stat[0:16, 67:68], float(CAP - 1), ALU.add, ALU.min))
    emit("vector", lambda v: v.drain())
    emit("vector", lambda v: v.tensor_scalar(
        ROW_SC[:], ROW_SC[:], smallc[:, 2:3], None, ALU.add))
    emit("vector", lambda v: v.drain())
    emit("vector", lambda v: v.tensor_tensor(ROW_SC[:], ROW_SC[:], ROW_OH[:], ALU.mult))
    emit("vector", lambda v: v.drain(), inc="v_rt")
    ROWS_V = val("v_rt")
    o16 = sbuf("o16", [16, 1])
    emit("vector", lambda v: v.memset(o16[:], 1.0))
    emit("vector", lambda v: v.drain(), inc="v_rt")
    O16V = val("v_rt")
    nQ = (T + 511) // 512
    for q in range(nQ):
        w = min(512, T - q * 512)
        emit("tensor", lambda tt, q=q, w=w: tt.matmul(
            ps[6][0:1, 0:w], o16[:], ONES16[:, q * 512:q * 512 + w],
            start=True, stop=True),
            waits=[("v_rt", ROWS_V), ("v_rt", O16V)] +
                  ([("v_rq", val("v_rq"))] if q else []),
            inc="t_rq")
        emit("tensor", lambda tt, q=q, w=w: tt.matmul(
            ps[5][0:1, 0:w], o16[:], ROW_A[:, q * 512:q * 512 + w],
            start=True, stop=True), inc="t_rq")
        emit("tensor", lambda tt, q=q, w=w: tt.matmul(
            ps[4][0:1, 0:w], o16[:], ROW_SC[:, q * 512:q * 512 + w],
            start=True, stop=True), inc="t_rq")
        emit("vector", lambda v, q=q, w=w: v.tensor_copy(
            R1[:, q * 512:q * 512 + w], ps[6][0:1, 0:w]),
            waits=[("t_rq", val("t_rq"))])
        emit("vector", lambda v, q=q, w=w: v.tensor_copy(
            R2[:, q * 512:q * 512 + w], ps[5][0:1, 0:w]))
        emit("vector", lambda v, q=q, w=w: v.tensor_copy(
            IDXROW[:, q * 512:q * 512 + w], ps[4][0:1, 0:w]))
        emit("vector", lambda v: v.drain(), inc="v_rq")
    CS_V = val("v_rq")
    emit("vector", lambda v: v.tensor_scalar(
        R2[:], R2[:], -BIG, BIG, ALU.mult, ALU.add), waits=[("v_rq", CS_V)])
    emit("vector", lambda v: v.drain())
    emit("vector", lambda v: v.tensor_tensor(R1[:], R1[:], R2[:], ALU.add))
    emit("vector", lambda v: v.drain(), inc="v_rq")
    DF_V = val("v_rq")
    dma("gpsimd", rowb[0, 0:T], R1[0:1, :], "g_d", [("v_rq", DF_V)])
    dma("gpsimd", rowb[1, 0:T], IDXROW[0:1, :], "g_d", [("v_rq", DF_V)])
    dma("gpsimd", rowb[2, 0:T], R2[0:1, :], "g_d", [("v_rq", DF_V)])
    RB_D = val("g_d")
    dma("gpsimd", dcol[:], rowb[0, 0:T].rearrange("(c p) -> p c", p=128),
        "g_d", [("g_d", RB_D)], slow=True)
    dma("gpsimd", rcol[:], rowb[1, 0:T].rearrange("(c p) -> p c", p=128), "g_d", slow=True)
    dma("gpsimd", mcol[:], rowb[2, 0:T].rearrange("(c p) -> p c", p=128), "g_d", slow=True)
    COLS_D = val("g_d")
    emit("vector", lambda v: v.tensor_copy(dcoli[:], dcol[:]),
         waits=[("g_d", COLS_D)])
    emit("vector", lambda v: v.tensor_copy(rcoli[:], rcol[:]))
    emit("vector", lambda v: v.drain())
    emit("vector", lambda v: v.tensor_scalar(mcol[:], mcol[:], 0.0, None, ALU.is_equal))
    emit("vector", lambda v: v.drain())
    emit("vector", lambda v: v.tensor_tensor(mcol[:], mcol[:], gates[:], ALU.mult))
    emit("vector", lambda v: v.drain(), inc="v_cl")
    CL_V = val("v_cl")

    for j in range(nCols):
        emit("gpsimd", lambda g, j=j: g.indirect_dma_start(
            out=smap_in.unsqueeze(0),
            out_offset=IndirectOffsetOnAxis(ap=dcoli[:, j:j + 1], axis=1),
            in_=tok1[:, j:j + 1], in_offset=None,
            bounds_check=S - 1, oob_is_err=False),
            waits=([("v_cl", CL_V), ("g_z", Z_S), ("c_ld", C_LD)] if j == 0
                   else [("g_sc", val("g_sc"))]),
            inc="g_sc", amt=16)
    SCAT = val("g_sc")
    emit("gpsimd", lambda g: g.collective_compute(
        "ReduceScatter", ALU.add, replica_groups=[list(range(cfg["n_cores"]))],
        ins=[smap_in.opt()], outs=[smap_out.opt()]),
        waits=[("g_sc", SCAT)], inc="g_cc")
    CC_RS = val("g_cc")
    dma("gpsimd", gmapf[:], smap_out.rearrange("(c p) -> p c", p=128),
        "g_d", [("g_cc", CC_RS)], slow=True)
    GM_D = val("g_d")
    emit("vector", lambda v: v.tensor_scalar(
        gmapf[:], gmapf[:], -1.0, 0.0, ALU.add, ALU.max), waits=[("g_d", GM_D)])
    emit("vector", lambda v: v.drain())
    emit("vector", lambda v: v.tensor_copy(gmapi[:], gmapf[:]))
    emit("vector", lambda v: v.drain(), inc="v_gm")
    GM_V = val("v_gm")

    if phases < 4:
        S_.barrier(exclude=())
        return finish_early()

    S_.barrier(exclude=("g_cc", "g_xr"))
    wait("gpsimd", [CC_XR])
    BF16 = mybir.dt.bfloat16
    HSh = A(B1_O, nF * S_TILE // 2).bitcast(BF16)
    HSl = A(B1_O + nF * S_TILE // 2, nF * S_TILE // 2).bitcast(BF16)
    XFh = A(B5_O, nD * S_TILE // 2).bitcast(BF16)
    XFl = A(B5_O + nD * S_TILE // 2, nD * S_TILE // 2).bitcast(BF16)
    W2land = [A(WQ_O, 1024), A(WQ_O + 1024, 1024)]
    W2p = [(A(WQ_O + 2048, 512).bitcast(BF16), A(WQ_O + 2560, 512).bitcast(BF16)),
           (A(WQ_O + 3072, 512).bitcast(BF16), A(WQ_O + 3584, 512).bitcast(BF16))]
    W1land = [A(WQ_O + 4096, 1024), A(WQ_O + 5120, 1024)]
    W1p = [(A(WQ_O + 6144, 512).bitcast(BF16), A(WQ_O + 6656, 512).bitcast(BF16)),
           (A(WQ_O + 7168, 512).bitcast(BF16), A(WQ_O + 7680, 512).bitcast(BF16))]
    XG2 = A(M2_O, D)
    STRIP = A(M2_O + D, 512)
    PIECE = [A(WS_O, 128), A(WS_O + 128, 128)]
    eo_marks = []
    for ex in range(cfg["EXP_PC"]):
        for sti in range(nST):
            for b in range(S_TILE // 128):
                jj = ex * (CAP // 128) + sti * (S_TILE // 128) + b
                emit("gpsimd", lambda g, jj=jj: g.indirect_dma_start(
                    out=XG2[:], out_offset=None,
                    in_=xr_all,
                    in_offset=IndirectOffsetOnAxis(ap=gmapi[:, jj:jj + 1], axis=0)),
                    waits=[("v_gm", GM_V), CC_XR,
                           ("txf", val("txf")), ("t_h", val("t_h")),
                           ("g_ig", val("g_ig"))],
                    inc="g_ig", amt=16)
                IG = val("g_ig")
                for k in range(nD):
                    i = b * nD + k
                    pb = ps[4 + i % 2][:, 0:128]
                    emit("tensor", lambda tt, k=k, i=i, pb=pb: tt.transpose(
                        pb, XG2[:, k * 128:(k + 1) * 128], id128[:]),
                        waits=[("g_ig", IG)] +
                              ([("vxf", val("vxf") - 1), ("sxf", val("sxf") - 1)]
                               if val("vxf") >= 2 else []),
                        inc="txf")
                    co = k * S_TILE + b * 128
                    emit("scalar", lambda sc, co=co, pb=pb: sc.activation(
                        XFh[:, co:co + 128], pb, AF.Copy),
                        waits=[("txf", val("txf"))], inc="sxf")
                    emit("vector", lambda v, co=co, pb=pb: v.tensor_tensor(
                        XFl[:, co:co + 128], pb, XFh[:, co:co + 128], ALU.subtract),
                        waits=[("txf", val("txf")), ("sxf", val("sxf"))], inc="vxf")
            XF_V, XF_S = val("vxf"), val("sxf")

            w1split = {}

            def w1_stage(fc):
                half = fc % 2
                wld = dma("sync", W1land[half][:].rearrange("p (k f) -> p k f", k=nD),
                          w1b[ex, fc].rearrange("k p f -> p k f"),
                          f"d_w1{half}",
                          [("s_w1", max(0, val("s_w1") - 1)),
                           ("v_w1", max(0, val("v_w1") - 1))] if val("v_w1") > 1 else [])
                emit("scalar", lambda sc, half=half: sc.copy(
                    W1p[half][0][:], W1land[half][:]),
                    waits=[(f"d_w1{half}", wld)] +
                          ([("t_h", max(0, val("t_h") - 1))] if val("t_h") > 1 else []),
                    inc="s_w1")
                emit("vector", lambda v, half=half: v.tensor_tensor(
                    W1p[half][1][:], W1land[half][:], W1p[half][0][:], ALU.subtract),
                    waits=[("s_w1", val("s_w1"))] +
                          ([("t_h", max(0, val("t_h") - 1))] if val("t_h") > 1 else []),
                    inc="v_w1")
                w1split[fc] = val("v_w1")

            w1_stage(0)
            for fc in range(nF):
                half = fc % 2
                bank = fc % 2
                mm3 = [(W1p[half][0], XFh), (W1p[half][0], XFl), (W1p[half][1], XFh)]
                nmm = 3 * nD
                j = 0
                for (wp, xp) in mm3:
                    for k in range(nD):
                        emit("tensor", lambda tt, wp=wp, xp=xp, k=k, j=j, bank=bank: tt.matmul(
                            ps[bank][:, 0:S_TILE],
                            wp[:, k * 128:(k + 1) * 128],
                            xp[:, k * S_TILE:(k + 1) * S_TILE],
                            start=(j == 0), stop=(j == nmm - 1)),
                            waits=([("vxf", XF_V), ("sxf", XF_S), ("v_w1", w1split[fc])] +
                                   ([("s_h", val("s_h") - 1), ("v_h", val("v_h") - 1)]
                                    if val("s_h") >= 2 else []))
                                  if j == 0 else [],
                            inc=("t_h" if j == nmm - 1 else None))
                        j += 1
                if fc + 1 < nF:
                    w1_stage(fc + 1)
                ho = fc * S_TILE
                emit("scalar", lambda sc, ho=ho, bank=bank: sc.activation(
                    HSh[:, ho:ho + S_TILE], ps[bank][:, 0:S_TILE], AF.Relu),
                    waits=[("t_h", val("t_h"))] +
                          ([("t_o2g", val("t_o2g"))] if fc == 0 else []), inc="s_h")
                emit("vector", lambda v, ho=ho, bank=bank: v.scalar_tensor_tensor(
                    HSl[:, ho:ho + S_TILE], ps[bank][:, 0:S_TILE], 0.0,
                    HSh[:, ho:ho + S_TILE], ALU.max, ALU.subtract),
                    waits=[("t_h", val("t_h")), ("s_h", val("s_h"))], inc="v_h")
            H_S, H_V = val("s_h"), val("v_h")

            nFG = 8
            nG = nF // nFG
            w2split = {}

            def w2_stage(ex, dc, g):
                gh = g % 2
                w2ld = dma("sync",
                           W2land[gh][:].rearrange("p (f d) -> p f d", f=nFG),
                           w2b[ex, dc, g * nFG:(g + 1) * nFG].rearrange("f p d -> p f d"),
                           f"d_w2{gh}",
                           [("s_w2", max(0, val("s_w2") - 1)),
                            ("v_w2", max(0, val("v_w2") - 1))] if val("v_w2") > 1 else [])
                emit("scalar", lambda sc, gh=gh: sc.copy(
                    W2p[gh][0][:], W2land[gh][:]),
                    waits=[(f"d_w2{gh}", w2ld)] +
                          ([("t_o2g", max(0, val("t_o2g") - 1))] if val("t_o2g") > 1 else []),
                    inc="s_w2")
                emit("vector", lambda v, gh=gh: v.tensor_tensor(
                    W2p[gh][1][:], W2land[gh][:], W2p[gh][0][:], ALU.subtract),
                    waits=[("s_w2", val("s_w2"))] +
                          ([("t_o2g", max(0, val("t_o2g") - 1))] if val("t_o2g") > 1 else []),
                    inc="v_w2")
                w2split[(dc, g)] = val("v_w2")

            for dc in range(nD):
                bank = 2 + dc % 2
                j2 = 0
                nmm2 = 3 * nF
                if dc == 0:
                    w2_stage(ex, 0, 0)
                for g in range(nG):
                    gh = g % 2
                    for (sel, xsel) in ((0, "h"), (0, "l"), (1, "h")):
                        for fi in range(nFG):
                            fc = g * nFG + fi
                            hs = HSh if xsel == "h" else HSl
                            emit("tensor", lambda tt, sel=sel, hs=hs, fi=fi, fc=fc,
                                 gh=gh, j2=j2, bank=bank: tt.matmul(
                                ps[bank][:, 0:S_TILE],
                                W2p[gh][sel][:, fi * 128:(fi + 1) * 128],
                                hs[:, fc * S_TILE:(fc + 1) * S_TILE],
                                start=(j2 == 0), stop=(j2 == nmm2 - 1)),
                                waits=(([("s_h", H_S), ("v_h", H_V),
                                         ("v_w2", w2split[(dc, g)])] +
                                        ([("v_o2", val("v_o2") - 1)] if val("v_o2") >= 2 else []))
                                       if j2 == 0 else
                                       ([("v_w2", w2split[(dc, g)])]
                                        if fi == 0 and sel == 0 and xsel == "h" else [])),
                                inc=("t_o2g" if (fi == nFG - 1 and sel == 1) else None))
                            j2 += 1
                    if g + 1 < nG:
                        w2_stage(ex, dc, g + 1)
                    elif dc + 1 < nD:
                        w2_stage(ex, dc + 1, 0)
                emit("vector", lambda v, bank=bank: v.tensor_copy(
                    STRIP[:], ps[bank][:, 0:S_TILE]),
                    waits=[("t_o2g", val("t_o2g"))], inc="v_o2")
                O2V = val("v_o2")
                for b in range(S_TILE // 128):
                    pb = ps[6 + b % 2][:, 0:128]
                    emit("tensor", lambda tt, b=b, pb=pb: tt.transpose(
                        pb, STRIP[:, b * 128:(b + 1) * 128], id128[:]),
                        waits=[("v_o2", O2V)] +
                              ([("veo", val("veo") - 1)] if val("veo") >= 2 else []),
                        inc="teo")
                    par = b % 2
                    emit("vector", lambda v, b=b, pb=pb, par=par: v.tensor_copy(
                        PIECE[par][:], pb),
                        waits=[("teo", val("teo")),
                               (f"d_eo{par}", val(f"d_eo{par}"))], inc="veo")
                    base = ex * CAP + sti * S_TILE + b * 128
                    dma("scalar", eo_in[base:base + 128, dc * 128:(dc + 1) * 128],
                        PIECE[par][:], f"d_eo{par}", [("veo", val("veo"))])
        eo_marks.append((ex, val("d_eo0"), val("d_eo1")))
    for (ex, v0, v1) in eo_marks:
        emit("gpsimd", lambda g, ex=ex: g.collective_compute(
            "AllGather", ALU.bypass, replica_groups=[list(range(cfg["n_cores"]))],
            ins=[eo_in[ex * CAP:(ex + 1) * CAP, :].opt()],
            outs=[eo_all[ex * (S // 2):(ex + 1) * (S // 2), :].opt()]),
            waits=[("d_eo0", v0), ("d_eo1", v1)], inc="g_cc")
    CC_EO = val("g_cc")
    S_.barrier(exclude=("g_cc",))
    GAp = [A(B1_O, D), A(B1_O + D, D)]
    OTp = [A(B2_O, D), A(B2_O + D, D)]
    for j in range(nCols):
        par = j % 2
        emit("gpsimd", lambda g, j=j, par=par: g.indirect_dma_start(
            out=GAp[par][:], out_offset=None,
            in_=eo_all,
            in_offset=IndirectOffsetOnAxis(ap=rcoli[:, j:j + 1], axis=0)),
            waits=([("g_cc", CC_EO)] if j == 0 else []) +
                  [("v_fin", max(0, val("v_fin") - 1)),
                   (f"g_ig{par}", val(f"g_ig{par}"))],
            inc=f"g_ig{par}", amt=16)
        IGE = val(f"g_ig{par}")
        emit("vector", lambda v, j=j, par=par: v.scalar_tensor_tensor(
            OTp[par][:], GAp[par][:], mcol[:, j:j + 1], XM[:, j * D:(j + 1) * D],
            ALU.mult, ALU.add),
            waits=[(f"g_ig{par}", IGE), (f"d_out{par}", val(f"d_out{par}"))])
        emit("vector", lambda v: v.drain(), inc="v_fin")
        dma("scalar", out_hbm[j * 128:(j + 1) * 128, :], OTp[par][:],
            f"d_out{par}", [("v_fin", val("v_fin"))])
    wait("gpsimd", [("d_out0", val("d_out0")), ("d_out1", val("d_out1"))])
    wait("sync", [("d_out0", val("d_out0")), ("d_out1", val("d_out1"))])

    block = es.enter_context(nc.Block())
    S_.build(block)
    es.close()
    return nc


def host_prep(cfg, L0_attn_x, L0_attn_w_qkv, L0_attn_w_swiglu, L0_attn_w_out,
              L0_ffn_Wg, L0_ffn_w1, L0_ffn_w2):
    n, T, D, E, F = cfg["n_cores"], cfg["T"], cfg["D"], cfg["E"], cfg["F"]
    nD, nF, CAP, nCols = cfg["nD"], cfg["nF"], cfg["CAP"], cfg["nCols"]
    QK = 2 * D
    nSW = (2 * D) // 128
    x = np.asarray(L0_attn_x, dtype=np.float32).reshape(n * T, D)
    wqkv = np.asarray(L0_attn_w_qkv, dtype=np.float32)
    wswg = np.asarray(L0_attn_w_swiglu, dtype=np.float32)
    wout = np.asarray(L0_attn_w_out, dtype=np.float32)
    wgm = np.asarray(L0_ffn_Wg, dtype=np.float32)
    w1 = np.asarray(L0_ffn_w1, dtype=np.float32)
    w2 = np.asarray(L0_ffn_w2, dtype=np.float32)

    wqk_b = np.ascontiguousarray(wqkv[:QK].T.reshape(nD, 128, QK))
    wv_b = np.ascontiguousarray(
        wqkv[QK:].T.reshape(nD, 128, nD, 128).transpose(2, 0, 1, 3))
    wsw_b = np.ascontiguousarray(
        wswg.T.reshape(nD, 128, nSW, 128).transpose(2, 0, 1, 3))
    wout_b = np.ascontiguousarray(wout.T.reshape(nD, 128, D))
    wg_b = np.ascontiguousarray(wgm.reshape(nD, 128, E))
    ident = np.eye(128, dtype=np.float32)
    iota16r = np.tile(np.arange(E, dtype=np.float32), (128, 1))
    iotaEc = np.arange(E, dtype=np.float32).reshape(E, 1)
    ebias = (np.arange(E, dtype=np.float32) * CAP - 1.0).reshape(E, 1)
    eids = np.arange(E, dtype=np.float32)
    rbias = ((eids % 2) * (n * T / 2) + (eids // 2) * CAP).reshape(E, 1).astype(np.float32)

    in_maps = []
    epc = cfg["EXP_PC"]
    for c in range(n):
        w1c = w1[epc * c:epc * (c + 1)]
        w2c = w2[epc * c:epc * (c + 1)]
        w1_b = np.ascontiguousarray(
            w1c.reshape(epc, nD, 128, nF, 128).transpose(0, 3, 1, 2, 4))
        w2_b = np.ascontiguousarray(
            w2c.reshape(epc, nF, 128, nD, 128).transpose(0, 3, 1, 2, 4))
        TILE_ = cfg["TILE"]
        cols = np.arange(nCols)
        pp = np.arange(128)
        tt_ = cols // (TILE_ // 128)
        off = (cols % (TILE_ // 128))[None, :] * 128 + pp[:, None]
        tokid1 = (tt_[None, :] * (n * TILE_) + c * TILE_ + off + 1.0).astype(np.float32)
        pm = np.zeros((8, 1), dtype=np.float32)
        pm[:c] = 1.0
        in_maps.append({
            "x": np.ascontiguousarray(x[c * T:(c + 1) * T]),
            "wqk": wqk_b, "wv": wv_b, "wsw": wsw_b, "wout": wout_b, "wg": wg_b,
            "w1b": w1_b, "w2b": w2_b,
            "ident": ident, "iota16r": iota16r, "iotaEc": iotaEc,
            "ebias": ebias, "rbias": rbias,
            "tokid1": np.ascontiguousarray(tokid1),
            "prevmask": pm,
        })
    return in_maps


def kernel(L0_attn_x, L0_attn_w_qkv, L0_attn_w_swiglu, L0_attn_w_out,
           L0_ffn_Wg, L0_ffn_w1, L0_ffn_w2):
    cfg = _cfg()
    in_maps = host_prep(cfg, L0_attn_x, L0_attn_w_qkv, L0_attn_w_swiglu,
                        L0_attn_w_out, L0_ffn_Wg, L0_ffn_w1, L0_ffn_w2)
    nc = build_nc(cfg)
    res = run_bass_kernel_spmd(nc, in_maps, core_ids=list(range(cfg["n_cores"])))
    out = np.concatenate([np.asarray(r["out"]) for r in res.results], axis=0)
    return out.reshape(1, cfg["S"], cfg["D"]).astype(np.float32)


# version 21 (final)
# speedup vs baseline: 1.0155x; 1.0030x over previous
import os
import numpy as np
from contextlib import ExitStack

import concourse.bass as bass
import concourse.mybir as mybir
from concourse.bass import IndirectOffsetOnAxis
from concourse.bass_utils import run_bass_kernel_spmd

F32 = mybir.dt.float32
I32 = mybir.dt.int32
AF = mybir.ActivationFunctionType
ALU = mybir.AluOpType
AXX = mybir.AxisListType.X

FP32_EPS = 1.1920929e-07
LN_EPS = 1e-5
ENGINES = ("tensor", "vector", "scalar", "gpsimd", "sync")


def _cfg(n_cores=8, T=2048, D=1024, E=16, F=4096, TILE=512, S_TILE=512):
    import os
    c = dict(n_cores=n_cores, T=T, D=D, E=E, F=F, TILE=TILE, S_TILE=S_TILE)
    c["S"] = n_cores * T
    c["CAP"] = c["S"] // E
    c["nD"] = D // 128
    c["nSub"] = TILE // 128
    c["nTiles"] = int(os.environ.get("K_NTILES", T // TILE))
    c["nCols"] = T // 128
    c["EXP_PC"] = E // n_cores
    c["SLOTS"] = c["EXP_PC"] * c["CAP"]
    c["nF"] = F // 128
    c["nST"] = c["CAP"] // S_TILE
    return c


class Sched:
    def __init__(self, nc, es):
        self.nc, self.es = nc, es
        self.ops = {e: [] for e in ENGINES}
        self.sems, self.cnt = {}, {}

    def sem(self, name):
        if name not in self.sems:
            self.sems[name] = self.es.enter_context(self.nc.semaphore(name))
            self.cnt[name] = 0
        return self.sems[name]

    def val(self, name):
        return self.cnt.get(name, 0)

    def emit(self, eng, fn, waits=(), inc=None, amt=1):
        w = [(self.sem(s), v) for (s, v) in waits if v > 0]
        if inc is not None:
            self.sem(inc)
            self.cnt[inc] += amt
        self.ops[eng].append((fn, w, (self.sems[inc], amt) if inc else None))

    def wait(self, eng, waits):
        self.emit(eng, None, waits=waits)

    def barrier(self, engines=ENGINES, exclude=()):
        snap = [(s, v) for s, v in self.cnt.items() if v > 0 and s not in exclude]
        for e in engines:
            self.emit(e, None, waits=snap)

    def build(self, block):
        for ename in ENGINES:
            stream = self.ops[ename]
            if not stream:
                continue

            def mk(stream):
                def body(eng):
                    for fn, waits, inc in stream:
                        for sem, v in waits:
                            eng.wait_ge(sem, v)
                        inst = fn(eng) if fn is not None else None
                        if inc is not None:
                            assert inst is not None
                            inst.then_inc(inc[0], inc[1])
                return body

            getattr(block, ename)(mk(stream))


def build_nc(cfg, dbg=False, phases=5):
    nD, TILE, nSub, nTiles = cfg["nD"], cfg["TILE"], cfg["nSub"], cfg["nTiles"]
    D, E, F, T = cfg["D"], cfg["E"], cfg["F"], cfg["T"]
    CAP, S, SLOTS, nCols, nF = cfg["CAP"], cfg["S"], cfg["SLOTS"], cfg["nCols"], cfg["nF"]
    S_TILE, nST = cfg["S_TILE"], cfg["nST"]
    QK = 2 * D
    nQKe = max(1, QK // 512)
    nSW = (2 * D) // 128
    nDO = max(1, D // 512)
    BIG = float(1 << 22)

    assert TILE == 512, "FM chunk width is hardwired to 512"
    nc = bass.Bass("TRN2", target_bir_lowering=False, dynamic_dma_scratch_size=26624)

    def dram_in(name, shape, dt=F32):
        return nc.dram_tensor(name, shape, dt, kind="ExternalInput").ap()

    x_hbm = dram_in("x", [T, D])
    wqk = dram_in("wqk", [nD, 128, QK])
    wv = dram_in("wv", [nD, nD, 128, 128])
    wsw = dram_in("wsw", [nSW, nD, 128, 128])
    wout = dram_in("wout", [nD, 128, D])
    wg = dram_in("wg", [nD, 128, E])
    w1b = dram_in("w1b", [cfg["EXP_PC"], nF, nD, 128, 128])
    w2b = dram_in("w2b", [cfg["EXP_PC"], nD, nF, 128, 128])
    ident = dram_in("ident", [128, 128])
    iota16r = dram_in("iota16r", [128, E])
    iotaEc = dram_in("iotaEc", [E, 1])
    ebias = dram_in("ebias", [E, 1])
    rbias = dram_in("rbias", [E, 1])
    tokid1 = dram_in("tokid1", [128, nCols])
    prevmask = dram_in("prevmask", [8, 1])

    def dram(name, shape, dt=F32):
        return nc.dram_tensor(name, shape, dt).ap()

    BF16D = mybir.dt.bfloat16
    wqk_h = dram("wqk_h", [nD, 128, QK], BF16D)
    wqk_l = dram("wqk_l", [nD, 128, QK], BF16D)
    wv_h = dram("wv_h", [nD, nD, 128, 128], BF16D)
    wv_l = dram("wv_l", [nD, nD, 128, 128], BF16D)
    wsw_h = dram("wsw_h", [nSW, nD, 128, 128], BF16D)
    wsw_l = dram("wsw_l", [nSW, nD, 128, 128], BF16D)
    wout_h = dram("wout_h", [nD, 128, D], BF16D)
    wout_l = dram("wout_l", [nD, 128, D], BF16D)
    sp_wcs = dram("sp_wcs", [D, T])
    sp_kvcs = dram("sp_kvcs", [D, T])
    sp_sigq = dram("sp_sigq", [D, T])
    sp_ms = dram("sp_ms", [128, nTiles * 2 * nSub])
    carry_in = dram("carry_in", [128, 16])
    carry_all = nc.dram_tensor("carry_all", [8 * 128, 16], F32, addr_space="Shared").ap()
    idxrow_d = dram("idxrow_d", [T])
    cnts_in = dram("cnts_in", [E])
    cnts_all = nc.dram_tensor("cnts_all", [8, E], F32, addr_space="Shared").ap()
    rowb = dram("rowb", [4, max(T, 2048)])
    xr_in = dram("xr_in", [T, D])
    xr_all = nc.dram_tensor("xr_all", [S, D], F32, addr_space="Shared").ap()
    smap_in = dram("smap_in", [S])
    smap_out = dram("smap_out", [SLOTS])
    eo_in = dram("eo_in", [SLOTS, D])
    eo_all = nc.dram_tensor("eo_all", [S, D], F32, addr_space="Shared").ap()
    out_hbm = nc.dram_tensor("out", [T, D], F32, kind="ExternalOutput").ap()
    dbg_hbm = nc.dram_tensor("dbgo", [16, max(T, D)], F32, kind="ExternalOutput").ap() if dbg else None

    es = ExitStack()

    AW = 16384 + 5 * (nSub * D) + 4 * nD * 512 + 2 * nD * 128 + 2048
    XM_O = 0
    B1_O = 16384
    B2_O = B1_O + nSub * D
    B3_O = B2_O + nSub * D
    B4_O = B3_O + nSub * D
    B5_O = B4_O + nSub * D
    WQ_O = B5_O + nSub * D
    WQ_W = 4 * nD * 512
    WS_O = WQ_O + 2 * nD * 512
    WS_W = 2 * nD * 128
    M2_O = WS_O + WS_W
    AW = M2_O + max(2048, 2 * D)
    AR = es.enter_context(nc.sbuf_tensor("AR", [128, AW], F32))

    def A(off, w, p0=0, p1=128):
        return AR[p0:p1, off:off + w]

    XM = A(XM_O, 16384)
    B1 = A(B1_O, nSub * D)
    B2 = A(B2_O, nSub * D)
    B3 = A(B3_O, nSub * D)
    B4 = A(B4_O, nSub * D)
    B5 = A(B5_O, nSub * D)
    WQh = [A(WQ_O, nD * 512), A(WQ_O + nD * 512, nD * 512)]
    WSh = [A(WS_O, nD * 128), A(WS_O + nD * 128, nD * 128)]
    WQp = [(A(WQ_O, nD * 256).bitcast(mybir.dt.bfloat16),
            A(WQ_O + nD * 256, nD * 256).bitcast(mybir.dt.bfloat16)),
           (A(WQ_O + nD * 512, nD * 256).bitcast(mybir.dt.bfloat16),
            A(WQ_O + nD * 512 + nD * 256, nD * 256).bitcast(mybir.dt.bfloat16))]
    WSp = [(A(WS_O, nD * 64).bitcast(mybir.dt.bfloat16),
            A(WS_O + nD * 64, nD * 64).bitcast(mybir.dt.bfloat16)),
           (A(WS_O + nD * 128, nD * 64).bitcast(mybir.dt.bfloat16),
            A(WS_O + nD * 128 + nD * 64, nD * 64).bitcast(mybir.dt.bfloat16))]
    XNh = A(B3_O, nSub * D // 2).bitcast(mybir.dt.bfloat16)
    XNl = A(B3_O + nSub * D // 2, nSub * D // 2).bitcast(mybir.dt.bfloat16)
    Yh = A(B2_O, nSub * D // 2).bitcast(mybir.dt.bfloat16)
    Yl = A(B2_O + nSub * D // 2, nSub * D // 2).bitcast(mybir.dt.bfloat16)
    SQA = A(XM_O, nSub * D)
    ROW_OH = A(B1_O, T, 0, 16)
    ROW_SC = A(B2_O, T, 0, 16)
    ROW_A = A(B3_O, T, 0, 16)
    ONES16 = A(B4_O, T, 0, 16)
    R1 = A(B5_O, T, 0, 1)
    R2 = A(B5_O + T, T, 0, 1)
    IDXROW = A(B5_O + 2 * T if 3 * T <= nSub * D else WQ_O, T, 0, 1)
    CGROW = A(M2_O, 2048, 0, 1)
    HS = A(B1_O, nF * S_TILE)
    XF = A(B5_O, nD * S_TILE)
    XG = A(M2_O, D)
    EOT = A(M2_O + D, D)
    W2HW = (nF // 2) * 128
    W2h = [A(WQ_O, W2HW), A(WQ_O + W2HW, W2HW)]
    OUT2D = A(WQ_O + 2 * W2HW, nD * S_TILE)
    GA = A(B1_O, D)
    OT = A(B2_O, D)

    assert nF * S_TILE <= 4 * nSub * D, "HS overflows B1..B4 region"
    assert nD * S_TILE <= nSub * D, "XF overflows B5"
    assert 2 * D <= max(2048, 2 * D), "XG/EOT overflow M2"
    assert 2 * W2HW + nD * S_TILE <= 2 * 2 * nD * 512, "W2+OUT2 overflow WQ region"

    def sbuf(name, shape, dt=F32):
        return es.enter_context(nc.sbuf_tensor(name, shape, dt))

    stat = sbuf("stat", [128, 96])
    carry = sbuf("carry", [128, 16])
    carryg = sbuf("carryg", [128, 24])
    gates = sbuf("gates", [128, nCols])
    idxc = sbuf("idxc", [128, nCols])
    mcol = sbuf("mcol", [128, nCols])
    rcol = sbuf("rcol", [128, nCols])
    dcol = sbuf("dcol", [128, nCols])
    dcoli = sbuf("dcoli", [128, nCols], I32)
    rcoli = sbuf("rcoli", [128, nCols], I32)
    tok1 = sbuf("tok1", [128, nCols])
    gmapi = sbuf("gmapi", [128, SLOTS // 128], I32)
    gmapf = sbuf("gmapf", [128, SLOTS // 128])
    id128 = sbuf("id128", [128, 128])
    iota16 = sbuf("iota16", [128, E])
    wgt = sbuf("wgt", [128, nD * E])
    smallc = sbuf("smallc", [16, 8])
    c8 = sbuf("c8", [8, E])
    pm8 = sbuf("pm8", [8, 1])
    ones512 = sbuf("ones512", [128, 512])
    zz = sbuf("zz", [128, max(nCols, S // 128, 16)])
    offrow = sbuf("offrow", [1, E])

    ps = [es.enter_context(nc.psum_tensor(f"ps{i}", [128, 512], F32)) for i in range(8)]

    S_ = Sched(nc, es)
    emit, val, wait = S_.emit, S_.val, S_.wait

    def dma(eng, out, in_, sem, waits=(), slow=False):
        emit(eng, lambda e, o=out, i=in_, s=slow: e.dma_start(
            out=o, in_=i, allow_slow_non_contiguous=s),
             waits=waits, inc=sem, amt=16)
        return val(sem)

    dma("sync", id128[:], ident, "c_ld")
    dma("sync", iota16[:], iota16r, "c_ld")
    dma("sync", smallc[:, 0:1], iotaEc, "c_ld")
    dma("sync", smallc[:, 1:2], ebias, "c_ld")
    dma("sync", smallc[:, 2:3], rbias, "c_ld")
    dma("sync", tok1[:], tokid1, "c_ld")
    dma("sync", pm8[:], prevmask, "c_ld")
    for k in range(nD):
        dma("sync", wgt[:, k * E:(k + 1) * E], wg[k], "c_ld")
    C_LD = val("c_ld")
    emit("vector", lambda v: v.memset(carry[:], 0.0))
    emit("vector", lambda v: v.memset(zz[:], 0.0))
    emit("vector", lambda v: v.memset(ones512[:], 1.0))
    emit("vector", lambda v: v.drain(), inc="v")
    V_INIT = val("v")
    dma("gpsimd", smap_in.rearrange("(p c) -> p c", p=128), zz[:, 0:S // 128],
        "g_z", [("v", V_INIT)])
    Z_S = val("g_z")

    if os.environ.get("K_PH0"):
        wait("vector", [("c_ld", C_LD), ("v", V_INIT), ("g_z", Z_S)])
        dma("gpsimd", out_hbm[0:128, :], AR[:, 0:D], "d_out")
        wait("gpsimd", [("d_out", val("d_out"))])
        blk0 = es.enter_context(nc.Block())
        S_.build(blk0)
        es.close()
        return nc

    def ln_alpha(src_ap, rms_after=True):
        emit("vector", lambda v, s=src_ap: v.bn_stats(stat[:, 48:54], s[:, 0:512]))
        emit("vector", lambda v, s=src_ap: v.bn_stats(stat[:, 54:60], s[:, 512:1024]))
        emit("vector", lambda v: v.drain())
        emit("vector", lambda v: v.bn_aggr(stat[:, 44:46], stat[:, 48:60]))
        emit("vector", lambda v: v.drain())
        emit("vector", lambda v: v.tensor_scalar(
            stat[:, 60:61], stat[:, 45:46], LN_EPS, None, ALU.add))
        emit("vector", lambda v: v.drain())
        emit("vector", lambda v: v.reciprocal(stat[:, 61:62], stat[:, 60:61]))
        emit("vector", lambda v: v.drain())
        if rms_after:
            emit("vector", lambda v: v.tensor_tensor(
                stat[:, 62:63], stat[:, 45:46], stat[:, 61:62], ALU.mult))
            emit("vector", lambda v: v.drain())
            emit("vector", lambda v: v.tensor_scalar(
                stat[:, 63:64], stat[:, 62:63], FP32_EPS, None, ALU.add))
            emit("vector", lambda v: v.drain())
            emit("vector", lambda v: v.reciprocal(stat[:, 64:65], stat[:, 63:64]))
        emit("vector", lambda v: v.drain(), inc="v")
        vv = val("v")
        emit("scalar", lambda sc: sc.activation(stat[:, 65:66], stat[:, 61:62], AF.Sqrt),
             waits=[("v", vv)])
        if rms_after:
            emit("scalar", lambda sc: sc.drain())
            emit("scalar", lambda sc: sc.activation(stat[:, 66:67], stat[:, 64:65], AF.Sqrt))
        emit("scalar", lambda sc: sc.drain(), inc="s")
        sv = val("s")
        if rms_after:
            emit("vector", lambda v: v.tensor_tensor(
                stat[:, 45:46], stat[:, 65:66], stat[:, 66:67], ALU.mult),
                waits=[("s", sv)])
        else:
            emit("vector", lambda v: v.tensor_copy(stat[:, 45:46], stat[:, 65:66]),
                 waits=[("s", sv)])
        emit("vector", lambda v: v.drain(), inc="v")

    def transpose_block(src_fn, dst_fn, n, bankpair, tag, pre_waits, rows=128):
        for i in range(n):
            pb = bankpair[i % 2][0:rows, 0:128]
            emit("tensor", lambda tt, i=i, pb=pb: tt.transpose(
                pb, src_fn(i), id128[:]),
                waits=(pre_waits if i == 0 else []) +
                      ([(f"v{tag}", val(f"v{tag}") - 1)] if i >= 2 else []),
                inc=f"t{tag}")
            emit("vector", lambda v, i=i, pb=pb: v.tensor_copy(dst_fn(i), pb),
                waits=[(f"t{tag}", val(f"t{tag}"))], inc=f"v{tag}")
        return val(f"v{tag}")

    BF16 = mybir.dt.bfloat16
    X0LD = dma("sync", SQA[:, 0:nSub * D].rearrange("p (s d) -> p s d", s=nSub),
               x_hbm[0:TILE, :].rearrange("(s p) d -> p s d", p=128), "d_x")
    PW = [dict(land=A(XM_O + 4096 + i * 4096, 2048),
               hi=A(XM_O + 4096 + i * 4096 + 2048, 1024).bitcast(BF16),
               lo=A(XM_O + 4096 + i * 4096 + 3072, 1024).bitcast(BF16))
          for i in range(2)]
    pw_n = [0]

    def presplit(fp32_src, hi_dst, lo_dst, width, on_gp=False):
        i = pw_n[0] % 2
        pw_n[0] += 1
        st = PW[i]
        ld = dma("scalar", st["land"][:, 0:width], fp32_src, f"d_pwi{i}",
                 [(f"d_pwo{i}", val(f"d_pwo{i}"))])
        if on_gp:
            emit("gpsimd", lambda g, st=st, width=width: g.tensor_copy(
                st["hi"][:, 0:width], st["land"][:, 0:width]),
                waits=[(f"d_pwi{i}", ld)], inc="g_pwh")
            emit("gpsimd", lambda g, st=st, width=width: g.tensor_tensor(
                st["lo"][:, 0:width], st["land"][:, 0:width], st["hi"][:, 0:width],
                ALU.subtract),
                waits=[("g_pwh", val("g_pwh"))], inc="g_pwl")
            PWV = ("g_pwl", val("g_pwl"))
        else:
            emit("scalar", lambda sc, st=st, width=width: sc.copy(
                st["hi"][:, 0:width], st["land"][:, 0:width]),
                waits=[(f"d_pwi{i}", ld)], inc="s_pw")
            emit("vector", lambda v, st=st, width=width: v.tensor_tensor(
                st["lo"][:, 0:width], st["land"][:, 0:width], st["hi"][:, 0:width],
                ALU.subtract),
                waits=[("s_pw", val("s_pw"))], inc="v_pw")
            PWV = ("v_pw", val("v_pw"))
        dma("scalar", hi_dst, st["hi"][:, 0:width], f"d_pwo{i}", [PWV])
        dma("scalar", lo_dst, st["lo"][:, 0:width], f"d_pwo{i}")

    MARKS = {}

    def presplit_qkv():
        for k in range(nD):
            presplit(wqk[k], wqk_h[k], wqk_l[k], QK)
        MARKS["QK"] = [("d_pwo0", val("d_pwo0")), ("d_pwo1", val("d_pwo1"))]
        for e in range(nD):
            presplit(wv[e].rearrange("k p f -> p k f"),
                     wv_h[e].rearrange("k p f -> p k f"),
                     wv_l[e].rearrange("k p f -> p k f"), nD * 128)
        MARKS["WV"] = [("d_pwo0", val("d_pwo0")), ("d_pwo1", val("d_pwo1"))]

    def prefetch_stats(xld_mark, tile):
        B = 72
        DMS0 = val("d_sms")
        for s in range(nSub):
            xs = SQA[:, s * D:(s + 1) * D]
            emit("vector", lambda v, xs=xs: v.bn_stats(stat[:, B + 4:B + 10], xs[:, 0:512]),
                 waits=[("d_x", xld_mark)] if s == 0 else [])
            emit("vector", lambda v, xs=xs: v.bn_stats(stat[:, B + 10:B + 16], xs[:, 512:1024]))
            emit("vector", lambda v: v.drain())
            emit("vector", lambda v: v.bn_aggr(stat[:, B:B + 2], stat[:, B + 4:B + 16]))
            emit("vector", lambda v: v.drain())
            emit("vector", lambda v: v.tensor_scalar(
                stat[:, B + 16:B + 17], stat[:, B + 1:B + 2], LN_EPS, None, ALU.add))
            emit("vector", lambda v: v.drain())
            emit("vector", lambda v: v.reciprocal(stat[:, B + 17:B + 18], stat[:, B + 16:B + 17]))
            emit("vector", lambda v: v.drain())
            emit("vector", lambda v: v.tensor_tensor(
                stat[:, B + 18:B + 19], stat[:, B + 1:B + 2], stat[:, B + 17:B + 18], ALU.mult))
            emit("vector", lambda v: v.drain())
            emit("vector", lambda v: v.tensor_scalar(
                stat[:, B + 19:B + 20], stat[:, B + 18:B + 19], FP32_EPS, None, ALU.add))
            emit("vector", lambda v: v.drain())
            emit("vector", lambda v: v.reciprocal(stat[:, B + 20:B + 21], stat[:, B + 19:B + 20]))
            emit("vector", lambda v: v.drain(), inc="v_pf")
            PFV = val("v_pf")
            emit("scalar", lambda sc: sc.activation(
                stat[:, B + 21:B + 22], stat[:, B + 17:B + 18], AF.Sqrt),
                waits=[("v_pf", PFV)])
            emit("scalar", lambda sc: sc.drain())
            emit("scalar", lambda sc: sc.activation(
                stat[:, B + 22:B + 23], stat[:, B + 20:B + 21], AF.Sqrt))
            emit("scalar", lambda sc: sc.drain(), inc="s_pf")
            SPF = val("s_pf")
            emit("vector", lambda v, s=s: v.tensor_copy(stat[:, s:s + 1], stat[:, B:B + 1]),
                 waits=[("s_pf", SPF)] + ([("d_sms", DMS0)] if s == 0 else []))
            emit("vector", lambda v, s=s: v.tensor_copy(
                stat[:, nSub + s:nSub + s + 1], stat[:, B + 21:B + 22]))
            emit("vector", lambda v, s=s: v.tensor_tensor(
                stat[:, 8 + s:9 + s], stat[:, B + 21:B + 22], stat[:, B + 22:B + 23],
                ALU.mult))
            emit("vector", lambda v: v.drain(), inc="v_pf")
        dma("scalar", sp_ms[:, tile * 2 * nSub:(tile + 1) * 2 * nSub],
            stat[:, 0:2 * nSub], "d_sms", [("v_pf", val("v_pf"))])
        return val("v_pf")

    STOPA = int(os.environ.get("K_STOPA", "0"))

    def bail():
        snap = [(s, v) for s, v in S_.cnt.items() if v > 0]
        for e in ENGINES:
            S_.emit(e, None, waits=snap)
        dma("gpsimd", out_hbm[0:128, :], AR[:, 0:D], "d_out")
        wait("gpsimd", [("d_out", val("d_out"))])
        blkb = es.enter_context(nc.Block())
        S_.build(blkb)
        es.close()
        return nc

    for t in range(nTiles):
        TB = t * TILE
        tb_snap = [(s, v) for s, v in S_.cnt.items()
                   if v > 0 and s[0] in "tvs"] + [("d_sp", val("d_sp"))]
        if t == 0:
            wait("vector", [("c_ld", C_LD), ("v", V_INIT)])
            PF = prefetch_stats(X0LD, 0)
        for s in range(nSub):
            emit("vector", lambda v, s=s: v.tensor_scalar(
                B2[:, s * D:(s + 1) * D], SQA[:, s * D:(s + 1) * D],
                stat[:, s:s + 1], stat[:, 8 + s:9 + s],
                ALU.subtract, ALU.mult),
                waits=([("v_pf", val("v_pf")), ("s_pf", val("s_pf"))] +
                       (tb_snap if t > 0 else [])) if s == 0 else [])
            emit("vector", lambda v: v.drain(), inc="v_xn")
        XN_V = val("v_xn")
        if t == 0:
            presplit_qkv()
        if STOPA == 2:
            return bail()

        for i in range(nSub * nD):
            s_, k_ = i // nD, i % nD
            pb = (ps[4], ps[5])[i % 2][:, 0:128]
            co = k_ * 512 + s_ * 128
            co2 = k_ * 512 + s_ * 128
            emit("tensor", lambda tt, i=i, s_=s_, k_=k_, pb=pb: tt.transpose(
                pb, B2[:, s_ * D + k_ * 128:s_ * D + k_ * 128 + 128], id128[:]),
                waits=([("v_xn", XN_V)] if i == 0 else []) +
                      ([("vtA", val("vtA") - 1), ("stA", val("stA") - 1)] if i >= 2 else []),
                inc="ttA")
            emit("scalar", lambda sc, co=co, pb=pb: sc.activation(
                XNh[:, co:co + 128], pb, AF.Copy),
                waits=[("ttA", val("ttA"))], inc="stA")
            emit("vector", lambda v, co=co, pb=pb: v.tensor_tensor(
                XNl[:, co:co + 128], pb, XNh[:, co:co + 128], ALU.subtract),
                waits=[("ttA", val("ttA")), ("stA", val("stA"))], inc="vtA")
        XNF_V, XNF_S = val("vtA"), val("stA")
        if STOPA == 3:
            return bail()

        for ec in range(nQKe):
            half = ec % 2
            wqld = dma("sync", WQp[half][0].rearrange("p (k e) -> p k e", k=nD),
                       wqk_h[:, :, ec * 512:(ec + 1) * 512].rearrange("k p e -> p k e"),
                       f"d_wqk{half}",
                       MARKS["QK"] +
                       ([("t_qk", max(0, val("t_qk") - nSub))] if val("t_qk") > nSub else []))
            wqld = dma("sync", WQp[half][1].rearrange("p (k e) -> p k e", k=nD),
                       wqk_l[:, :, ec * 512:(ec + 1) * 512].rearrange("k p e -> p k e"),
                       f"d_wqk{half}")
            for s in range(nSub):
                bank = (ec * nSub + s) % 2
                j = 0
                for (xp, wp) in ((XNh, WQp[half][0]), (XNh, WQp[half][1]),
                                 (XNl, WQp[half][0])):
                    for k in range(nD):
                        emit("tensor", lambda tt, s=s, k=k, j=j, xp=xp, wp=wp, bank=bank: tt.matmul(
                            ps[bank][:],
                            xp[:, k * 512 + s * 128:k * 512 + s * 128 + 128],
                            wp[:, k * 512:(k + 1) * 512],
                            start=(j == 0), stop=(j == 3 * nD - 1)),
                            waits=([("vtA", XNF_V), ("stA", XNF_S), (f"d_wqk{half}", wqld)] +
                                   ([("v_qk", val("v_qk") - 1)] if val("v_qk") >= 2 else []))
                                  if j == 0 else [],
                            inc=("t_qk" if j == 3 * nD - 1 else None))
                        j += 1
                dst = B2 if ec < nQKe // 2 else B1
                eoff = (ec % (nQKe // 2)) * 512
                emit("vector", lambda v, s=s, dst=dst, eoff=eoff, bank=bank: v.tensor_copy(
                    dst[:, s * D + eoff:s * D + eoff + 512], ps[bank][:]),
                    waits=[("t_qk", val("t_qk"))], inc="v_qk")
        QK_V = val("v_qk")
        if t + 1 < nTiles:
            xldn = dma("sync", SQA[:, 0:nSub * D].rearrange("p (s d) -> p s d", s=nSub),
                       x_hbm[(t + 1) * TILE:(t + 2) * TILE, :].rearrange(
                           "(s p) d -> p s d", p=128),
                       "d_x", [("v_xn", XN_V), ("s_sq", val("s_sq"))])
            prefetch_stats(xldn, t + 1)
        if STOPA == 4:
            return bail()

        for ec in range(nD):
            half = ec % 2
            wvld = dma("sync", WSp[half][0].rearrange("p (k f) -> p k f", k=nD),
                       wv_h[ec].rearrange("k p f -> p k f"),
                       f"d_wv{half}",
                       MARKS["WV"] +
                       ([("t_v", max(0, val("t_v") - 1))] if val("t_v") > 1 else []))
            wvld = dma("sync", WSp[half][1].rearrange("p (k f) -> p k f", k=nD),
                       wv_l[ec].rearrange("k p f -> p k f"), f"d_wv{half}")
            bank = 2 + ec % 2
            j = 0
            for (wp, xp) in ((WSp[half][0], XNh), (WSp[half][0], XNl),
                             (WSp[half][1], XNh)):
                for k in range(nD):
                    emit("tensor", lambda tt, k=k, j=j, wp=wp, xp=xp, bank=bank: tt.matmul(
                        ps[bank][:],
                        wp[:, k * 128:(k + 1) * 128],
                        xp[:, k * 512:(k + 1) * 512],
                        start=(j == 0), stop=(j == 3 * nD - 1)),
                        waits=([("vtA", XNF_V), ("stA", XNF_S), (f"d_wv{half}", wvld)] +
                               ([("v_v", val("v_v") - 1)] if val("v_v") >= 2 else []))
                              if j == 0 else [],
                        inc=("t_v" if j == 3 * nD - 1 else None))
                    j += 1
            emit("vector", lambda v, ec=ec, bank=bank: v.tensor_copy(
                B4[:, ec * 512:(ec + 1) * 512], ps[bank][:]),
                waits=[("t_v", val("t_v"))], inc="v_v")
        V_V = val("v_v")
        if STOPA == 5:
            return bail()

        for s in range(nSub):
            emit("scalar", lambda sc, s=s: sc.activation(
                A(B3_O, nSub * D)[:, s * D:(s + 1) * D], B2[:, s * D:(s + 1) * D],
                AF.Square, accum_out=stat[:, 16 + s:17 + s]),
                waits=[("v_qk", QK_V), ("t_v", val("t_v"))], inc="s_sq")
            emit("scalar", lambda sc: sc.drain())
            emit("scalar", lambda sc, s=s: sc.activation(
                A(B3_O, nSub * D)[:, s * D:(s + 1) * D], B1[:, s * D:(s + 1) * D],
                AF.Square,
                accum_out=stat[:, 20 + s:21 + s]), inc="s_sq")
            SQS = val("s_sq")
            for j, off in ((16 + s, 24 + s), (20 + s, 28 + s)):
                emit("vector", lambda v, j=j: v.tensor_scalar(
                    stat[:, 60:61], stat[:, j:j + 1], 1.0 / D, FP32_EPS, ALU.mult, ALU.add),
                    waits=[("s_sq", SQS), ("s_a", val("s_a"))])
                emit("vector", lambda v: v.drain())
                emit("vector", lambda v: v.reciprocal(stat[:, 61:62], stat[:, 60:61]))
                emit("vector", lambda v: v.drain(), inc="v_a")
                emit("scalar", lambda sc, off=off: sc.activation(
                    stat[:, off:off + 1], stat[:, 61:62], AF.Sqrt),
                    waits=[("v_a", val("v_a"))], inc="s_a")
            AQK = val("s_a")
            emit("scalar", lambda sc, s=s: sc.activation(
                B2[:, s * D:(s + 1) * D], B2[:, s * D:(s + 1) * D], AF.Sigmoid,
                scale=stat[:, 24 + s:25 + s]), waits=[("s_a", AQK)], inc="s_g")
            emit("scalar", lambda sc, s=s: sc.activation(
                B1[:, s * D:(s + 1) * D], B1[:, s * D:(s + 1) * D], AF.Exp,
                scale=stat[:, 28 + s:29 + s]), inc="s_g")
        GATE_S = val("s_g")
        if STOPA == 6:
            return bail()

        TRB_V = transpose_block(
            lambda i: B2[:, (i // nD) * D + (i % nD) * 128:(i // nD) * D + (i % nD) * 128 + 128],
            lambda i: B5[:, (i % nD) * 512 + (i // nD) * 128:(i % nD) * 512 + (i // nD) * 128 + 128],
            nSub * nD, (ps[4], ps[5]), "tB", [("s_g", GATE_S)])
        TRC_V = transpose_block(
            lambda i: B1[:, (i // nD) * D + (i % nD) * 128:(i // nD) * D + (i % nD) * 128 + 128],
            lambda i: B3[:, (i % nD) * 512 + (i // nD) * 128:(i % nD) * 512 + (i // nD) * 128 + 128],
            nSub * nD, (ps[6], ps[7]), "tC",
            [("s_g", GATE_S), ("t_v", val("t_v")), ("v_v", V_V)])
        if STOPA == 7:
            return bail()

        for k in range(nD):
            emit("vector", lambda v, k=k: v.tensor_tensor(
                B2[:, k * 512:(k + 1) * 512], B3[:, k * 512:(k + 1) * 512],
                B4[:, k * 512:(k + 1) * 512], ALU.mult),
                waits=[("vtC", TRC_V), ("v_v", V_V)] if k == 0 else [],
                inc="v_kv")
        KV_V = val("v_kv")
        for k in range(nD):
            emit("vector", lambda v, k=k: v.tensor_tensor_scan(
                B1[:, k * 512:(k + 1) * 512], ones512[:], B3[:, k * 512:(k + 1) * 512],
                carry[:, k:k + 1], ALU.mult, ALU.add),
                waits=[("v_kv", KV_V)] if k == 0 else [])
            emit("vector", lambda v: v.drain())
            emit("vector", lambda v, k=k: v.tensor_copy(
                carry[:, k:k + 1], B1[:, (k + 1) * 512 - 1:(k + 1) * 512]))
            emit("vector", lambda v: v.drain(), inc="v_sc")
            emit("vector", lambda v, k=k: v.tensor_tensor_scan(
                B4[:, k * 512:(k + 1) * 512], ones512[:], B2[:, k * 512:(k + 1) * 512],
                carry[:, 8 + k:9 + k], ALU.mult, ALU.add))
            emit("vector", lambda v: v.drain())
            emit("vector", lambda v, k=k: v.tensor_copy(
                carry[:, 8 + k:9 + k], B4[:, (k + 1) * 512 - 1:(k + 1) * 512]))
            emit("vector", lambda v: v.drain(), inc="v_sc")
        SC_V = val("v_sc")
        if STOPA == 8:
            return bail()

        for k in range(nD):
            dma("scalar", sp_sigq[k * 128:(k + 1) * 128, TB:TB + TILE],
                B5[:, k * 512:(k + 1) * 512], "d_sp",
                [("vtB", TRB_V)] if k == 0 else [])
            dma("scalar", sp_wcs[k * 128:(k + 1) * 128, TB:TB + TILE],
                B1[:, k * 512:(k + 1) * 512], "d_sp",
                [("v_sc", SC_V)] if k == 0 else [])
            dma("scalar", sp_kvcs[k * 128:(k + 1) * 128, TB:TB + TILE],
                B4[:, k * 512:(k + 1) * 512], "d_sp", [])
    SPILL_D = val("d_sp")
    TVF = val("t_v")

    for f in range(nSW):
        presplit(wsw[f].rearrange("k p f2 -> p k f2"),
                 wsw_h[f].rearrange("k p f2 -> p k f2"),
                 wsw_l[f].rearrange("k p f2 -> p k f2"), nD * 128)
    MARK_SW = [("d_pwo0", val("d_pwo0")), ("d_pwo1", val("d_pwo1"))]
    for f in range(nD):
        presplit(wout[f], wout_h[f], wout_l[f], D)
    MARK_WO = [("d_pwo0", val("d_pwo0")), ("d_pwo1", val("d_pwo1"))]

    SKIP_CC = bool(os.environ.get("K_SKIP_CC"))
    dma("gpsimd", carry_in, carry[:], "g_d", [("v_sc", val("v_sc"))])
    if not SKIP_CC:
        emit("gpsimd", lambda g: g.collective_compute(
            "AllGather", ALU.bypass, replica_groups=[list(range(cfg["n_cores"]))],
            ins=[carry_in.opt()], outs=[carry_all.opt()]),
            waits=[("g_d", val("g_d"))], inc="g_cc")
    else:
        dma("gpsimd", carry_all[0:128, :], carry_in, "g_d", [("g_d", val("g_d"))])
        emit("gpsimd", lambda g: g.nop(), waits=[("g_d", val("g_d"))], inc="g_cc")
    CC_CARRY = val("g_cc")
    car8 = A(B1_O, 2048, 0, 8)
    dma("gpsimd", car8.rearrange("j (p c) -> j p c", p=128),
        carry_all.rearrange("(j p) c -> j p c", j=8),
        "g_d", [("g_cc", CC_CARRY), ("d_sp", SPILL_D)])
    CAR_D = val("g_d")
    for q in range(4):
        emit("tensor", lambda tt, q=q: tt.matmul(
            ps[7][0:1, :], pm8[:], car8[:, q * 512:(q + 1) * 512],
            start=True, stop=True),
            waits=[("g_d", CAR_D), ("c_ld", C_LD)] + ([("v_cg", val("v_cg"))] if q else []),
            inc="t_cg")
        emit("vector", lambda v, q=q: v.tensor_copy(
            CGROW[:, q * 512:(q + 1) * 512], ps[7][0:1, :]),
            waits=[("t_cg", val("t_cg"))], inc="v_cg")
    CG_V = val("v_cg")
    dma("gpsimd", rowb[3, 0:2048], CGROW[:], "g_d", [("v_cg", CG_V)])
    dma("gpsimd", carryg[:, 0:16], rowb[3, 0:2048].rearrange("(p c) -> p c", p=128),
        "g_d", [("g_d", val("g_d"))])
    CARRYG_D = val("g_d")

    for k in range(nD):
        dma("sync", B5[:, k * 512:(k + 1) * 512],
            sp_sigq[k * 128:(k + 1) * 128, 0:TILE], "d_ldB",
            [("d_sp", SPILL_D)] if k == 0 else [])
        dma("sync", B4[:, k * 512:(k + 1) * 512],
            sp_kvcs[k * 128:(k + 1) * 128, 0:TILE], "d_ldB")
    dma("sync", B3[:, 0:nSub * D].rearrange("p (s d) -> p s d", s=nSub),
        x_hbm[0:TILE, :].rearrange("(s p) d -> p s d", p=128), "d_ldB")
    dma("sync", stat[:, 16:16 + 2 * nSub], sp_ms[:, 0:2 * nSub], "d_ldB",
        [("d_sms", val("d_sms"))])
    for k in range(nD):
        dma("sync", B1[:, k * 512:(k + 1) * 512],
            sp_wcs[k * 128:(k + 1) * 128, 0:TILE], "d_ldB",
            [("v_cg", CG_V)] if k == 0 else [])
    LDB0 = val("d_ldB")

    def finish_early():
        dma("gpsimd", out_hbm[0:128, :], zz[:, 0:1].partition_broadcast(128) if False else AR[:, 0:D], "d_out")
        wait("gpsimd", [("d_out", val("d_out"))])
        blk = es.enter_context(nc.Block())
        S_.build(blk)
        es.close()
        return nc

    if phases < 2:
        S_.barrier()
        return finish_early()


    for t in range(nTiles):
        TB = t * TILE
        tb_snap = [(s, v) for s, v in S_.cnt.items()
                   if v > 0 and s[0] in "tvs"] + [("d_xr", val("d_xr"))]
        if t > 0:
            for k in range(nD):
                dma("sync", B5[:, k * 512:(k + 1) * 512],
                    sp_sigq[k * 128:(k + 1) * 128, TB:TB + TILE], "d_ldB",
                    tb_snap if k == 0 else [])
                dma("sync", B1[:, k * 512:(k + 1) * 512],
                    sp_wcs[k * 128:(k + 1) * 128, TB:TB + TILE], "d_ldB")
                dma("sync", B4[:, k * 512:(k + 1) * 512],
                    sp_kvcs[k * 128:(k + 1) * 128, TB:TB + TILE], "d_ldB")
            dma("sync", B3[:, 0:nSub * D].rearrange("p (s d) -> p s d", s=nSub),
                x_hbm[TB:TB + TILE, :].rearrange("(s p) d -> p s d", p=128),
                "d_ldB")
            dma("sync", stat[:, 16:16 + 2 * nSub],
                sp_ms[:, t * 2 * nSub:(t + 1) * 2 * nSub], "d_ldB",
                [("d_sms", val("d_sms"))])
        LDB = val("d_ldB")
        CWP_PRE = val("v_y")

        for s in range(nSub):
            xs3 = B3[:, s * D:(s + 1) * D]
            emit("vector", lambda v, s=s: v.tensor_scalar(
                stat[:, 63:64], stat[:, 16 + nSub + s:17 + nSub + s], 1.0,
                None, ALU.add),
                waits=[("d_ldB", LDB)] if s == 0 else [])
            emit("vector", lambda v: v.drain())
            emit("vector", lambda v: v.reciprocal(stat[:, 64:65], stat[:, 63:64]))
            emit("vector", lambda v: v.drain())
            emit("vector", lambda v, s=s: v.tensor_tensor(
                stat[:, 65:66], stat[:, 16 + s:17 + s],
                stat[:, 16 + nSub + s:17 + nSub + s], ALU.mult))
            emit("vector", lambda v: v.drain())
            emit("vector", lambda v: v.tensor_tensor(
                stat[:, 65:66], stat[:, 65:66], stat[:, 64:65], ALU.mult))
            emit("vector", lambda v: v.drain())
            emit("vector", lambda v, xs3=xs3: v.tensor_scalar(
                xs3, xs3, stat[:, 65:66], stat[:, 63:64], ALU.subtract, ALU.mult))
            emit("vector", lambda v: v.drain(), inc="v_y")
        XZ_V = val("v_y")
        if t == 0:
            emit("vector", lambda v: v.tensor_scalar(
                carryg[:, 16:24], carryg[:, 0:8], 1e-6, None, ALU.add),
                waits=[("g_d", CARRYG_D)])
            emit("vector", lambda v: v.drain(), inc="v_y")
        CWP_V = val("v_y") if t == 0 else CWP_PRE
        for k in range(nD):
            emit("scalar", lambda sc, k=k: sc.activation(
                B1[:, k * 512:(k + 1) * 512], B1[:, k * 512:(k + 1) * 512], AF.Ln,
                bias=carryg[:, 16 + k:17 + k]),
                waits=[("v_y", CWP_V), ("d_ldB", LDB)] if k == 0 else [])
            emit("scalar", lambda sc: sc.drain())
            emit("scalar", lambda sc, k=k: sc.activation(
                B1[:, k * 512:(k + 1) * 512], B1[:, k * 512:(k + 1) * 512], AF.Exp,
                scale=-1.0), inc="s_y")
            SY = val("s_y")
            emit("vector", lambda v, k=k: v.tensor_scalar(
                B4[:, k * 512:(k + 1) * 512], B4[:, k * 512:(k + 1) * 512],
                carryg[:, 8 + k:9 + k], None, ALU.add),
                waits=[("s_y", SY), ("d_ldB", LDB)])
            emit("vector", lambda v: v.drain())
            emit("vector", lambda v, k=k: v.tensor_tensor(
                B4[:, k * 512:(k + 1) * 512], B4[:, k * 512:(k + 1) * 512],
                B1[:, k * 512:(k + 1) * 512], ALU.mult))
            emit("vector", lambda v: v.drain())
            emit("vector", lambda v, k=k: v.tensor_tensor(
                B1[:, k * 512:(k + 1) * 512], B5[:, k * 512:(k + 1) * 512],
                B4[:, k * 512:(k + 1) * 512], ALU.mult))
            emit("vector", lambda v: v.drain())
            emit("vector", lambda v, k=k: v.tensor_copy(
                Yh[:, k * 512:(k + 1) * 512], B1[:, k * 512:(k + 1) * 512]))
            emit("vector", lambda v: v.drain())
            emit("vector", lambda v, k=k: v.tensor_tensor(
                Yl[:, k * 512:(k + 1) * 512], B1[:, k * 512:(k + 1) * 512],
                Yh[:, k * 512:(k + 1) * 512], ALU.subtract), inc="v_y")
        Y_V = val("v_y")

        sw_marks = []
        for fc in range(nSW):
            half = fc % 2
            wld = dma("sync", WSp[half][0].rearrange("p (k f) -> p k f", k=nD),
                      wsw_h[fc].rearrange("k p f -> p k f"),
                      f"d_wsw{half}",
                      MARK_SW +
                      ([("t_v", TVF)] if t == 0 and fc == 0 else []) +
                      ([("t_sw", max(0, val("t_sw") - 1))] if val("t_sw") > 1 else []))
            wld = dma("sync", WSp[half][1].rearrange("p (k f) -> p k f", k=nD),
                      wsw_l[fc].rearrange("k p f -> p k f"), f"d_wsw{half}")
            bank = fc % 2
            j = 0
            for (wp, yp) in ((WSp[half][0], Yh), (WSp[half][0], Yl),
                             (WSp[half][1], Yh)):
                for k in range(nD):
                    emit("tensor", lambda tt, k=k, j=j, wp=wp, yp=yp, bank=bank: tt.matmul(
                        ps[bank][:],
                        wp[:, k * 128:(k + 1) * 128],
                        yp[:, k * 512:(k + 1) * 512],
                        start=(j == 0), stop=(j == 3 * nD - 1)),
                        waits=([("v_y", Y_V), (f"d_wsw{half}", wld)] +
                               ([sw_marks[fc - 2]] if fc >= 2 else []))
                              if j == 0 else [],
                        inc=("t_sw" if j == 3 * nD - 1 else None))
                    j += 1
            if fc < nSW // 2:
                emit("vector", lambda v, fc=fc, bank=bank: v.tensor_copy(
                    B4[:, fc * 512:(fc + 1) * 512], ps[bank][:]),
                    waits=[("t_sw", val("t_sw"))], inc="v_sw")
                sw_marks.append(("v_sw", val("v_sw")))
            else:
                emit("scalar", lambda sc, fc=fc, bank=bank: sc.activation(
                    B1[:, (fc - nSW // 2) * 512:(fc - nSW // 2 + 1) * 512],
                    ps[bank][:], AF.Silu),
                    waits=[("t_sw", val("t_sw"))], inc="s_sw")
                sw_marks.append(("s_sw", val("s_sw")))
        SW_V, SW_S = val("v_sw"), val("s_sw")
        for k in range(nD):
            emit("vector", lambda v, k=k: v.tensor_tensor(
                B5[:, k * 512:(k + 1) * 512], B4[:, k * 512:(k + 1) * 512],
                B1[:, k * 512:(k + 1) * 512], ALU.mult),
                waits=[("v_sw", SW_V), ("s_sw", SW_S), ("t_sw", val("t_sw"))]
                      if k == 0 else [])
            emit("vector", lambda v: v.drain())
            emit("vector", lambda v, k=k: v.tensor_copy(
                Yh[:, k * 512:(k + 1) * 512], B5[:, k * 512:(k + 1) * 512]))
            emit("vector", lambda v: v.drain())
            emit("vector", lambda v, k=k: v.tensor_tensor(
                Yl[:, k * 512:(k + 1) * 512], B5[:, k * 512:(k + 1) * 512],
                Yh[:, k * 512:(k + 1) * 512], ALU.subtract), inc="v_usg")
        USG_V = val("v_usg")

        for dc in range(nDO):
            wld = dma("sync", WQp[dc % 2][0].rearrange("p (f d) -> p f d", f=nD),
                      wout_h[:, :, dc * 512:(dc + 1) * 512].rearrange("f p d -> p f d"),
                      f"d_wo{dc % 2}",
                      MARK_WO +
                      ([("t_ou", max(0, val("t_ou") - nSub))] if val("t_ou") > nSub else []))
            wld = dma("sync", WQp[dc % 2][1].rearrange("p (f d) -> p f d", f=nD),
                      wout_l[:, :, dc * 512:(dc + 1) * 512].rearrange("f p d -> p f d"),
                      f"d_wo{dc % 2}")
            for s in range(nSub):
                bank = 2 + (dc * nSub + s) % 2
                j = 0
                for (up, wp) in ((Yh, WQp[dc % 2][0]), (Yh, WQp[dc % 2][1]),
                                 (Yl, WQp[dc % 2][0])):
                    for k in range(nD):
                        emit("tensor", lambda tt, s=s, k=k, j=j, up=up, wp=wp, bank=bank: tt.matmul(
                            ps[bank][:],
                            up[:, k * 512 + s * 128:k * 512 + s * 128 + 128],
                            wp[:, k * 512:(k + 1) * 512],
                            start=(j == 0), stop=(j == 3 * nD - 1)),
                            waits=([("v_usg", USG_V), (f"d_wo{dc % 2}", wld)] +
                                   ([("v_xm", val("v_xm") - 1)] if val("v_xm") >= 2 else []))
                                  if j == 0 else [],
                            inc=("t_ou" if j == 3 * nD - 1 else None))
                        j += 1
                col = t * nSub + s
                emit("vector", lambda v, s=s, dc=dc, col=col, bank=bank: v.scalar_tensor_tensor(
                    XM[:, col * D + dc * 512:col * D + (dc + 1) * 512],
                    ps[bank][:], 1.0, B3[:, s * D + dc * 512:s * D + (dc + 1) * 512],
                    ALU.mult, ALU.add),
                    waits=[("t_ou", val("t_ou")), ("d_ldB", LDB), ("v_y", XZ_V)] +
                          (MARK_WO if t == 0 and dc == 0 and s == 0 else []),
                    inc="v_xm")
        XM_V = val("v_xm")

        for s in range(nSub):
            col = t * nSub + s
            xms = XM[:, col * D:(col + 1) * D]
            wait("vector", [("v_xm", XM_V)])
            ln_alpha(xms, rms_after=False)
            emit("vector", lambda v, s=s, xms=xms: v.tensor_scalar(
                B3[:, s * D:(s + 1) * D], xms, stat[:, 44:45], stat[:, 45:46],
                ALU.subtract, ALU.mult))
            emit("vector", lambda v: v.drain(), inc="v_xr")
        XR_V = val("v_xr")
        dma("scalar", xr_in[TB:TB + TILE, :].rearrange("(s p) d -> p s d", p=128),
            B3[:, 0:nSub * D].rearrange("p (s d) -> p s d", s=nSub),
            "d_xr", [("v_xr", XR_V)])
        emit("gpsimd", lambda g, t=t, TB=TB: g.collective_compute(
            "AllGather", ALU.bypass, replica_groups=[list(range(cfg["n_cores"]))],
            ins=[xr_in[TB:TB + TILE, :].opt()],
            outs=[xr_all[t * cfg["n_cores"] * TILE:(t + 1) * cfg["n_cores"] * TILE, :].opt()]),
            waits=[("d_xr", val("d_xr"))], inc="g_xr")

        XRF_V = transpose_block(
            lambda i: B3[:, (i // nD) * D + (i % nD) * 128:(i // nD) * D + (i % nD) * 128 + 128],
            lambda i: B5[:, (i % nD) * 512 + (i // nD) * 128:(i % nD) * 512 + (i // nD) * 128 + 128],
            nSub * nD, (ps[6], ps[7]), "tD",
            [("v_xr", XR_V), ("t_ou", val("t_ou"))])

        for s in range(nSub):
            col = t * nSub + s
            for k in range(nD):
                emit("tensor", lambda tt, s=s, k=k: tt.matmul(
                    ps[7][0:128, 0:E],
                    B5[:, k * 512 + s * 128:k * 512 + s * 128 + 128],
                    wgt[:, k * E:(k + 1) * E],
                    start=(k == 0), stop=(k == nD - 1)),
                    waits=([("vtD", XRF_V), ("c_ld", C_LD),
                            ("v_gt", val("v_gt")), ("s_gt", val("s_gt"))])
                          if k == 0 else [],
                    inc=("t_gt" if k == nD - 1 else None))
            TGT = val("t_gt")
            emit("vector", lambda v: v.tensor_copy(stat[:, 32:48], ps[7][0:128, 0:E]),
                 waits=[("t_gt", TGT)])
            emit("vector", lambda v: v.drain())
            emit("vector", lambda v: v.reduce_max(stat[:, 48:49], stat[:, 32:48], axis=AXX))
            emit("vector", lambda v: v.drain())
            emit("vector", lambda v: v.tensor_scalar(
                stat[:, 49:50], stat[:, 48:49], -1.0, None, ALU.mult))
            emit("vector", lambda v: v.drain())
            emit("vector", lambda v: v.tensor_scalar(
                stat[:, 50:66], stat[:, 32:48], stat[:, 48:49], None, ALU.is_equal))
            emit("vector", lambda v: v.drain())
            emit("vector", lambda v, col=col: v.scalar_tensor_tensor(
                stat[:, 50:66], stat[:, 50:66], 1.0, iota16[:],
                ALU.mult, ALU.mult, accum_out=idxc[:, col:col + 1]))
            emit("vector", lambda v: v.drain(), inc="v_gt")
            VGT = val("v_gt")
            emit("scalar", lambda sc: sc.activation(
                stat[:, 50:66], stat[:, 32:48], AF.Exp, bias=stat[:, 49:50],
                accum_out=stat[:, 66:67]), waits=[("v_gt", VGT)], inc="s_gt")
            SGT = val("s_gt")
            emit("vector", lambda v, col=col: v.reciprocal(
                gates[:, col:col + 1], stat[:, 66:67]), waits=[("s_gt", SGT)])
            emit("vector", lambda v: v.drain(), inc="v_gt")
    GATES_V = val("v_gt")
    XR_D = val("d_xr")

    CC_XR = ("g_xr", val("g_xr"))

    if phases < 3:
        S_.barrier(exclude=())
        return finish_early()

    S_.barrier(exclude=("g_cc",))
    wait("gpsimd", [("g_cc", CC_CARRY)])
    for j in range(nCols):
        pb = ps[6 + j % 2][0:1, 0:128]
        emit("tensor", lambda tt, j=j, pb=pb: tt.transpose(
            pb, idxc[:, j:j + 1], id128[:]),
            waits=([("v_gt", GATES_V)] if j == 0 else []) +
                  ([("v_ir", val("v_ir") - 1)] if j >= 2 else []),
            inc="t_ir")
        emit("vector", lambda v, j=j, pb=pb: v.tensor_copy(
            IDXROW[:, j * 128:(j + 1) * 128], pb),
            waits=[("t_ir", val("t_ir"))], inc="v_ir")
    IR_V = val("v_ir")
    dma("gpsimd", idxrow_d, IDXROW[0:1, :], "g_d", [("v_ir", IR_V)])
    IRD = val("g_d")
    dma("gpsimd", A(B1_O, T, 0, 16), idxrow_d.unsqueeze(0).partition_broadcast(16),
        "g_d", [("g_d", IRD)])
    B16 = val("g_d")
    emit("vector", lambda v: v.memset(ONES16[:], 1.0))
    emit("vector", lambda v: v.drain(), inc="v_rt")
    emit("vector", lambda v: v.tensor_scalar(
        ROW_OH[:], ROW_OH[:], smallc[:, 0:1], None, ALU.is_equal),
        waits=[("g_d", B16), ("c_ld", C_LD), ("v_rt", val("v_rt"))])
    emit("vector", lambda v: v.drain())
    emit("vector", lambda v: v.tensor_tensor_scan(
        ROW_SC[:], ONES16[:], ROW_OH[:], 0.0, ALU.mult, ALU.add))
    emit("vector", lambda v: v.drain())
    emit("vector", lambda v: v.tensor_copy(stat[0:16, 70:71], ROW_SC[:, T - 1:T]))
    emit("vector", lambda v: v.drain(), inc="v_rt")
    RT1 = val("v_rt")
    dma("gpsimd", cnts_in, stat[0:16, 70:71], "g_d", [("v_rt", RT1)])
    emit("gpsimd", lambda g: g.collective_compute(
        "AllGather", ALU.bypass, replica_groups=[list(range(cfg["n_cores"]))],
        ins=[cnts_in.opt()], outs=[cnts_all.opt()]),
        waits=[("g_d", val("g_d"))], inc="g_cc")
    CC_CNT = val("g_cc")
    dma("gpsimd", c8[:], cnts_all, "g_d", [("g_cc", CC_CNT)])
    C8D = val("g_d")
    emit("tensor", lambda tt: tt.matmul(ps[7][0:1, 0:E], pm8[:], c8[:],
                                        start=True, stop=True),
         waits=[("g_d", C8D), ("c_ld", C_LD), ("v_rt", RT1)], inc="t_rt")
    emit("vector", lambda v: v.tensor_copy(offrow[:], ps[7][0:1, 0:E]),
         waits=[("t_rt", val("t_rt"))])
    emit("vector", lambda v: v.drain(), inc="v_rt")
    emit("tensor", lambda tt: tt.transpose(ps[7][0:16, 0:1], offrow[:], id128[0:1, 0:1]),
         waits=[("v_rt", val("v_rt"))], inc="t_rt")
    emit("vector", lambda v: v.tensor_copy(smallc[:, 3:4], ps[7][0:16, 0:1]),
         waits=[("t_rt", val("t_rt"))])
    emit("vector", lambda v: v.drain(), inc="v_rt")
    OFF_V = val("v_rt")

    emit("vector", lambda v: v.tensor_scalar(
        stat[0:16, 68:69], smallc[:, 3:4], -1.0, float(CAP + 1), ALU.mult, ALU.add),
        waits=[("v_rt", OFF_V)])
    emit("vector", lambda v: v.drain())
    emit("vector", lambda v: v.tensor_scalar(
        ROW_A[:], ROW_SC[:], stat[0:16, 68:69], None, ALU.is_lt))
    emit("vector", lambda v: v.drain())
    emit("vector", lambda v: v.tensor_tensor(ROW_A[:], ROW_A[:], ROW_OH[:], ALU.mult))
    emit("vector", lambda v: v.drain(), inc="v_rt")
    emit("vector", lambda v: v.tensor_tensor(
        stat[0:16, 69:70], smallc[:, 1:2], smallc[:, 3:4], ALU.add))
    emit("vector", lambda v: v.drain())
    emit("vector", lambda v: v.tensor_scalar(
        stat[0:16, 67:68], smallc[:, 3:4], 1.0, -1.0, ALU.mult, ALU.add))
    emit("vector", lambda v: v.drain())
    emit("vector", lambda v: v.tensor_scalar(
        ONES16[:], ROW_SC[:], stat[0:16, 69:70], None, ALU.add))
    emit("vector", lambda v: v.drain())
    emit("vector", lambda v: v.tensor_tensor(ONES16[:], ONES16[:], ROW_A[:], ALU.mult))
    emit("vector", lambda v: v.drain(), inc="v_rt")
    emit("vector", lambda v: v.tensor_scalar(
        ROW_SC[:], ROW_SC[:], stat[0:16, 67:68], float(CAP - 1), ALU.add, ALU.min))
    emit("vector", lambda v: v.drain())
    emit("vector", lambda v: v.tensor_scalar(
        ROW_SC[:], ROW_SC[:], smallc[:, 2:3], None, ALU.add))
    emit("vector", lambda v: v.drain())
    emit("vector", lambda v: v.tensor_tensor(ROW_SC[:], ROW_SC[:], ROW_OH[:], ALU.mult))
    emit("vector", lambda v: v.drain(), inc="v_rt")
    ROWS_V = val("v_rt")
    o16 = sbuf("o16", [16, 1])
    emit("vector", lambda v: v.memset(o16[:], 1.0))
    emit("vector", lambda v: v.drain(), inc="v_rt")
    O16V = val("v_rt")
    nQ = (T + 511) // 512
    for q in range(nQ):
        w = min(512, T - q * 512)
        emit("tensor", lambda tt, q=q, w=w: tt.matmul(
            ps[6][0:1, 0:w], o16[:], ONES16[:, q * 512:q * 512 + w],
            start=True, stop=True),
            waits=[("v_rt", ROWS_V), ("v_rt", O16V)] +
                  ([("v_rq", val("v_rq"))] if q else []),
            inc="t_rq")
        emit("tensor", lambda tt, q=q, w=w: tt.matmul(
            ps[5][0:1, 0:w], o16[:], ROW_A[:, q * 512:q * 512 + w],
            start=True, stop=True), inc="t_rq")
        emit("tensor", lambda tt, q=q, w=w: tt.matmul(
            ps[4][0:1, 0:w], o16[:], ROW_SC[:, q * 512:q * 512 + w],
            start=True, stop=True), inc="t_rq")
        emit("vector", lambda v, q=q, w=w: v.tensor_copy(
            R1[:, q * 512:q * 512 + w], ps[6][0:1, 0:w]),
            waits=[("t_rq", val("t_rq"))])
        emit("vector", lambda v, q=q, w=w: v.tensor_copy(
            R2[:, q * 512:q * 512 + w], ps[5][0:1, 0:w]))
        emit("vector", lambda v, q=q, w=w: v.tensor_copy(
            IDXROW[:, q * 512:q * 512 + w], ps[4][0:1, 0:w]))
        emit("vector", lambda v: v.drain(), inc="v_rq")
    CS_V = val("v_rq")
    emit("vector", lambda v: v.tensor_scalar(
        R2[:], R2[:], -BIG, BIG, ALU.mult, ALU.add), waits=[("v_rq", CS_V)])
    emit("vector", lambda v: v.drain())
    emit("vector", lambda v: v.tensor_tensor(R1[:], R1[:], R2[:], ALU.add))
    emit("vector", lambda v: v.drain(), inc="v_rq")
    DF_V = val("v_rq")
    dma("gpsimd", rowb[0, 0:T], R1[0:1, :], "g_d", [("v_rq", DF_V)])
    dma("gpsimd", rowb[1, 0:T], IDXROW[0:1, :], "g_d", [("v_rq", DF_V)])
    dma("gpsimd", rowb[2, 0:T], R2[0:1, :], "g_d", [("v_rq", DF_V)])
    RB_D = val("g_d")
    dma("gpsimd", dcol[:], rowb[0, 0:T].rearrange("(c p) -> p c", p=128),
        "g_d", [("g_d", RB_D)], slow=True)
    dma("gpsimd", rcol[:], rowb[1, 0:T].rearrange("(c p) -> p c", p=128), "g_d", slow=True)
    dma("gpsimd", mcol[:], rowb[2, 0:T].rearrange("(c p) -> p c", p=128), "g_d", slow=True)
    COLS_D = val("g_d")
    emit("vector", lambda v: v.tensor_copy(dcoli[:], dcol[:]),
         waits=[("g_d", COLS_D)])
    emit("vector", lambda v: v.tensor_copy(rcoli[:], rcol[:]))
    emit("vector", lambda v: v.drain())
    emit("vector", lambda v: v.tensor_scalar(mcol[:], mcol[:], 0.0, None, ALU.is_equal))
    emit("vector", lambda v: v.drain())
    emit("vector", lambda v: v.tensor_tensor(mcol[:], mcol[:], gates[:], ALU.mult))
    emit("vector", lambda v: v.drain(), inc="v_cl")
    CL_V = val("v_cl")

    for j in range(nCols):
        emit("gpsimd", lambda g, j=j: g.indirect_dma_start(
            out=smap_in.unsqueeze(0),
            out_offset=IndirectOffsetOnAxis(ap=dcoli[:, j:j + 1], axis=1),
            in_=tok1[:, j:j + 1], in_offset=None,
            bounds_check=S - 1, oob_is_err=False),
            waits=([("v_cl", CL_V), ("g_z", Z_S), ("c_ld", C_LD)] if j == 0
                   else [("g_sc", val("g_sc"))]),
            inc="g_sc", amt=16)
    SCAT = val("g_sc")
    emit("gpsimd", lambda g: g.collective_compute(
        "ReduceScatter", ALU.add, replica_groups=[list(range(cfg["n_cores"]))],
        ins=[smap_in.opt()], outs=[smap_out.opt()]),
        waits=[("g_sc", SCAT)], inc="g_cc")
    CC_RS = val("g_cc")
    dma("gpsimd", gmapf[:], smap_out.rearrange("(c p) -> p c", p=128),
        "g_d", [("g_cc", CC_RS)], slow=True)
    GM_D = val("g_d")
    emit("vector", lambda v: v.tensor_scalar(
        gmapf[:], gmapf[:], -1.0, 0.0, ALU.add, ALU.max), waits=[("g_d", GM_D)])
    emit("vector", lambda v: v.drain())
    emit("vector", lambda v: v.tensor_copy(gmapi[:], gmapf[:]))
    emit("vector", lambda v: v.drain(), inc="v_gm")
    GM_V = val("v_gm")

    if phases < 4:
        S_.barrier(exclude=())
        return finish_early()

    S_.barrier(exclude=("g_cc", "g_xr"))
    wait("gpsimd", [CC_XR])
    BF16 = mybir.dt.bfloat16
    HSh = A(B1_O, nF * S_TILE // 2).bitcast(BF16)
    HSl = A(B1_O + nF * S_TILE // 2, nF * S_TILE // 2).bitcast(BF16)
    XFh = A(B5_O, nD * S_TILE // 2).bitcast(BF16)
    XFl = A(B5_O + nD * S_TILE // 2, nD * S_TILE // 2).bitcast(BF16)
    W2land = [A(WQ_O, 1024), A(WQ_O + 1024, 1024)]
    W2p = [(A(WQ_O + 2048, 512).bitcast(BF16), A(WQ_O + 2560, 512).bitcast(BF16)),
           (A(WQ_O + 3072, 512).bitcast(BF16), A(WQ_O + 3584, 512).bitcast(BF16))]
    W1land = [A(WQ_O + 4096, 1024), A(WQ_O + 5120, 1024)]
    W1p = [(A(WQ_O + 6144, 512).bitcast(BF16), A(WQ_O + 6656, 512).bitcast(BF16)),
           (A(WQ_O + 7168, 512).bitcast(BF16), A(WQ_O + 7680, 512).bitcast(BF16))]
    XG2 = A(M2_O, D)
    STRIP = A(M2_O + D, 512)
    PIECE = [A(WS_O, 128), A(WS_O + 128, 128)]
    eo_marks = []
    for ex in range(cfg["EXP_PC"]):
        for sti in range(nST):
            for b in range(S_TILE // 128):
                jj = ex * (CAP // 128) + sti * (S_TILE // 128) + b
                emit("gpsimd", lambda g, jj=jj: g.indirect_dma_start(
                    out=XG2[:], out_offset=None,
                    in_=xr_all,
                    in_offset=IndirectOffsetOnAxis(ap=gmapi[:, jj:jj + 1], axis=0)),
                    waits=[("v_gm", GM_V), CC_XR,
                           ("txf", val("txf")), ("t_h", val("t_h")),
                           ("g_ig", val("g_ig"))],
                    inc="g_ig", amt=16)
                IG = val("g_ig")
                for k in range(nD):
                    i = b * nD + k
                    pb = ps[4 + i % 2][:, 0:128]
                    emit("tensor", lambda tt, k=k, i=i, pb=pb: tt.transpose(
                        pb, XG2[:, k * 128:(k + 1) * 128], id128[:]),
                        waits=[("g_ig", IG)] +
                              ([("vxf", val("vxf") - 1), ("sxf", val("sxf") - 1)]
                               if val("vxf") >= 2 else []),
                        inc="txf")
                    co = k * S_TILE + b * 128
                    emit("scalar", lambda sc, co=co, pb=pb: sc.activation(
                        XFh[:, co:co + 128], pb, AF.Copy),
                        waits=[("txf", val("txf"))], inc="sxf")
                    emit("vector", lambda v, co=co, pb=pb: v.tensor_tensor(
                        XFl[:, co:co + 128], pb, XFh[:, co:co + 128], ALU.subtract),
                        waits=[("txf", val("txf")), ("sxf", val("sxf"))], inc="vxf")
            XF_V, XF_S = val("vxf"), val("sxf")

            w1split = {}

            def w1_stage(fc):
                half = fc % 2
                wld = dma("sync", W1land[half][:].rearrange("p (k f) -> p k f", k=nD),
                          w1b[ex, fc].rearrange("k p f -> p k f"),
                          f"d_w1{half}",
                          [("s_w1", max(0, val("s_w1") - 1)),
                           ("v_w1", max(0, val("v_w1") - 1))] if val("v_w1") > 1 else [])
                emit("scalar", lambda sc, half=half: sc.copy(
                    W1p[half][0][:], W1land[half][:]),
                    waits=[(f"d_w1{half}", wld)] +
                          ([("t_h", max(0, val("t_h") - 1))] if val("t_h") > 1 else []),
                    inc="s_w1")
                emit("vector", lambda v, half=half: v.tensor_tensor(
                    W1p[half][1][:], W1land[half][:], W1p[half][0][:], ALU.subtract),
                    waits=[("s_w1", val("s_w1"))] +
                          ([("t_h", max(0, val("t_h") - 1))] if val("t_h") > 1 else []),
                    inc="v_w1")
                w1split[fc] = val("v_w1")

            w1_stage(0)
            for fc in range(nF):
                half = fc % 2
                bank = fc % 2
                mm3 = [(W1p[half][0], XFh), (W1p[half][0], XFl), (W1p[half][1], XFh)]
                nmm = 3 * nD
                j = 0
                for (wp, xp) in mm3:
                    for k in range(nD):
                        emit("tensor", lambda tt, wp=wp, xp=xp, k=k, j=j, bank=bank: tt.matmul(
                            ps[bank][:, 0:S_TILE],
                            wp[:, k * 128:(k + 1) * 128],
                            xp[:, k * S_TILE:(k + 1) * S_TILE],
                            start=(j == 0), stop=(j == nmm - 1)),
                            waits=([("vxf", XF_V), ("sxf", XF_S), ("v_w1", w1split[fc])] +
                                   ([("s_h", val("s_h") - 1), ("v_h", val("v_h") - 1)]
                                    if val("s_h") >= 2 else []))
                                  if j == 0 else [],
                            inc=("t_h" if j == nmm - 1 else None))
                        j += 1
                if fc + 1 < nF:
                    w1_stage(fc + 1)
                ho = fc * S_TILE
                emit("scalar", lambda sc, ho=ho, bank=bank: sc.activation(
                    HSh[:, ho:ho + S_TILE], ps[bank][:, 0:S_TILE], AF.Relu),
                    waits=[("t_h", val("t_h"))] +
                          ([("t_o2g", val("t_o2g"))] if fc == 0 else []), inc="s_h")
                emit("vector", lambda v, ho=ho, bank=bank: v.scalar_tensor_tensor(
                    HSl[:, ho:ho + S_TILE], ps[bank][:, 0:S_TILE], 0.0,
                    HSh[:, ho:ho + S_TILE], ALU.max, ALU.subtract),
                    waits=[("t_h", val("t_h")), ("s_h", val("s_h"))], inc="v_h")
            H_S, H_V = val("s_h"), val("v_h")

            nFG = 8
            nG = nF // nFG
            w2split = {}

            def w2_stage(ex, dc, g):
                gh = g % 2
                w2ld = dma("sync",
                           W2land[gh][:].rearrange("p (f d) -> p f d", f=nFG),
                           w2b[ex, dc, g * nFG:(g + 1) * nFG].rearrange("f p d -> p f d"),
                           f"d_w2{gh}",
                           [("s_w2", max(0, val("s_w2") - 1)),
                            ("v_w2", max(0, val("v_w2") - 1))] if val("v_w2") > 1 else [])
                emit("scalar", lambda sc, gh=gh: sc.copy(
                    W2p[gh][0][:], W2land[gh][:]),
                    waits=[(f"d_w2{gh}", w2ld)] +
                          ([("t_o2g", max(0, val("t_o2g") - 1))] if val("t_o2g") > 1 else []),
                    inc="s_w2")
                emit("vector", lambda v, gh=gh: v.tensor_tensor(
                    W2p[gh][1][:], W2land[gh][:], W2p[gh][0][:], ALU.subtract),
                    waits=[("s_w2", val("s_w2"))] +
                          ([("t_o2g", max(0, val("t_o2g") - 1))] if val("t_o2g") > 1 else []),
                    inc="v_w2")
                w2split[(dc, g)] = val("v_w2")

            for dc in range(nD):
                bank = 2 + dc % 2
                j2 = 0
                nmm2 = 3 * nF
                if dc == 0:
                    w2_stage(ex, 0, 0)
                for g in range(nG):
                    gh = g % 2
                    for (sel, xsel) in ((0, "h"), (0, "l"), (1, "h")):
                        for fi in range(nFG):
                            fc = g * nFG + fi
                            hs = HSh if xsel == "h" else HSl
                            emit("tensor", lambda tt, sel=sel, hs=hs, fi=fi, fc=fc,
                                 gh=gh, j2=j2, bank=bank: tt.matmul(
                                ps[bank][:, 0:S_TILE],
                                W2p[gh][sel][:, fi * 128:(fi + 1) * 128],
                                hs[:, fc * S_TILE:(fc + 1) * S_TILE],
                                start=(j2 == 0), stop=(j2 == nmm2 - 1)),
                                waits=(([("s_h", H_S), ("v_h", H_V),
                                         ("v_w2", w2split[(dc, g)])] +
                                        ([("v_o2", val("v_o2") - 1)] if val("v_o2") >= 2 else []))
                                       if j2 == 0 else
                                       ([("v_w2", w2split[(dc, g)])]
                                        if fi == 0 and sel == 0 and xsel == "h" else [])),
                                inc=("t_o2g" if (fi == nFG - 1 and sel == 1) else None))
                            j2 += 1
                    if g + 1 < nG:
                        w2_stage(ex, dc, g + 1)
                    elif dc + 1 < nD:
                        w2_stage(ex, dc + 1, 0)
                emit("vector", lambda v, bank=bank: v.tensor_copy(
                    STRIP[:], ps[bank][:, 0:S_TILE]),
                    waits=[("t_o2g", val("t_o2g"))], inc="v_o2")
                O2V = val("v_o2")
                for b in range(S_TILE // 128):
                    pb = ps[6 + b % 2][:, 0:128]
                    emit("tensor", lambda tt, b=b, pb=pb: tt.transpose(
                        pb, STRIP[:, b * 128:(b + 1) * 128], id128[:]),
                        waits=[("v_o2", O2V)] +
                              ([("veo", val("veo") - 1)] if val("veo") >= 2 else []),
                        inc="teo")
                    par = b % 2
                    emit("vector", lambda v, b=b, pb=pb, par=par: v.tensor_copy(
                        PIECE[par][:], pb),
                        waits=[("teo", val("teo")),
                               (f"d_eo{par}", val(f"d_eo{par}"))], inc="veo")
                    base = ex * CAP + sti * S_TILE + b * 128
                    dma("scalar", eo_in[base:base + 128, dc * 128:(dc + 1) * 128],
                        PIECE[par][:], f"d_eo{par}", [("veo", val("veo"))])
        eo_marks.append((ex, val("d_eo0"), val("d_eo1")))
    for (ex, v0, v1) in eo_marks:
        emit("gpsimd", lambda g, ex=ex: g.collective_compute(
            "AllGather", ALU.bypass, replica_groups=[list(range(cfg["n_cores"]))],
            ins=[eo_in[ex * CAP:(ex + 1) * CAP, :].opt()],
            outs=[eo_all[ex * (S // 2):(ex + 1) * (S // 2), :].opt()]),
            waits=[("d_eo0", v0), ("d_eo1", v1)], inc="g_cc")
    CC_EO = val("g_cc")
    S_.barrier(exclude=("g_cc",))
    GAp = [A(B1_O, D), A(B1_O + D, D)]
    OTp = [A(B2_O, D), A(B2_O + D, D)]
    for j in range(nCols):
        par = j % 2
        emit("gpsimd", lambda g, j=j, par=par: g.indirect_dma_start(
            out=GAp[par][:], out_offset=None,
            in_=eo_all,
            in_offset=IndirectOffsetOnAxis(ap=rcoli[:, j:j + 1], axis=0)),
            waits=([("g_cc", CC_EO)] if j == 0 else []) +
                  [("v_fin", max(0, val("v_fin") - 1)),
                   (f"g_ig{par}", val(f"g_ig{par}"))],
            inc=f"g_ig{par}", amt=16)
        IGE = val(f"g_ig{par}")
        emit("vector", lambda v, j=j, par=par: v.scalar_tensor_tensor(
            OTp[par][:], GAp[par][:], mcol[:, j:j + 1], XM[:, j * D:(j + 1) * D],
            ALU.mult, ALU.add),
            waits=[(f"g_ig{par}", IGE), (f"d_out{par}", val(f"d_out{par}"))])
        emit("vector", lambda v: v.drain(), inc="v_fin")
        dma("scalar", out_hbm[j * 128:(j + 1) * 128, :], OTp[par][:],
            f"d_out{par}", [("v_fin", val("v_fin"))])
    wait("gpsimd", [("d_out0", val("d_out0")), ("d_out1", val("d_out1"))])
    wait("sync", [("d_out0", val("d_out0")), ("d_out1", val("d_out1"))])

    block = es.enter_context(nc.Block())
    S_.build(block)
    es.close()
    return nc


def host_prep(cfg, L0_attn_x, L0_attn_w_qkv, L0_attn_w_swiglu, L0_attn_w_out,
              L0_ffn_Wg, L0_ffn_w1, L0_ffn_w2):
    n, T, D, E, F = cfg["n_cores"], cfg["T"], cfg["D"], cfg["E"], cfg["F"]
    nD, nF, CAP, nCols = cfg["nD"], cfg["nF"], cfg["CAP"], cfg["nCols"]
    QK = 2 * D
    nSW = (2 * D) // 128
    x = np.asarray(L0_attn_x, dtype=np.float32).reshape(n * T, D)
    wqkv = np.asarray(L0_attn_w_qkv, dtype=np.float32)
    wswg = np.asarray(L0_attn_w_swiglu, dtype=np.float32)
    wout = np.asarray(L0_attn_w_out, dtype=np.float32)
    wgm = np.asarray(L0_ffn_Wg, dtype=np.float32)
    w1 = np.asarray(L0_ffn_w1, dtype=np.float32)
    w2 = np.asarray(L0_ffn_w2, dtype=np.float32)

    wqk_b = np.ascontiguousarray(wqkv[:QK].T.reshape(nD, 128, QK))
    wv_b = np.ascontiguousarray(
        wqkv[QK:].T.reshape(nD, 128, nD, 128).transpose(2, 0, 1, 3))
    wsw_b = np.ascontiguousarray(
        wswg.T.reshape(nD, 128, nSW, 128).transpose(2, 0, 1, 3))
    wout_b = np.ascontiguousarray(wout.T.reshape(nD, 128, D))
    wg_b = np.ascontiguousarray(wgm.reshape(nD, 128, E))
    ident = np.eye(128, dtype=np.float32)
    iota16r = np.tile(np.arange(E, dtype=np.float32), (128, 1))
    iotaEc = np.arange(E, dtype=np.float32).reshape(E, 1)
    ebias = (np.arange(E, dtype=np.float32) * CAP - 1.0).reshape(E, 1)
    eids = np.arange(E, dtype=np.float32)
    rbias = ((eids % 2) * (n * T / 2) + (eids // 2) * CAP).reshape(E, 1).astype(np.float32)

    in_maps = []
    epc = cfg["EXP_PC"]
    for c in range(n):
        w1c = w1[epc * c:epc * (c + 1)]
        w2c = w2[epc * c:epc * (c + 1)]
        w1_b = np.ascontiguousarray(
            w1c.reshape(epc, nD, 128, nF, 128).transpose(0, 3, 1, 2, 4))
        w2_b = np.ascontiguousarray(
            w2c.reshape(epc, nF, 128, nD, 128).transpose(0, 3, 1, 2, 4))
        TILE_ = cfg["TILE"]
        cols = np.arange(nCols)
        pp = np.arange(128)
        tt_ = cols // (TILE_ // 128)
        off = (cols % (TILE_ // 128))[None, :] * 128 + pp[:, None]
        tokid1 = (tt_[None, :] * (n * TILE_) + c * TILE_ + off + 1.0).astype(np.float32)
        pm = np.zeros((8, 1), dtype=np.float32)
        pm[:c] = 1.0
        in_maps.append({
            "x": np.ascontiguousarray(x[c * T:(c + 1) * T]),
            "wqk": wqk_b, "wv": wv_b, "wsw": wsw_b, "wout": wout_b, "wg": wg_b,
            "w1b": w1_b, "w2b": w2_b,
            "ident": ident, "iota16r": iota16r, "iotaEc": iotaEc,
            "ebias": ebias, "rbias": rbias,
            "tokid1": np.ascontiguousarray(tokid1),
            "prevmask": pm,
        })
    return in_maps


def kernel(L0_attn_x, L0_attn_w_qkv, L0_attn_w_swiglu, L0_attn_w_out,
           L0_ffn_Wg, L0_ffn_w1, L0_ffn_w2):
    cfg = _cfg()
    in_maps = host_prep(cfg, L0_attn_x, L0_attn_w_qkv, L0_attn_w_swiglu,
                        L0_attn_w_out, L0_ffn_Wg, L0_ffn_w1, L0_ffn_w2)
    nc = build_nc(cfg)
    res = run_bass_kernel_spmd(nc, in_maps, core_ids=list(range(cfg["n_cores"])))
    out = np.concatenate([np.asarray(r["out"]) for r in res.results], axis=0)
    return out.reshape(1, cfg["S"], cfg["D"]).astype(np.float32)
